# version 02
# baseline (speedup 1.0000x reference)
import numpy as np
import concourse.bass as bass
import concourse.mybir as mybir
from concourse.bass_utils import run_bass_kernel_spmd

F32 = mybir.dt.float32
BF16 = mybir.dt.bfloat16
AF = mybir.ActivationFunctionType
ALU = mybir.AluOpType
AX = mybir.AxisListType

D = 2048
H = 16
NTC = 16
NOWN = 8
NE = 8
EPS = 1e-5
NEGV = -1.0e30
ISQ = 0.08838834764831845

DMA_SEMS = ("dy", "dg", "dx", "dx2", "dsp", "dsp2", "dab", "dwa", "dwb",
            "dm1a", "dm1b", "dm3a", "dm3b", "dm2a", "dm2b", "dzc", "dout")
ALL_SEMS = DMA_SEMS + ("t", "v", "s", "b", "ag")


class Prog:
    ENGS = ("tensor", "vector", "scalar", "sync", "gpsimd")

    def __init__(self):
        self.ops = {e: [] for e in self.ENGS}
        self.cnt = {}
        self.ecnt = {}
        self.gcnt = {}

    def op(self, eng, fn, waits=(), inc=None, inc_amt=1, big=False):
        w = [t for t in waits if t is not None]
        self.ops[eng].append((fn, w, inc, inc_amt, big))
        if inc:
            self.cnt[inc] = self.cnt.get(inc, 0) + inc_amt
            return (inc, self.cnt[inc], eng)
        return None

    def dma(self, eng, out, in_, waits=(), sem="dy"):
        return self.op(eng, lambda e, o=out, i=in_: e.dma_start(out=o, in_=i),
                       waits, sem, 16, big=True)

    def emit(self, blk, semh):
        for eng in self.ENGS:
            oplist = self.ops[eng]
            if not oplist:
                continue

            hop = eng in ("vector", "scalar")

            def body(e, oplist=oplist, eng=eng, hop=hop):
                seen = {}
                prev = None
                for fn, waits, inc, amt, big in oplist:
                    if hop and prev is not None and not (prev[2] and big):
                        s, n = prev[0], prev[1]
                        if seen.get(s, 0) < n:
                            e.wait_ge(semh[s], n)
                            seen[s] = n
                    for s, n, peng in waits:
                        if n <= 0 or seen.get(s, 0) >= n:
                            continue
                        if peng == eng and s not in DMA_SEMS:
                            continue
                        e.wait_ge(semh[s], n)
                        seen[s] = n
                    inst = fn(e)
                    if inc is not None:
                        inst.then_inc(semh[inc], amt)
                        prev = (inc, self.cnt_emit[eng].setdefault(inc, 0) + 1, big) if False else None
                    prev = None
                    if inc is not None:
                        c = self.ecnt.setdefault((eng, inc), 0) + 1
                        self.ecnt[(eng, inc)] = c
                        g = self.gcnt.setdefault(inc, 0) + 1
                        self.gcnt[inc] = g
                        prev = (inc, g, big)
            getattr(blk, eng)(body)


def build_nc(upto="p4", tap=False):
    nc = bass.Bass("TRN2", target_bir_lowering=False, detect_race_conditions=False)
    P = Prog()

    def din(name, shape, dt=F32):
        return nc.dram_tensor(name, list(shape), dt, kind="ExternalInput").ap()

    x_own = din("x_own", [NOWN, 128, D])
    x_full = din("x_full", [NTC, 128, D])
    wq = din("wq", [4, 16, 128, 512])
    wk = din("wk", [4, 16, 128, 512])
    wv = din("wv", [4, 16, 128, 512])
    wo = din("wo", [4, 4, 4, 128, 512])
    gqt = din("gqt", [128, D])
    gkt = din("gkt", [128, D])
    gate_wd = din("gate_w", [16, 128, NE])
    sw1 = din("sw1", [22, 16, 128, 128])
    sw3 = din("sw3", [22, 16, 128, 128])
    sw2 = din("sw2", [4, 22, 128, 512])
    rw1 = din("rw1", [NE, 11, 16, 128, 128])
    rw3 = din("rw3", [NE, 11, 16, 128, 128])
    rw2 = din("rw2", [NE, 4, 11, 128, 512])
    l1w = din("l1w", [4, 16, 128, 512])
    lam_logit = din("lam_logit", [1, 4])
    eta_row = din("eta", [1, 4])
    l1gam = din("l1gam", [128, D])
    l1bet = din("l1bet", [128, D])
    ffnw = din("ffnw", [4, 16, 128, 512])
    ffnw_abs = din("ffnw_abs", [16, 128, D])
    lnw_t = din("lnw_t", [128, D])
    lnb_t = din("lnb_t", [128, D])
    fgam = din("fgam", [128, D])
    fbet = din("fbet", [128, D])
    ident = din("ident", [128, 128])
    onesr = din("onesr", [1, 128])
    bmask = din("bmask", [128, 256])
    delta = din("delta", [128, 128])
    neg2 = din("neg2", [128, 256])
    eye2 = din("eye2", [128, 256])
    iotam = din("iotam", [1, 16])

    y_out = nc.dram_tensor("y", [NOWN, 128, D], F32, kind="ExternalOutput").ap()
    dbg = None
    if tap is True:
        dbg = nc.dram_tensor("dbg", [NOWN, 128, D], F32, kind="ExternalOutput").ap()
    dbg_ar = None
    if isinstance(tap, str) and (tap.startswith("ar") or tap.startswith("ln")):
        dbg_ar = nc.dram_tensor("dbgar", [128, 36864], BF16, kind="ExternalOutput").ap()

    xt_spill = nc.dram_tensor("xt_spill", [NTC, 128, D], BF16, kind="Internal").ap()
    xt_own = nc.dram_tensor("xt_own", [NOWN, 128, D], BF16, kind="Internal").ap()
    ag_in = nc.dram_tensor("ag_in", [NOWN, 128, D], BF16, kind="Internal").ap()
    ag_out = nc.dram_tensor("ag_out", [2, NOWN, 128, D], BF16, kind="Internal").ap()

    XR = nc.alloc_sbuf_tensor("XR", [128, NOWN * D], F32)
    AR = nc.alloc_sbuf_tensor("AR", [128, 36864], BF16)
    SC = nc.alloc_sbuf_tensor("SC", [128, 4096], F32)
    G1 = nc.alloc_sbuf_tensor("G1", [128, D], F32)
    G2 = nc.alloc_sbuf_tensor("G2", [128, D], F32)
    WS = nc.alloc_sbuf_tensor("WS", [128, 8192], BF16)
    C_ID = nc.alloc_sbuf_tensor("C_ID", [128, 128], BF16)
    C_IDF = nc.alloc_sbuf_tensor("C_IDF", [128, 128], F32)
    C_ON = nc.alloc_sbuf_tensor("C_ON", [1, 128], F32)
    C_BM = nc.alloc_sbuf_tensor("C_BM", [128, 256], F32)
    C_DL = nc.alloc_sbuf_tensor("C_DL", [128, 128], F32)
    C_N2 = nc.alloc_sbuf_tensor("C_N2", [128, 256], F32)
    C_E2 = nc.alloc_sbuf_tensor("C_E2", [128, 256], F32)
    C_IM = nc.alloc_sbuf_tensor("C_IM", [1, 16], F32)
    ST = nc.alloc_sbuf_tensor("ST", [128, 64], F32)
    R1 = nc.alloc_sbuf_tensor("R1", [1, 256], F32)

    pb = [nc.alloc_psum_tensor(f"pb{i}", [128, 512], F32) for i in range(6)]
    pt6 = nc.alloc_psum_tensor("pt6", [128, 1024], BF16)
    pt7 = nc.alloc_psum_tensor("pt7", [128, 1024], BF16)
    pbt = {"t6": pt6, "t7": pt7}

    bank_r = {i: None for i in range(6)}
    bank_r["t6"] = None
    bank_r["t7"] = None

    def T_(fn, waits=(), inc="t"):
        return P.op("tensor", fn, waits, inc)

    def V_(fn, waits=(), inc="v"):
        return P.op("vector", fn, waits, inc)

    def S_(fn, waits=(), inc="s"):
        return P.op("scalar", fn, waits, inc)

    ST_MU, ST_RS, ST_SQ, ST_T0 = 0, 1, 2, 3
    ST_SS, ST_IR = 8, 12
    ST_NM, ST_NT = 16, 20
    ST_LS, ST_LT, ST_LI = 24, 28, 29
    ST_CM, ST_SCL = 32, 33
    ST_M1, ST_M2 = 34, 35
    ST_LAMC = 40
    ST_ETAC = 44
    SC_DUMP = 2048

    def st(c, n=1):
        return ST[:, c:c + n]
    DBG_LN = [bool(tap == "ln1t")]

    def emit_ln(x_ap, out_ap, ready):
        mu, rs, sq, t0 = st(ST_MU), st(ST_RS), st(ST_SQ), st(ST_T0)
        dump = SC[:, SC_DUMP:SC_DUMP + D]
        tmu = S_(lambda e: e.activation(dump, x_ap, AF.Copy, accum_out=mu), ready)
        tsq = S_(lambda e: e.activation(dump, x_ap, AF.Square, accum_out=sq), ready)
        V_(lambda e: e.tensor_scalar(mu, mu, 1.0 / D, None, ALU.mult), [tmu])
        tv = V_(lambda e: e.tensor_scalar(sq, sq, 1.0 / D, None, ALU.mult), [tsq])
        if DBG_LN[0]:
            V_(lambda e: e.tensor_copy(st(16), mu))
            V_(lambda e: e.tensor_copy(st(17), sq), [tv])
        V_(lambda e: e.tensor_tensor(t0, mu, mu, ALU.mult))
        if DBG_LN[0]:
            V_(lambda e: e.tensor_copy(st(18), t0))
        V_(lambda e: e.tensor_tensor(t0, sq, t0, ALU.subtract), [tv])
        if DBG_LN[0]:
            V_(lambda e: e.tensor_copy(st(19), t0))
        tve = V_(lambda e: e.tensor_scalar(t0, t0, 1.0, EPS, ALU.mult, ALU.add))
        if DBG_LN[0]:
            V_(lambda e: e.tensor_copy(st(20), t0))
        ts2 = S_(lambda e: e.sqrt(t0, t0), [tve])
        if DBG_LN[0]:
            V_(lambda e: e.tensor_copy(st(21), t0), [ts2])
        V_(lambda e: e.reciprocal(rs, t0), [ts2])
        return V_(lambda e: e.tensor_scalar(out_ap, x_ap, mu, rs, ALU.subtract, ALU.mult),
                  list(ready))

    tr_flip = [0]

    def emit_tr(src_base_ap, dst_ap, n, ready):
        toks = []
        done = 0
        while done < n:
            take = min(4, n - done)
            bank = "t6" if tr_flip[0] % 2 == 0 else "t7"
            tr_flip[0] += 1
            pbank = pbt[bank]
            ttok = None
            for j in range(take):
                w = list(ready) + [bank_r[bank]] if j == 0 else []
                ttok = T_(lambda e, j=j, dj=done, pbk=pbank: e.transpose(
                    pbk[:, j * 128:(j + 1) * 128],
                    src_base_ap[:, (dj + j) * 128:(dj + j + 1) * 128], C_ID[:]),
                    w, inc="t" if j == take - 1 else None)
            ev = V_(lambda e, dj=done, tk=take, pbk=pbank: e.tensor_copy(
                dst_ap[:, dj * 128:(dj + tk) * 128], pbk[:, :tk * 128]), [ttok])
            bank_r[bank] = ev
            toks.append(ev)
            done += take
        return toks[-1]

    tok_const = [
        P.dma("gpsimd", C_ID[:], ident[:], sem="dg"),
        P.dma("sync", C_IDF[:], ident[:], sem="dy"),
        P.dma("sync", C_ON[:], onesr[:], sem="dy"),
        P.dma("sync", C_BM[:], bmask[:], sem="dy"),
        P.dma("sync", C_DL[:], delta[:], sem="dy"),
        P.dma("sync", C_N2[:], neg2[:], sem="dy"),
        P.dma("sync", C_E2[:], eye2[:], sem="dy"),
        P.dma("sync", C_IM[:], iotam[:], sem="dy"),
    ]
    tok_g1 = P.dma("sync", G1[:], gqt[:], sem="dy")
    tok_g2 = P.dma("sync", G2[:], gkt[:], sem="dy")

    def dy_barrier():
        return P.op("sync", lambda e: e.nop(),
                    waits=[("dy", P.cnt["dy"], "sync")], inc="b")

    b_const = dy_barrier()
    tok_const = [b_const]
    tok_g1 = b_const
    tok_g2 = b_const

    A_LN = 0
    A_ROW = 2048
    A_XF = 6144

    def xf32(k):
        return AR[:, A_XF + k * 4096: A_XF + (k + 1) * 4096].bitcast(F32)

    spill_last = None
    xtile_free = [None, None]
    row_free = [None, None]
    ln_free = [None]

    def p0_tile(src_dram, dst_dram, idx, k):
        nonlocal spill_last
        tdma = P.dma("sync", xf32(k), src_dram[idx], waits=[xtile_free[k]],
                     sem="dx" if k == 0 else "dx2")
        ln_ap = AR[:, A_LN:A_LN + D]
        t_ln = emit_ln(xf32(k), ln_ap, [tdma, ln_free[0]] + tok_const)
        xtile_free[k] = t_ln
        rb = AR[:, A_ROW + k * 2048:A_ROW + k * 2048 + D]
        ev = emit_tr(ln_ap, rb, 16, [t_ln, row_free[k]])
        ln_free[0] = ev
        spill_last = P.dma("sync", dst_dram[idx], rb, waits=[ev],
                           sem="dsp" if k == 0 else "dsp2")
        row_free[k] = spill_last

    n_p0 = NTC
    if isinstance(tap, str) and tap.startswith("ln1t"):
        n_p0 = 1
    for tc in range(n_p0):
        p0_tile(x_full, xt_spill, tc, tc % 2)
    if n_p0 == NTC:
        for i in range(NOWN):
            p0_tile(x_own, xt_own, i, i % 2)
    if tap == "ln1t":
        P.dma("sync", dbg_ar[:, :2048], AR[:, :2048],
              waits=[("dsp", P.cnt["dsp"], "sync")], sem="dout")
        P.dma("gpsimd", dbg_ar[:, 2048:2080], ST[:, 0:32],
              waits=[("dsp", P.cnt["dsp"], "sync")], sem="dout")
        P.op("sync", lambda e: e.nop(), waits=[("dout", P.cnt["dout"], "sync")])
    spill_all = [("dsp", P.cnt.get("dsp", 0), "sync"), ("dsp2", P.cnt.get("dsp2", 0), "sync")]

    if tap == "ar_raw":
        P.dma("sync", dbg_ar[:, :6144], AR[:, :6144], waits=spill_all, sem="dout")
        P.op("sync", lambda e: e.nop(), waits=[("dout", P.cnt["dout"], "sync")])
    if tap == "ar_xt":
        rb1 = []
        for tc in range(8):
            rb1.append(P.dma("sync", AR[:, tc * 2048:(tc + 1) * 2048],
                             xt_spill[tc], waits=spill_all, sem="dy"))
        for i in range(8):
            rb1.append(P.dma("sync", AR[:, 16384 + i * 2048:16384 + (i + 1) * 2048],
                             xt_own[i], waits=spill_all, sem="dy"))
        P.op("sync", lambda e: e.nop(), waits=[("dy", P.cnt["dy"], "sync")])
        P.dma("sync", dbg_ar, AR[:], waits=rb1, sem="dout")
        P.op("sync", lambda e: e.nop(), waits=[("dout", P.cnt["dout"], "sync")])

    A_XTW = 0
    A_KT = 8192
    A_V = 16384
    A_QT = 24576
    A_P = 28672
    A_PT = 30720
    A_AOT = 32768

    S_VAL, S_TMP, S_SQ = 0, 512, 1024

    xr_tok = [[None] * 4 for _ in range(NOWN)]
    if upto in ("p1", "p2", "p3", "p4"):
        for i in range(NOWN):
            P.dma("sync", XR[:, i * D:(i + 1) * D], x_own[i], sem="dy")
        b_xr = dy_barrier()
        for i in range(NOWN):
            for s4 in range(4):
                xr_tok[i][s4] = b_xr

        ws_free = None
        wave_free = [None, None]
        qk_guard = [None]
        stage_guard = [None]

        def wave_load(src_dram, w, k, sem):
            return P.dma("sync",
                         AR[:, A_XTW + k * 4096:A_XTW + (k + 1) * 4096]
                         .rearrange("p (dc tc f) -> p dc tc f", tc=2, f=128),
                         src_dram[2 * w:2 * w + 2]
                         .rearrange("tc p (dc f) -> p dc tc f", f=128),
                         waits=[wave_free[k]] + spill_all, sem=sem)

        def proj(kind, g, wsrc):
            nonlocal ws_free
            wtok = P.dma("gpsimd", WS[:].rearrange("p (dc o) -> p dc o", o=512),
                         wsrc[g].rearrange("dc p o -> p dc o"),
                         waits=[ws_free], sem="dwa")
            nw = 4 if kind == "q" else 8
            src = xt_own if kind == "q" else xt_spill
            lastev = None
            wtoks = {}
            for w in range(min(nw, 2)):
                wtoks[w] = wave_load(src, w, w % 2, "dx" if w % 2 == 0 else "dx2")
            emitted = set(range(min(nw, 2)))
            for w in range(nw):
                k = w % 2
                for tt in range(2):
                    tcg = 2 * w + tt
                    bk = (2 * w + tt) % 4
                    mtok = None
                    for dc in range(16):
                        wts = [bank_r[bk], wtoks[w], wtok] if dc == 0 else []
                        mtok = T_(lambda e, bk=bk, k=k, dc=dc, tt=tt:
                                  e.matmul(pb[bk][:],
                                           AR[:, A_XTW + k * 4096 + dc * 256 + tt * 128:
                                              A_XTW + k * 4096 + dc * 256 + tt * 128 + 128],
                                           WS[:, dc * 512:(dc + 1) * 512],
                                           start=(dc == 0), stop=(dc == 15)),
                                  wts, inc="t" if dc == 15 else None)
                    lastev = proj_out(kind, g, tcg, bk, mtok)
                wave_free[k] = lastev
                nxt = w + 2
                if nxt < nw and nxt not in emitted:
                    wtoks[nxt] = wave_load(src, nxt, nxt % 2,
                                           "dx" if nxt % 2 == 0 else "dx2")
                    emitted.add(nxt)
            ws_free = lastev
            return lastev

        def proj_out(kind, g, tcg, bk, mtok):
            if kind == "v":
                ev = V_(lambda e: e.tensor_copy(
                    AR[:, A_V + tcg * 512:A_V + tcg * 512 + 512], pb[bk][:]), [mtok])
                bank_r[bk] = ev
                return ev
            val = SC[:, S_VAL:S_VAL + 512]
            tmp = SC[:, S_TMP:S_TMP + 512]
            sqv = SC[:, S_SQ:S_SQ + 512]
            ss, ir = st(ST_SS, 4), st(ST_IR, 4)
            t1 = V_(lambda e, bk=bk: e.tensor_copy(val, pb[bk][:]),
                    [mtok, qk_guard[0]])
            bank_r[bk] = t1
            t2 = S_(lambda e: e.square(sqv, val), [t1])
            t3 = V_(lambda e: e.tensor_reduce(
                ss, sqv.rearrange("p (a b) -> p a b", b=128),
                axis=AX.X, op=ALU.add), [t2])
            V_(lambda e: e.tensor_scalar(ss, ss, 1.0 / 128.0, EPS, ALU.mult, ALU.add), [t3])
            t4 = V_(lambda e: e.reciprocal(ss, ss))
            t5 = S_(lambda e: e.sqrt(ir, ss), [t4])
            for hh in range(4):
                V_(lambda e, hh=hh: e.tensor_scalar(
                    tmp[:, hh * 128:(hh + 1) * 128], val[:, hh * 128:(hh + 1) * 128],
                    ir[:, hh:hh + 1], None, ALU.mult), [t5])
            gm = G1 if kind == "q" else G2
            stage = AR[:, A_P:A_P + 512]
            t7 = V_(lambda e: e.tensor_tensor(
                stage, tmp, gm[:, g * 512:(g + 1) * 512], ALU.mult),
                [tok_g1, stage_guard[0]])
            qk_guard[0] = t2
            dst_base = A_QT if kind == "q" else A_KT
            per = 1024 if kind == "q" else 2048
            toks = []
            bank = "t6" if tr_flip[0] % 2 == 0 else "t7"
            tr_flip[0] += 1
            pbank = pbt[bank]
            ttok = None
            for j in range(4):
                w = [t7, bank_r[bank]] if j == 0 else []
                ttok = T_(lambda e, j=j, pbk=pbank: e.transpose(
                    pbk[:, j * 128:(j + 1) * 128],
                    stage[:, j * 128:(j + 1) * 128], C_ID[:]),
                    w, inc="t" if j == 3 else None)
            stage_guard[0] = ttok
            evl = None
            for j in range(4):
                evl = V_(lambda e, j=j, pbk=pbank: e.tensor_copy(
                    AR[:, dst_base + j * per + tcg * 128:
                       dst_base + j * per + tcg * 128 + 128],
                    pbk[:, j * 128:(j + 1) * 128]), [ttok])
            bank_r[bank] = evl
            return evl

        for g in range(4):
            tq = proj("q", g, wq)
            tk = proj("k", g, wk)
            tv = proj("v", g, wv)
            proj_done = [tq, tk, tv]
            if tap == "ar_qkv" and g == 0:
                P.dma("sync", dbg_ar[:, :28672], AR[:, :28672],
                      waits=proj_done, sem="dout")
                P.op("sync", lambda e: e.nop(),
                     waits=[("dout", P.cnt["dout"], "sync")])
                break

            wo_tok = P.dma("gpsimd",
                           WS[:].rearrange("p (s hc o) -> p s hc o", hc=4, o=512),
                           wo[g].rearrange("s hc p o -> p s hc o"),
                           waits=[ws_free], sem="dwb")

            pv_flip = 0
            aot_tok = []
            p_guard = [None]
            for i in range(NOWN):
                ncol = 256 * (i + 1)
                nch = 2 * i + 2
                nsl = (ncol + 511) // 512
                for h in range(4):
                    qt_ap = AR[:, A_QT + h * 1024 + i * 128:
                               A_QT + h * 1024 + i * 128 + 128]
                    smm = []
                    for sl in range(nsl):
                        cn = min(512, ncol - sl * 512)
                        tkn = T_(lambda e, sl=sl, h=h, cn=cn, q=qt_ap: e.matmul(
                            pb[sl][:, :cn], q,
                            AR[:, A_KT + h * 2048 + sl * 512:
                               A_KT + h * 2048 + sl * 512 + cn],
                            start=True, stop=True),
                            [bank_r[sl]] + (proj_done if (i == 0 and h == 0) else []),
                            inc="t")
                        smm.append(tkn)
                    sl_m, off_m = (256 * i) // 512, (256 * i) % 512
                    tmask = V_(lambda e, sl=sl_m, off=off_m: e.tensor_tensor(
                        pb[sl][:, off:off + 256], pb[sl][:, off:off + 256],
                        C_BM[:], ALU.add), [smm[sl_m]])
                    rmax = []
                    for sl in range(nsl):
                        cn = min(512, ncol - sl * 512)
                        rmax.append(V_(lambda e, sl=sl, cn=cn: e.reduce_max(
                            st(ST_NM + sl), pb[sl][:, :cn], axis=AX.X, negate=True),
                            [smm[sl], tmask]))
                    tot = rmax[0]
                    for sl in range(1, nsl):
                        tot = V_(lambda e, sl=sl: e.tensor_tensor(
                            st(ST_NM), st(ST_NM), st(ST_NM + sl), ALU.min), [rmax[sl]])
                    tb = V_(lambda e: e.tensor_scalar(
                        st(ST_NT), st(ST_NM), ISQ, None, ALU.mult), [tot])
                    exps = []
                    for sl in range(nsl):
                        cn = min(512, ncol - sl * 512)
                        exps.append(S_(lambda e, sl=sl, cn=cn: e.activation(
                            AR[:, A_P + sl * 512:A_P + sl * 512 + cn],
                            pb[sl][:, :cn], AF.Exp, bias=st(ST_NT), scale=ISQ,
                            accum_out=st(ST_LS + sl)), [tb, tmask, p_guard[0]]))
                        bank_r[sl] = exps[-1]
                    tl = exps[0]
                    V_(lambda e: e.tensor_copy(st(ST_LT), st(ST_LS)), [exps[0]])
                    for sl in range(1, nsl):
                        tl = V_(lambda e, sl=sl: e.tensor_tensor(
                            st(ST_LT), st(ST_LT), st(ST_LS + sl), ALU.add), [exps[sl]])
                    tli = V_(lambda e: e.reciprocal(st(ST_LI), st(ST_LT)), [tl])
                    psc = []
                    for sl in range(nsl):
                        cn = min(512, ncol - sl * 512)
                        psc.append(V_(lambda e, sl=sl, cn=cn: e.tensor_scalar(
                            AR[:, A_P + sl * 512:A_P + sl * 512 + cn],
                            AR[:, A_P + sl * 512:A_P + sl * 512 + cn],
                            st(ST_LI), None, ALU.mult), [tli, exps[sl]]))
                    pt_tok = emit_tr(AR[:, A_P:A_P + ncol], AR[:, A_PT:A_PT + ncol],
                                     nch, psc)
                    p_guard[0] = pt_tok
                    pvb = 4 + (pv_flip % 2)
                    pv_flip += 1
                    mtok = None
                    for kc in range(nch):
                        w = [bank_r[pvb], pt_tok] if kc == 0 else []
                        mtok = T_(lambda e, kc=kc, h=h, pvb=pvb, nch=nch: e.matmul(
                            pb[pvb][:, :128],
                            AR[:, A_V + kc * 512 + h * 128:A_V + kc * 512 + h * 128 + 128],
                            AR[:, A_PT + kc * 128:A_PT + kc * 128 + 128],
                            start=(kc == 0), stop=(kc == nch - 1)),
                            w, inc="t" if kc == nch - 1 else None)
                    ev = V_(lambda e, h=h, i=i, pvb=pvb: e.tensor_copy(
                        AR[:, A_AOT + h * 1024 + i * 128:A_AOT + h * 1024 + i * 128 + 128],
                        pb[pvb][:, :128]), [mtok])
                    bank_r[pvb] = ev
                    aot_tok.append(ev)

            last_wo = None
            for s4 in range(4):
                for i in range(NOWN):
                    bk = i % 4
                    mtok = None
                    for hc in range(4):
                        w = [bank_r[bk], wo_tok] + (aot_tok if (s4 == 0 and i == 0) else [])
                        w = w if hc == 0 else []
                        mtok = T_(lambda e, bk=bk, hc=hc, i=i, s4=s4: e.matmul(
                            pb[bk][:],
                            AR[:, A_AOT + hc * 1024 + i * 128:
                               A_AOT + hc * 1024 + i * 128 + 128],
                            WS[:, s4 * 2048 + hc * 512:s4 * 2048 + hc * 512 + 512],
                            start=(hc == 0), stop=(hc == 3)),
                            w, inc="t" if hc == 3 else None)
                    ev = V_(lambda e, bk=bk, i=i, s4=s4: e.tensor_tensor(
                        XR[:, i * D + s4 * 512:i * D + s4 * 512 + 512],
                        pb[bk][:],
                        XR[:, i * D + s4 * 512:i * D + s4 * 512 + 512], ALU.add),
                        [mtok, xr_tok[i][s4]])
                    bank_r[bk] = ev
                    xr_tok[i][s4] = ev
                    last_wo = ev
            ws_free = last_wo

    A1_H = 0
    A1_W1 = 11264
    A1_W3 = 15360
    A1_W2 = 19456
    A1_XN = 30720
    A1_G8 = 32768

    if upto in ("p2", "p3", "p4"):
        g1b = G1[:].bitcast(BF16)
        g2b = G2[:].bitcast(BF16)

        def xt1_strip(dc):
            if dc < 8:
                return WS[:, dc * 1024:(dc + 1) * 1024]
            if dc < 12:
                return g1b[:, (dc - 8) * 1024:(dc - 7) * 1024]
            return g2b[:, (dc - 12) * 1024:(dc - 11) * 1024]

        tgw = P.dma("gpsimd",
                    AR[:, A1_G8:A1_G8 + 128].rearrange("p (dc o) -> p dc o", o=NE),
                    gate_wd.rearrange("dc p o -> p dc o"),
                    waits=[ws_free], sem="dg")

        xt1_tok = []
        for i in range(NOWN):
            xn = AR[:, A1_XN:A1_XN + D]
            tln = emit_ln(XR[:, i * D:(i + 1) * D], xn,
                          xr_tok[i] + [xt1_tok[-1] if xt1_tok else ws_free])
            done = 0
            while done < 16:
                bank = "t6" if tr_flip[0] % 2 == 0 else "t7"
                tr_flip[0] += 1
                pbank = pbt[bank]
                ttok = None
                for j in range(4):
                    w = [tln, bank_r[bank]] if j == 0 else []
                    ttok = T_(lambda e, j=j, dj=done, pbk=pbank, xn=xn: e.transpose(
                        pbk[:, j * 128:(j + 1) * 128],
                        xn[:, (dj + j) * 128:(dj + j + 1) * 128], C_ID[:]),
                        w, inc="t" if j == 3 else None)
                evl = None
                for j in range(4):
                    evl = V_(lambda e, j=j, dj=done, i=i, pbk=pbank: e.tensor_copy(
                        xt1_strip(dj + j)[:, i * 128:(i + 1) * 128],
                        pbk[:, j * 128:(j + 1) * 128]), [ttok, tok_g1, tok_g2])
                bank_r[bank] = evl
                xt1_tok.append(evl)
                done += 4
        xt1_all = xt1_tok[-1]

        SC_CW = 1536
        cw_tok = [None] * NOWN
        for i in range(NOWN):
            mtok = None
            for dc in range(16):
                w = [xt1_all, tgw, bank_r[5]] if dc == 0 else []
                mtok = T_(lambda e, dc=dc, i=i: e.matmul(
                    pb[5][:, :NE],
                    xt1_strip(dc)[:, i * 128:(i + 1) * 128],
                    AR[:, A1_G8 + dc * NE:A1_G8 + (dc + 1) * NE],
                    start=(dc == 0), stop=(dc == 15)),
                    w, inc="t" if dc == 15 else None)
            pr = SC[:, SC_CW + i * NE:SC_CW + (i + 1) * NE]
            t1 = V_(lambda e: e.reduce_max(st(ST_M1), pb[5][:, :NE], axis=AX.X,
                                           negate=True), [mtok])
            t2 = S_(lambda e, pr=pr: e.activation(pr, pb[5][:, :NE], AF.Exp,
                                                  bias=st(ST_M1),
                                                  accum_out=st(ST_M2)), [t1])
            bank_r[5] = t2
            t3 = V_(lambda e: e.reciprocal(st(ST_M2), st(ST_M2)), [t2])
            t4 = V_(lambda e, pr=pr: e.tensor_scalar(pr, pr, st(ST_M2), None, ALU.mult), [t3])
            tmp8 = SC[:, SC_DUMP:SC_DUMP + NE]
            t5 = V_(lambda e, pr=pr: e.reduce_max(st(ST_M1), pr, axis=AX.X), [t4])
            t6 = V_(lambda e, pr=pr: e.tensor_scalar(tmp8, pr, st(ST_M1), None, ALU.is_ge), [t5])
            t7 = V_(lambda e, pr=pr: e.scalar_tensor_tensor(
                tmp8, tmp8, -1.0e9, pr, ALU.mult, ALU.add), [t6])
            t8 = V_(lambda e: e.reduce_max(st(ST_M2), tmp8, axis=AX.X), [t7])
            t9 = V_(lambda e, pr=pr: e.tensor_scalar(tmp8, pr, st(ST_M2), None, ALU.is_ge), [t8])
            cw_tok[i] = V_(lambda e, pr=pr: e.tensor_tensor(pr, pr, tmp8, ALU.mult), [t9])

        w1_free = [None, None]
        w3_free = [None, None]
        w2_free = [None, None]
        h_free = [None] * 11

        def expert_pass(ew1, ew3, ew2, eidx, cw_col, n_fc, fc0):
            h_toks = []
            for fc in range(n_fc):
                kb = fc % 2
                t1w = P.dma("gpsimd",
                            AR[:, A1_W1 + kb * 2048:A1_W1 + (kb + 1) * 2048]
                            .rearrange("p (dc f) -> p dc f", f=128),
                            ew1[fc].rearrange("dc p f -> p dc f"),
                            waits=[w1_free[kb]], sem="dm1a" if kb == 0 else "dm1b")
                t3w = P.dma("gpsimd",
                            AR[:, A1_W3 + kb * 2048:A1_W3 + (kb + 1) * 2048]
                            .rearrange("p (dc f) -> p dc f", f=128),
                            ew3[fc].rearrange("dc p f -> p dc f"),
                            waits=[w3_free[kb]], sem="dm3a" if kb == 0 else "dm3b")
                ma = mb = None
                for sl in range(2):
                    for dc in range(16):
                        w = [bank_r[sl], t1w, xt1_all] if dc == 0 else []
                        ma = T_(lambda e, sl=sl, dc=dc, kb=kb: e.matmul(
                            pb[sl][:],
                            AR[:, A1_W1 + kb * 2048 + dc * 128:
                               A1_W1 + kb * 2048 + (dc + 1) * 128],
                            xt1_strip(dc)[:, sl * 512:(sl + 1) * 512],
                            start=(dc == 0), stop=(dc == 15)),
                            w, inc="t" if dc == 15 else None)
                    bank_r[sl] = ma
                for sl in range(2):
                    for dc in range(16):
                        w = [bank_r[2 + sl], t3w] if dc == 0 else []
                        mb = T_(lambda e, sl=sl, dc=dc, kb=kb: e.matmul(
                            pb[2 + sl][:],
                            AR[:, A1_W3 + kb * 2048 + dc * 128:
                               A1_W3 + kb * 2048 + (dc + 1) * 128],
                            xt1_strip(dc)[:, sl * 512:(sl + 1) * 512],
                            start=(dc == 0), stop=(dc == 15)),
                            w, inc="t" if dc == 15 else None)
                hts = []
                for sl in range(2):
                    sil = SC[:, sl * 512:(sl + 1) * 512]
                    tsl = S_(lambda e, sl=sl, sil=sil: e.activation(
                        sil, pb[sl][:], AF.Silu), [ma, mb, h_free[fc]])
                    bank_r[sl] = tsl
                    hv = V_(lambda e, sl=sl, fc=fc, sil=sil: e.tensor_tensor(
                        AR[:, A1_H + fc * 1024 + sl * 512:A1_H + fc * 1024 + (sl + 1) * 512],
                        sil, pb[2 + sl][:], ALU.mult), [tsl])
                    bank_r[2 + sl] = hv
                    hts.append(hv)
                w1_free[kb] = hts[-1]
                w3_free[kb] = hts[-1]
                h_toks.append(hts[-1])
            for s4 in range(4):
                kb = s4 % 2
                t2w = P.dma("gpsimd",
                            AR[:, A1_W2 + kb * 5632:A1_W2 + kb * 5632 + n_fc * 512]
                            .rearrange("p (fc o) -> p fc o", o=512),
                            ew2[s4].rearrange("fc p o -> p fc o"),
                            waits=[w2_free[kb]], sem="dm2a" if kb == 0 else "dm2b")
                lastmm = None
                for i in range(NOWN):
                    bk = 4 + (i % 2)
                    mtok = None
                    for fc in range(n_fc):
                        w = ([bank_r[bk], t2w, h_toks[-1]] if fc == 0 else [])
                        mtok = T_(lambda e, bk=bk, fc=fc, i=i, kb=kb: e.matmul(
                            pb[bk][:],
                            AR[:, A1_H + fc * 1024 + i * 128:A1_H + fc * 1024 + i * 128 + 128],
                            AR[:, A1_W2 + kb * 5632 + fc * 512:
                               A1_W2 + kb * 5632 + (fc + 1) * 512],
                            start=(fc == 0), stop=(fc == n_fc - 1)),
                            w, inc="t" if fc == n_fc - 1 else None)
                    lastmm = mtok
                    if cw_col is None:
                        ev = V_(lambda e, bk=bk, i=i, s4=s4: e.tensor_tensor(
                            XR[:, i * D + s4 * 512:i * D + s4 * 512 + 512],
                            pb[bk][:],
                            XR[:, i * D + s4 * 512:i * D + s4 * 512 + 512],
                            ALU.add), [mtok, xr_tok[i][s4]])
                    else:
                        ev = V_(lambda e, bk=bk, i=i, s4=s4: e.scalar_tensor_tensor(
                            XR[:, i * D + s4 * 512:i * D + s4 * 512 + 512],
                            pb[bk][:],
                            SC[:, SC_CW + i * NE + eidx:SC_CW + i * NE + eidx + 1],
                            XR[:, i * D + s4 * 512:i * D + s4 * 512 + 512],
                            ALU.mult, ALU.add), [mtok, xr_tok[i][s4], cw_tok[i]])
                    bank_r[bk] = ev
                    xr_tok[i][s4] = ev
                w2_free[kb] = ev
            for fc in range(n_fc):
                h_free[fc] = lastmm
            return lastmm

        moe_free = expert_pass([sw1[fc] for fc in range(11)],
                               [sw3[fc] for fc in range(11)],
                               [sw2[s][0:11] for s in range(4)], None, None, 11, 0)
        moe_free = expert_pass([sw1[fc] for fc in range(11, 22)],
                               [sw3[fc] for fc in range(11, 22)],
                               [sw2[s][11:22] for s in range(4)], None, None, 11, 0)
        for e_ in range(NE):
            moe_free = expert_pass([rw1[e_][fc] for fc in range(11)],
                                   [rw3[e_][fc] for fc in range(11)],
                                   [rw2[e_][s] for s in range(4)], e_, True, 11, 0)

    A2_XT = 0
    A2_ZS = 16384
    A2_ZF = 0
    A2_MT = 32768
    SC_AK = 2048
    SC_ARG = 2560
    SC_TK = 2688
    SC_ACC = 2816
    SC_BC = 2944
    SC_H = 0

    if upto in ("p3", "p4"):
        tg1 = P.dma("sync", G1[:], l1gam[:], waits=[xr_tok[0][0], moe_free],
                    sem="dy")
        tg2 = P.dma("sync", G2[:], l1bet[:], sem="dy")
        bg3 = dy_barrier()
        tg1 = tg2 = bg3


        xt2_tok = []
        for i in range(NOWN):
            xn = AR[:, A2_ZS:A2_ZS + D]
            tln = emit_ln(XR[:, i * D:(i + 1) * D], xn,
                          xr_tok[i] + ([xt2_tok[-1]] if xt2_tok else []))
            done = 0
            while done < 16:
                bank = "t6" if tr_flip[0] % 2 == 0 else "t7"
                tr_flip[0] += 1
                pbank = pbt[bank]
                ttok = None
                for j in range(4):
                    w = [tln, bank_r[bank]] if j == 0 else []
                    ttok = T_(lambda e, j=j, dj=done, pbk=pbank, xn=xn: e.transpose(
                        pbk[:, j * 128:(j + 1) * 128],
                        xn[:, (dj + j) * 128:(dj + j + 1) * 128], C_ID[:]),
                        w, inc="t" if j == 3 else None)
                evl = None
                for j in range(4):
                    evl = V_(lambda e, j=j, dj=done, i=i, pbk=pbank: e.tensor_copy(
                        AR[:, A2_XT + (dj + j) * 1024 + i * 128:
                           A2_XT + (dj + j) * 1024 + i * 128 + 128],
                        pbk[:, j * 128:(j + 1) * 128]), [ttok])
                bank_r[bank] = evl
                xt2_tok.append(evl)
                done += 4
        xt2_all = xt2_tok[-1]

        zdma = []
        zproj_last = None
        tl1w = None
        zst_free = [None, None]
        for s4 in range(4):
            tl1w = P.dma("gpsimd",
                         WS[:].rearrange("p (dc o) -> p dc o", o=512),
                         l1w[s4].rearrange("dc p o -> p dc o"),
                         waits=[moe_free if s4 == 0 else zproj_last], sem="dwa")
            for i in range(NOWN):
                bk = i % 4
                mtok = None
                for dc in range(16):
                    w = [bank_r[bk], tl1w, xt2_all] if dc == 0 else []
                    mtok = T_(lambda e, bk=bk, dc=dc, i=i: e.matmul(
                        pb[bk][:],
                        AR[:, A2_XT + dc * 1024 + i * 128:A2_XT + dc * 1024 + i * 128 + 128],
                        WS[:, dc * 512:(dc + 1) * 512],
                        start=(dc == 0), stop=(dc == 15)),
                        w, inc="t" if dc == 15 else None)
                zproj_last = mtok
                ko = i % 2
                ev = V_(lambda e, bk=bk, ko=ko: e.tensor_copy(
                    AR[:, A2_ZS + ko * 512:A2_ZS + (ko + 1) * 512], pb[bk][:]),
                    [mtok, zst_free[ko]])
                bank_r[bk] = ev
                zdma.append(P.dma("sync", ag_in[i][:, s4 * 512:(s4 + 1) * 512],
                                  AR[:, A2_ZS + ko * 512:A2_ZS + (ko + 1) * 512],
                                  waits=[ev], sem="dzc"))
                zst_free[ko] = zdma[-1]

        t_ag = P.op("gpsimd", lambda e: e.collective_compute(
            "AllGather", ALU.bypass,
            replica_groups=[[0, 1], [2, 3], [4, 5], [6, 7]],
            ins=[ag_in], outs=[ag_out]),
            waits=[("dzc", P.cnt["dzc"], "sync")], inc="ag", inc_amt=1)

        tcm = V_(lambda e: e.memset(st(ST_CM), 0.0))
        ab_free = None
        for dc in range(NTC):
            tda = P.dma("sync", SC[:, 0:D], ffnw_abs[dc], waits=[ab_free], sem="dab")
            trd = V_(lambda e: e.reduce_max(st(ST_T0), SC[:, 0:D], axis=AX.X,
                                            apply_absolute_value=True), [tda, tcm])
            ab_free = V_(lambda e: e.tensor_tensor(st(ST_CM), st(ST_CM), st(ST_T0),
                                                   ALU.max), [trd])
        tt1 = T_(lambda e: e.matmul(pb[5][0:1, :128], st(ST_CM), C_IDF[:],
                                    start=True, stop=True),
                 [ab_free, bank_r[5]] + tok_const)
        tt2 = V_(lambda e: e.reduce_max(R1[0:1, 128:129], pb[5][0:1, :128], axis=AX.X), [tt1])
        bank_r[5] = tt2
        tt3 = T_(lambda e: e.matmul(pb[5][:, :1], C_ON[:], R1[0:1, 128:129],
                                    start=True, stop=True), [tt2, bank_r[5]])
        tscl = V_(lambda e: e.tensor_copy(st(ST_SCL), pb[5][:, :1]), [tt3])
        bank_r[5] = tscl

        P.dma("sync", R1[0:1, 32:36], lam_logit[:], sem="dy")
        P.dma("sync", R1[0:1, 64:68], eta_row[:], sem="dy")
        tll = tet = dy_barrier()
        ts1 = S_(lambda e: e.activation(R1[0:1, 36:40], R1[0:1, 32:36], AF.Sigmoid), [tll])
        ts2b = V_(lambda e: e.tensor_scalar(R1[0:1, 36:40], R1[0:1, 36:40], 1.0e-30,
                                            None, ALU.max), [ts1])
        ts3 = S_(lambda e: e.activation(R1[0:1, 68:72], R1[0:1, 36:40], AF.Ln), [ts2b])
        rv = []
        for k4 in range(4):
            rv.append(V_(lambda e, k4=k4: e.tensor_scalar(
                R1[0:1, k4 * 16:(k4 + 1) * 16], C_IM[:],
                R1[0:1, 68 + k4:69 + k4], None, ALU.mult), [ts3] + tok_const))
        tbc = T_(lambda e: e.matmul(pb[5][:, :72], C_ON[:], R1[0:1, 0:72],
                                    start=True, stop=True),
                 rv + [tet, ts3, bank_r[5]])
        tbce = V_(lambda e: e.tensor_copy(SC[:, SC_BC:SC_BC + 72], pb[5][:, :72]), [tbc])
        bank_r[5] = tbce

        ak_tok = []
        for k4 in range(4):
            ak_tok.append(V_(lambda e, k4=k4: e.tensor_scalar(
                SC[:, SC_AK + k4 * 128:SC_AK + (k4 + 1) * 128], C_DL[:],
                SC[:, SC_BC + 68 + k4:SC_BC + 69 + k4], None, ALU.mult),
                [tbce] + tok_const))
        mt_tok = []
        for m in range(16):
            tacc = V_(lambda e: e.memset(SC[:, SC_ACC:SC_ACC + 128], 0.0),
                      [mt_tok[-1] if mt_tok else None])
            for k4 in range(4):
                if m < 2:
                    targ = V_(lambda e, k4=k4, m=m: e.tensor_tensor(
                        SC[:, SC_ARG:SC_ARG + 128],
                        SC[:, SC_AK + k4 * 128:SC_AK + (k4 + 1) * 128],
                        C_N2[:, m * 128:(m + 1) * 128], ALU.add), [ak_tok[k4]])
                    src = SC[:, SC_ARG:SC_ARG + 128]
                else:
                    targ = ak_tok[k4]
                    src = SC[:, SC_AK + k4 * 128:SC_AK + (k4 + 1) * 128]
                tex = S_(lambda e, src=src, k4=k4, m=m: e.activation(
                    SC[:, SC_TK:SC_TK + 128], src, AF.Exp,
                    bias=SC[:, SC_BC + 16 * k4 + m:SC_BC + 16 * k4 + m + 1]),
                    [targ, tacc])
                tacc = V_(lambda e, k4=k4: e.scalar_tensor_tensor(
                    SC[:, SC_ACC:SC_ACC + 128], SC[:, SC_TK:SC_TK + 128],
                    SC[:, SC_BC + 64 + k4:SC_BC + 64 + k4 + 1],
                    SC[:, SC_ACC:SC_ACC + 128], ALU.mult, ALU.add), [tex])
            if m < 2:
                tacc = V_(lambda e, m=m: e.tensor_tensor(
                    SC[:, SC_ACC:SC_ACC + 128], SC[:, SC_ACC:SC_ACC + 128],
                    C_E2[:, m * 128:(m + 1) * 128], ALU.add), [tacc])
            mt_tok.append(V_(lambda e, m=m: e.tensor_copy(
                AR[:, A2_MT + m * 128:A2_MT + (m + 1) * 128],
                SC[:, SC_ACC:SC_ACC + 128]), [tacc]))
        mt_all = mt_tok[-1]

        zrb = []
        agw = [t_ag, zproj_last]
        for g2 in range(NTC):
            p2, idx = g2 % 2, g2 // 2
            zrb.append(P.dma("sync", AR[:, A2_ZF + g2 * 2048:A2_ZF + (g2 + 1) * 2048],
                             ag_out[p2, idx], waits=agw, sem="dzc"))
        zrb_all = ("dzc", P.cnt["dzc"], "sync")

        if tap == "ar_l1":
            P.dma("sync", dbg_ar[:, :34816], AR[:, :34816],
                  waits=[mt_all, ("dzc", P.cnt["dzc"], "sync")], sem="dout")
            P.op("sync", lambda e: e.nop(), waits=[("dout", P.cnt["dout"], "sync")])

        for i in range(NOWN):
            htoks = []
            for s4 in range(4):
                bk = s4 % 4
                mtok = None
                nj = 2 * i + 2
                for j in range(nj):
                    midx = (2 * i + 1) - j
                    w = [bank_r[bk], mt_all, zrb_all] if j == 0 else []
                    mtok = T_(lambda e, bk=bk, j=j, s4=s4, midx=midx, nj=nj: e.matmul(
                        pb[bk][:],
                        AR[:, A2_MT + midx * 128:A2_MT + (midx + 1) * 128],
                        AR[:, A2_ZF + j * 2048 + s4 * 512:A2_ZF + j * 2048 + (s4 + 1) * 512],
                        start=(j == 0), stop=(j == nj - 1)),
                        w, inc="t" if j == nj - 1 else None)
                ev = V_(lambda e, bk=bk, s4=s4: e.tensor_copy(
                    SC[:, SC_H + s4 * 512:SC_H + (s4 + 1) * 512], pb[bk][:]),
                    [mtok, ab_free])
                bank_r[bk] = ev
                htoks.append(ev)
            hn = SC[:, SC_DUMP:SC_DUMP + D]
            h_ap = SC[:, SC_H:SC_H + D]
            mu, rs, sq, t0 = st(ST_MU), st(ST_RS), st(ST_SQ), st(ST_T0)
            V_(lambda e: e.reduce_sum(mu, h_ap, axis=AX.X), htoks)
            tsq = S_(lambda e: e.activation(SC[:, SC_DUMP:SC_DUMP + D], h_ap,
                                            AF.Square, accum_out=sq), htoks)
            V_(lambda e: e.tensor_scalar(mu, mu, 1.0 / D, None, ALU.mult))
            tv = V_(lambda e: e.tensor_scalar(sq, sq, 1.0 / D, None, ALU.mult), [tsq])
            V_(lambda e: e.tensor_tensor(t0, mu, mu, ALU.mult))
            V_(lambda e: e.tensor_tensor(t0, sq, t0, ALU.subtract), [tv])
            tve2 = V_(lambda e: e.tensor_scalar(t0, t0, 1.0, EPS, ALU.mult, ALU.add))
            tsr = S_(lambda e: e.sqrt(t0, t0), [tve2])
            trc = V_(lambda e: e.reciprocal(rs, t0), [tsr])
            tnm = V_(lambda e: e.tensor_scalar(h_ap, h_ap, mu, rs,
                                               ALU.subtract, ALU.mult), [trc])
            tg = V_(lambda e: e.tensor_tensor(h_ap, h_ap, G1[:], ALU.mult),
                    [tnm, tg1])
            tb2 = V_(lambda e: e.tensor_tensor(h_ap, h_ap, G2[:], ALU.add), [tg, tg2])
            for s4 in range(4):
                tfin = V_(lambda e, i=i, s4=s4: e.tensor_tensor(
                    XR[:, i * D + s4 * 512:i * D + s4 * 512 + 512],
                    XR[:, i * D + s4 * 512:i * D + s4 * 512 + 512],
                    SC[:, SC_H + s4 * 512:SC_H + (s4 + 1) * 512], ALU.add),
                    [tb2, xr_tok[i][s4]])
                xr_tok[i][s4] = tfin

    A3_XNT = 0
    A3_SW = 16384

    if upto == "p4":
        tg1b = P.dma("sync", G1[:], lnw_t[:],
                     waits=[xr_tok[i][s] for i in range(NOWN) for s in range(4)],
                     sem="dy")
        tg2b = P.dma("sync", G2[:], lnb_t[:], sem="dy")

        xnt_tok = []
        for i in range(NOWN):
            xn = SC[:, 0:D]
            tln = emit_ln(XR[:, i * D:(i + 1) * D], xn, xr_tok[i] + [tg1b, tg2b])
            tm1 = V_(lambda e: e.tensor_tensor(xn, xn, G1[:], ALU.mult), [tln, tg1b])
            xnb = AR[:, A2_ZS:A2_ZS + D]
            tm2 = V_(lambda e: e.tensor_tensor(xnb, xn, G2[:], ALU.add),
                     [tm1, tg2b] + ([xnt_tok[-1]] if xnt_tok else []))
            done = 0
            while done < 16:
                bank = "t6" if tr_flip[0] % 2 == 0 else "t7"
                tr_flip[0] += 1
                pbank = pbt[bank]
                ttok = None
                for j in range(4):
                    w = [tm2, bank_r[bank]] if j == 0 else []
                    ttok = T_(lambda e, j=j, dj=done, pbk=pbank, xnb=xnb: e.transpose(
                        pbk[:, j * 128:(j + 1) * 128],
                        xnb[:, (dj + j) * 128:(dj + j + 1) * 128], C_ID[:]),
                        w, inc="t" if j == 3 else None)
                evl = None
                for j in range(4):
                    evl = V_(lambda e, j=j, dj=done, i=i, pbk=pbank: e.tensor_copy(
                        AR[:, A3_XNT + (dj + j) * 1024 + i * 128:
                           A3_XNT + (dj + j) * 1024 + i * 128 + 128],
                        pbk[:, j * 128:(j + 1) * 128]), [ttok])
                bank_r[bank] = evl
                xnt_tok.append(evl)
                done += 4
        xnt_all = xnt_tok[-1]

        tg1c = P.dma("sync", G1[:], fgam[:], waits=[xnt_all], sem="dy")
        tg2c = P.dma("sync", G2[:], fbet[:], sem="dy")

        sw_free = [None, None]
        out_cnt = 0
        for s4 in range(4):
            kb = s4 % 2
            swb = AR[:, A3_SW + kb * 8192:A3_SW + (kb + 1) * 8192]
            tdw = P.dma("gpsimd",
                        swb.rearrange("p (dc o) -> p dc o", o=512),
                        ffnw[s4].rearrange("dc p o -> p dc o"),
                        waits=[sw_free[kb], xnt_all],
                        sem="dm1a" if kb == 0 else "dm1b")
            tsg = S_(lambda e, swb=swb: e.sign(swb, swb), [tdw])
            lastmm = None
            for i in range(NOWN):
                bk = i % 4
                mtok = None
                for dc in range(16):
                    w = [bank_r[bk], tsg, xnt_all] if dc == 0 else []
                    mtok = T_(lambda e, bk=bk, dc=dc, i=i, swb=swb: e.matmul(
                        pb[bk][:],
                        AR[:, A3_XNT + dc * 1024 + i * 128:A3_XNT + dc * 1024 + i * 128 + 128],
                        swb[:, dc * 512:(dc + 1) * 512],
                        start=(dc == 0), stop=(dc == 15)),
                        w, inc="t" if dc == 15 else None)
                lastmm = mtok
                ko = out_cnt % 2
                out_cnt += 1
                t1 = V_(lambda e, bk=bk, ko=ko, s4=s4: e.scalar_tensor_tensor(
                    SC[:, ko * 512:ko * 512 + 512], pb[bk][:], st(ST_SCL),
                    G1[:, s4 * 512:s4 * 512 + 512], ALU.mult, ALU.mult),
                    [mtok, tscl, tg1c])
                bank_r[bk] = t1
                t2 = V_(lambda e, ko=ko, s4=s4: e.tensor_tensor(
                    SC[:, ko * 512:ko * 512 + 512], SC[:, ko * 512:ko * 512 + 512],
                    G2[:, s4 * 512:s4 * 512 + 512], ALU.add), [t1, tg2c])
                t3 = V_(lambda e, ko=ko, i=i, s4=s4: e.tensor_tensor(
                    SC[:, ko * 512:ko * 512 + 512], SC[:, ko * 512:ko * 512 + 512],
                    XR[:, i * D + s4 * 512:i * D + s4 * 512 + 512], ALU.add),
                    [t2, xr_tok[i][s4], ("dout", P.cnt.get("dout", 0), "sync")])
                P.dma("sync", y_out[i][:, s4 * 512:s4 * 512 + 512],
                      SC[:, ko * 512:ko * 512 + 512], waits=[t3], sem="dout")
            sw_free[kb] = lastmm
        P.op("sync", lambda e: e.nop(), waits=[("dout", P.cnt["dout"], "sync")])

    if tap is True:
        tw = []
        for i in range(NOWN):
            tw.append(P.dma("sync", dbg[i], XR[:, i * D:(i + 1) * D],
                            waits=xr_tok[i] if xr_tok[i][0] else spill_all,
                            sem="dout"))
        P.op("sync", lambda e: e.nop(), waits=[("dout", P.cnt["dout"], "sync")])
    if upto != "p4" and not tap:
        P.op("sync", lambda e: e.nop(), waits=spill_all)

    semh = {s: nc.alloc_semaphore(s) for s in ALL_SEMS}
    blk_cm = nc.Block()
    blk = blk_cm.__enter__()
    P.emit(blk, semh)
    blk_cm.__exit__(None, None, None)
    return nc


def prep_inputs(L0_attn_x, L0_attn_Wq, L0_attn_Wk, L0_attn_Wv, L0_attn_Wo,
                L0_attn_gq, L0_attn_gk, L0_ffn_gate_w, L0_ffn_sw1, L0_ffn_sw2,
                L0_ffn_sw3, L0_ffn_rw1, L0_ffn_rw2, L0_ffn_rw3, L1_attn_W,
                L1_attn_lam_logit, L1_attn_eta, L1_attn_gamma, L1_attn_beta,
                L1_ffn_weight, L1_ffn_gamma, L1_ffn_beta, L1_ffn_ln_weight,
                L1_ffn_ln_bias):
    f = np.float32
    asn = np.ascontiguousarray

    def t2(w):
        return asn(np.asarray(w, f).T)

    wqT = t2(L0_attn_Wq)
    wkT = t2(L0_attn_Wk)
    wvT = t2(L0_attn_Wv)
    woT = t2(L0_attn_Wo)
    def gsplit(w):
        return asn(w.reshape(16, 128, 4, 512).transpose(2, 0, 1, 3))
    wq_h = gsplit(wqT)
    wk_h = gsplit(wkT)
    wv_h = gsplit(wvT)
    wo_h = asn(woT.reshape(4, 4, 128, 4, 512).transpose(0, 3, 1, 2, 4))
    gq = np.asarray(L0_attn_gq, f)
    gk = np.asarray(L0_attn_gk, f)
    gqt = asn(np.broadcast_to(np.tile(gq, 16)[None, :], (128, D)))
    gkt = asn(np.broadcast_to(np.tile(gk, 16)[None, :], (128, D)))
    gate_h = asn(np.asarray(L0_ffn_gate_w, f).reshape(16, 128, NE))
    sw1_h = asn(np.asarray(L0_ffn_sw1, f).reshape(16, 128, 22, 128)
                .transpose(2, 0, 1, 3))
    sw3_h = asn(np.asarray(L0_ffn_sw3, f).reshape(16, 128, 22, 128)
                .transpose(2, 0, 1, 3))
    sw2_h = asn(np.asarray(L0_ffn_sw2, f).reshape(22, 128, 4, 512)
                .transpose(2, 0, 1, 3))
    rw1_h = asn(np.asarray(L0_ffn_rw1, f).reshape(NE, 16, 128, 11, 128)
                .transpose(0, 3, 1, 2, 4))
    rw3_h = asn(np.asarray(L0_ffn_rw3, f).reshape(NE, 16, 128, 11, 128)
                .transpose(0, 3, 1, 2, 4))
    rw2_h = asn(np.asarray(L0_ffn_rw2, f).reshape(NE, 11, 128, 4, 512)
                .transpose(0, 3, 1, 2, 4))
    l1wT = t2(L1_attn_W)
    l1w_h = asn(l1wT.reshape(16, 128, 4, 512).transpose(2, 0, 1, 3))
    ffnT = t2(L1_ffn_weight)
    ffn_h = asn(ffnT.reshape(16, 128, 4, 512).transpose(2, 0, 1, 3))
    ffn_abs = asn(ffnT.reshape(16, 128, D))
    lam_h = np.asarray(L1_attn_lam_logit, f).reshape(1, 4)
    eta_h = np.asarray(L1_attn_eta, f).reshape(1, 4)

    def tile_row(v):
        return asn(np.broadcast_to(np.asarray(v, f)[None, :], (128, D)))

    l1gam = tile_row(L1_attn_gamma)
    l1bet = tile_row(L1_attn_beta)
    lnw_t = tile_row(L1_ffn_ln_weight)
    lnb_t = tile_row(L1_ffn_ln_bias)
    fgam = tile_row(L1_ffn_gamma)
    fbet = tile_row(L1_ffn_beta)

    ident = np.eye(128, dtype=f)
    onesr = np.ones((1, 128), f)
    r = np.arange(128)
    deltam = asn((r[None, :] - r[:, None]).astype(f))
    iotam_base = 128.0 * (np.arange(16, dtype=f) - 1.0)

    x = np.asarray(L0_attn_x, f)
    in_maps = []
    for c in range(8):
        b, p = c // 2, c % 2
        own = [2 * i + p for i in range(NOWN)]
        x_own = asn(x[b].reshape(NTC, 128, D)[own])
        x_full = asn(x[b].reshape(NTC, 128, D))
        cidx = np.arange(256)[None, :]
        ridx = r[:, None]
        bmask = np.where(cidx <= 128 * p + ridx, 0.0, NEGV).astype(f)
        n2 = np.zeros((128, 256), f)
        e2 = np.zeros((128, 256), f)
        tri_bad = (deltam < 0)
        for m in range(2):
            mm = m * 128
            if m == 1 - p:
                n2[:, mm:mm + 128] = np.where(tri_bad, NEGV, 0.0)
                e2[:, mm:mm + 128] = ident
            elif m < 1 - p:
                n2[:, mm:mm + 128] = NEGV
        iotam = (iotam_base + 128.0 * p).reshape(1, 16).astype(f)
        in_maps.append({
            "x_own": x_own, "x_full": x_full,
            "wq": wq_h, "wk": wk_h, "wv": wv_h, "wo": wo_h,
            "gqt": gqt, "gkt": gkt, "gate_w": gate_h,
            "sw1": sw1_h, "sw3": sw3_h, "sw2": sw2_h,
            "rw1": rw1_h, "rw3": rw3_h, "rw2": rw2_h,
            "l1w": l1w_h, "lam_logit": lam_h, "eta": eta_h,
            "l1gam": l1gam, "l1bet": l1bet,
            "ffnw": ffn_h, "ffnw_abs": ffn_abs,
            "lnw_t": lnw_t, "lnb_t": lnb_t, "fgam": fgam, "fbet": fbet,
            "ident": ident, "onesr": onesr, "bmask": bmask,
            "delta": deltam, "neg2": n2, "eye2": e2, "iotam": iotam,
        })
    return in_maps


def assemble(results, key="y"):
    x_out = np.empty((B_GLOBAL, T_GLOBAL, D), np.float32)
    for c in range(8):
        b, p = c // 2, c % 2
        yc = results[c][key]
        for i in range(NOWN):
            g = 2 * i + p
            x_out[b, g * 128:(g + 1) * 128] = yc[i]
    return x_out


B_GLOBAL = 4
T_GLOBAL = 2048


def kernel(**inputs):
    in_maps = prep_inputs(**inputs)
    nc = build_nc()
    res = run_bass_kernel_spmd(nc, in_maps, core_ids=list(range(8)))
    return assemble(res.results)


if __name__ == "__main__":
    import sys, time
    t0 = time.time()
    nc = build_nc(upto=sys.argv[1] if len(sys.argv) > 1 else "p4")
    print("built ok in %.1fs" % (time.time() - t0))


# version 03
# speedup vs baseline: 1.0673x; 1.0673x over previous
import numpy as np
import concourse.bass as bass
import concourse.mybir as mybir
from concourse.bass_utils import run_bass_kernel_spmd

F32 = mybir.dt.float32
BF16 = mybir.dt.bfloat16
AF = mybir.ActivationFunctionType
ALU = mybir.AluOpType
AX = mybir.AxisListType

D = 2048
H = 16
NTC = 16
NOWN = 8
NE = 8
EPS = 1e-5
NEGV = -1.0e30
ISQ = 0.08838834764831845

DMA_SEMS = ("dy", "dg", "dx", "dx2", "dsp", "dsp2", "dab", "dwa", "dwb",
            "dm1a", "dm1b", "dm3a", "dm3b", "dm2a", "dm2b", "dzc", "dout")
ALL_SEMS = DMA_SEMS + ("t", "v", "s", "b", "ag")


class Prog:
    ENGS = ("tensor", "vector", "scalar", "sync", "gpsimd")

    def __init__(self):
        self.ops = {e: [] for e in self.ENGS}
        self.cnt = {}
        self.ecnt = {}
        self.gcnt = {}

    def op(self, eng, fn, waits=(), inc=None, inc_amt=1, big=False):
        w = [t for t in waits if t is not None]
        self.ops[eng].append((fn, w, inc, inc_amt, big))
        if inc:
            self.cnt[inc] = self.cnt.get(inc, 0) + inc_amt
            return (inc, self.cnt[inc], eng)
        return None

    def dma(self, eng, out, in_, waits=(), sem="dy"):
        return self.op(eng, lambda e, o=out, i=in_: e.dma_start(out=o, in_=i),
                       waits, sem, 16, big=True)

    def emit(self, blk, semh):
        for eng in self.ENGS:
            oplist = self.ops[eng]
            if not oplist:
                continue

            hop = eng in ("vector", "scalar")

            def body(e, oplist=oplist, eng=eng, hop=hop):
                seen = {}
                prev = None
                for fn, waits, inc, amt, big in oplist:
                    if hop and prev is not None and not (prev[2] and big):
                        s, n = prev[0], prev[1]
                        if seen.get(s, 0) < n:
                            e.wait_ge(semh[s], n)
                            seen[s] = n
                    for s, n, peng in waits:
                        if n <= 0 or seen.get(s, 0) >= n:
                            continue
                        if peng == eng and s not in DMA_SEMS:
                            continue
                        e.wait_ge(semh[s], n)
                        seen[s] = n
                    inst = fn(e)
                    if inc is not None:
                        inst.then_inc(semh[inc], amt)
                        prev = (inc, self.cnt_emit[eng].setdefault(inc, 0) + 1, big) if False else None
                    prev = None
                    if inc is not None:
                        c = self.ecnt.setdefault((eng, inc), 0) + 1
                        self.ecnt[(eng, inc)] = c
                        g = self.gcnt.setdefault(inc, 0) + 1
                        self.gcnt[inc] = g
                        prev = (inc, g, big)
            getattr(blk, eng)(body)


def build_nc(upto="p4", tap=False):
    nc = bass.Bass("TRN2", target_bir_lowering=False, detect_race_conditions=False)
    P = Prog()

    def din(name, shape, dt=F32):
        return nc.dram_tensor(name, list(shape), dt, kind="ExternalInput").ap()

    x_own = din("x_own", [NOWN, 128, D])
    x_full = din("x_full", [NTC, 128, D])
    wq = din("wq", [4, 16, 128, 512])
    wk = din("wk", [4, 16, 128, 512])
    wv = din("wv", [4, 16, 128, 512])
    wo = din("wo", [4, 4, 4, 128, 512])
    gqt = din("gqt", [128, D])
    gkt = din("gkt", [128, D])
    gate_wd = din("gate_w", [16, 128, NE])
    sw1 = din("sw1", [22, 16, 128, 128])
    sw3 = din("sw3", [22, 16, 128, 128])
    sw2 = din("sw2", [4, 22, 128, 512])
    rw1 = din("rw1", [NE, 11, 16, 128, 128])
    rw3 = din("rw3", [NE, 11, 16, 128, 128])
    rw2 = din("rw2", [NE, 4, 11, 128, 512])
    l1w = din("l1w", [4, 16, 128, 512])
    lam_logit = din("lam_logit", [1, 4])
    eta_row = din("eta", [1, 4])
    l1gam = din("l1gam", [128, D])
    l1bet = din("l1bet", [128, D])
    ffnw = din("ffnw", [4, 16, 128, 512])
    ffnw_abs = din("ffnw_abs", [16, 128, D])
    lnw_t = din("lnw_t", [128, D])
    lnb_t = din("lnb_t", [128, D])
    fgam = din("fgam", [128, D])
    fbet = din("fbet", [128, D])
    ident = din("ident", [128, 128])
    onesr = din("onesr", [1, 128])
    bmask = din("bmask", [128, 256])
    delta = din("delta", [128, 128])
    neg2 = din("neg2", [128, 256])
    eye2 = din("eye2", [128, 256])
    iotam = din("iotam", [1, 16])

    y_out = nc.dram_tensor("y", [NOWN, 128, D], F32, kind="ExternalOutput").ap()
    dbg = None
    if tap is True:
        dbg = nc.dram_tensor("dbg", [NOWN, 128, D], F32, kind="ExternalOutput").ap()
    dbg_ar = None
    if isinstance(tap, str) and (tap.startswith("ar") or tap.startswith("ln")):
        dbg_ar = nc.dram_tensor("dbgar", [128, 36864], BF16, kind="ExternalOutput").ap()

    xt_spill = nc.dram_tensor("xt_spill", [NTC, 128, D], BF16, kind="Internal").ap()
    xt_own = nc.dram_tensor("xt_own", [NOWN, 128, D], BF16, kind="Internal").ap()
    ag_in = nc.dram_tensor("ag_in", [NOWN, 128, D], BF16, kind="Internal").ap()
    ag_out = nc.dram_tensor("ag_out", [2, NOWN, 128, D], BF16, kind="Internal").ap()

    XR = nc.alloc_sbuf_tensor("XR", [128, NOWN * D], F32)
    AR = nc.alloc_sbuf_tensor("AR", [128, 36864], BF16)
    SC = nc.alloc_sbuf_tensor("SC", [128, 4096], F32)
    G1 = nc.alloc_sbuf_tensor("G1", [128, D], F32)
    G2 = nc.alloc_sbuf_tensor("G2", [128, D], F32)
    WS = nc.alloc_sbuf_tensor("WS", [128, 8192], BF16)
    C_ID = nc.alloc_sbuf_tensor("C_ID", [128, 128], BF16)
    C_IDF = nc.alloc_sbuf_tensor("C_IDF", [128, 128], F32)
    C_ON = nc.alloc_sbuf_tensor("C_ON", [1, 128], F32)
    C_BM = nc.alloc_sbuf_tensor("C_BM", [128, 256], F32)
    C_DL = nc.alloc_sbuf_tensor("C_DL", [128, 128], F32)
    C_N2 = nc.alloc_sbuf_tensor("C_N2", [128, 256], F32)
    C_E2 = nc.alloc_sbuf_tensor("C_E2", [128, 256], F32)
    C_IM = nc.alloc_sbuf_tensor("C_IM", [1, 16], F32)
    ST = nc.alloc_sbuf_tensor("ST", [128, 64], F32)
    R1 = nc.alloc_sbuf_tensor("R1", [1, 256], F32)

    pb = [nc.alloc_psum_tensor(f"pb{i}", [128, 512], F32) for i in range(6)]
    pt6 = nc.alloc_psum_tensor("pt6", [128, 1024], BF16)
    pt7 = nc.alloc_psum_tensor("pt7", [128, 1024], BF16)
    pbt = {"t6": pt6, "t7": pt7}

    bank_r = {i: None for i in range(6)}
    bank_r["t6"] = None
    bank_r["t7"] = None

    def T_(fn, waits=(), inc="t", big=True):
        return P.op("tensor", fn, waits, inc, big=big)

    def V_(fn, waits=(), inc="v", big=False):
        return P.op("vector", fn, waits, inc, big=big)

    def S_(fn, waits=(), inc="s", big=False):
        return P.op("scalar", fn, waits, inc, big=big)

    ST_MU, ST_RS, ST_SQ, ST_T0 = 0, 1, 2, 3
    ST_SS, ST_IR = 8, 12
    ST_NM, ST_NT = 16, 20
    ST_LS, ST_LT, ST_LI = 24, 28, 29
    ST_CM, ST_SCL = 32, 33
    ST_M1, ST_M2 = 34, 35
    ST_LAMC = 40
    ST_ETAC = 44
    SC_DUMP = 2048

    def st(c, n=1):
        return ST[:, c:c + n]
    DBG_LN = [bool(tap == "ln1t")]

    def emit_ln(x_ap, out_ap, ready):
        mu, rs, sq, t0 = st(ST_MU), st(ST_RS), st(ST_SQ), st(ST_T0)
        dump = SC[:, SC_DUMP:SC_DUMP + D]
        tmu = S_(lambda e: e.activation(dump, x_ap, AF.Copy, accum_out=mu), ready, big=True)
        tsq = S_(lambda e: e.activation(dump, x_ap, AF.Square, accum_out=sq), ready, big=True)
        V_(lambda e: e.tensor_scalar(mu, mu, 1.0 / D, None, ALU.mult), [tmu])
        tv = V_(lambda e: e.tensor_scalar(sq, sq, 1.0 / D, None, ALU.mult), [tsq])
        if DBG_LN[0]:
            V_(lambda e: e.tensor_copy(st(16), mu))
            V_(lambda e: e.tensor_copy(st(17), sq), [tv])
        V_(lambda e: e.tensor_tensor(t0, mu, mu, ALU.mult))
        if DBG_LN[0]:
            V_(lambda e: e.tensor_copy(st(18), t0))
        V_(lambda e: e.tensor_tensor(t0, sq, t0, ALU.subtract), [tv])
        if DBG_LN[0]:
            V_(lambda e: e.tensor_copy(st(19), t0))
        tve = V_(lambda e: e.tensor_scalar(t0, t0, 1.0, EPS, ALU.mult, ALU.add))
        if DBG_LN[0]:
            V_(lambda e: e.tensor_copy(st(20), t0))
        ts2 = S_(lambda e: e.sqrt(t0, t0), [tve])
        if DBG_LN[0]:
            V_(lambda e: e.tensor_copy(st(21), t0), [ts2])
        V_(lambda e: e.reciprocal(rs, t0), [ts2])
        return V_(lambda e: e.tensor_scalar(out_ap, x_ap, mu, rs, ALU.subtract, ALU.mult),
                  list(ready), big=True)

    tr_flip = [0]

    def emit_tr(src_base_ap, dst_ap, n, ready):
        toks = []
        done = 0
        while done < n:
            take = min(4, n - done)
            bank = "t6" if tr_flip[0] % 2 == 0 else "t7"
            tr_flip[0] += 1
            pbank = pbt[bank]
            ttok = None
            for j in range(take):
                w = list(ready) + [bank_r[bank]] if j == 0 else []
                ttok = T_(lambda e, j=j, dj=done, pbk=pbank: e.transpose(
                    pbk[:, j * 128:(j + 1) * 128],
                    src_base_ap[:, (dj + j) * 128:(dj + j + 1) * 128], C_ID[:]),
                    w, inc="t" if j == take - 1 else None)
            ev = V_(lambda e, dj=done, tk=take, pbk=pbank: e.tensor_copy(
                dst_ap[:, dj * 128:(dj + tk) * 128], pbk[:, :tk * 128]), [ttok], big=True)
            bank_r[bank] = ev
            toks.append(ev)
            done += take
        return toks[-1]

    tok_const = [
        P.dma("gpsimd", C_ID[:], ident[:], sem="dg"),
        P.dma("sync", C_IDF[:], ident[:], sem="dy"),
        P.dma("sync", C_ON[:], onesr[:], sem="dy"),
        P.dma("sync", C_BM[:], bmask[:], sem="dy"),
        P.dma("sync", C_DL[:], delta[:], sem="dy"),
        P.dma("sync", C_N2[:], neg2[:], sem="dy"),
        P.dma("sync", C_E2[:], eye2[:], sem="dy"),
        P.dma("sync", C_IM[:], iotam[:], sem="dy"),
    ]
    tok_g1 = P.dma("sync", G1[:], gqt[:], sem="dy")
    tok_g2 = P.dma("sync", G2[:], gkt[:], sem="dy")

    def dy_barrier():
        return P.op("sync", lambda e: e.nop(),
                    waits=[("dy", P.cnt["dy"], "sync")], inc="b")

    b_const = dy_barrier()
    tok_const = [b_const]
    tok_g1 = b_const
    tok_g2 = b_const

    A_LN = 0
    A_ROW = 2048
    A_XF = 6144

    def xf32(k):
        return AR[:, A_XF + k * 4096: A_XF + (k + 1) * 4096].bitcast(F32)

    spill_last = None
    xtile_free = [None, None]
    row_free = [None, None]
    ln_free = [None]

    def p0_tile(src_dram, dst_dram, idx, k):
        nonlocal spill_last
        tdma = P.dma("sync", xf32(k), src_dram[idx], waits=[xtile_free[k]],
                     sem="dx" if k == 0 else "dx2")
        ln_ap = AR[:, A_LN:A_LN + D]
        t_ln = emit_ln(xf32(k), ln_ap, [tdma, ln_free[0]] + tok_const)
        xtile_free[k] = t_ln
        rb = AR[:, A_ROW + k * 2048:A_ROW + k * 2048 + D]
        ev = emit_tr(ln_ap, rb, 16, [t_ln, row_free[k]])
        ln_free[0] = ev
        spill_last = P.dma("sync", dst_dram[idx], rb, waits=[ev],
                           sem="dsp" if k == 0 else "dsp2")
        row_free[k] = spill_last

    n_p0 = NTC
    if isinstance(tap, str) and tap.startswith("ln1t"):
        n_p0 = 1
    for tc in range(n_p0):
        p0_tile(x_full, xt_spill, tc, tc % 2)
    if n_p0 == NTC:
        for i in range(NOWN):
            p0_tile(x_own, xt_own, i, i % 2)
    if tap == "ln1t":
        P.dma("sync", dbg_ar[:, :2048], AR[:, :2048],
              waits=[("dsp", P.cnt["dsp"], "sync")], sem="dout")
        P.dma("gpsimd", dbg_ar[:, 2048:2080], ST[:, 0:32],
              waits=[("dsp", P.cnt["dsp"], "sync")], sem="dout")
        P.op("sync", lambda e: e.nop(), waits=[("dout", P.cnt["dout"], "sync")])
    spill_all = [("dsp", P.cnt.get("dsp", 0), "sync"), ("dsp2", P.cnt.get("dsp2", 0), "sync")]

    if tap == "ar_raw":
        P.dma("sync", dbg_ar[:, :6144], AR[:, :6144], waits=spill_all, sem="dout")
        P.op("sync", lambda e: e.nop(), waits=[("dout", P.cnt["dout"], "sync")])
    if tap == "ar_xt":
        rb1 = []
        for tc in range(8):
            rb1.append(P.dma("sync", AR[:, tc * 2048:(tc + 1) * 2048],
                             xt_spill[tc], waits=spill_all, sem="dy"))
        for i in range(8):
            rb1.append(P.dma("sync", AR[:, 16384 + i * 2048:16384 + (i + 1) * 2048],
                             xt_own[i], waits=spill_all, sem="dy"))
        P.op("sync", lambda e: e.nop(), waits=[("dy", P.cnt["dy"], "sync")])
        P.dma("sync", dbg_ar, AR[:], waits=rb1, sem="dout")
        P.op("sync", lambda e: e.nop(), waits=[("dout", P.cnt["dout"], "sync")])

    A_XTW = 0
    A_KT = 8192
    A_V = 16384
    A_QT = 24576
    A_P = 28672
    A_PT = 30720
    A_AOT = 32768

    S_VAL, S_TMP, S_SQ = 0, 512, 1024

    xr_tok = [[None] * 4 for _ in range(NOWN)]
    if upto in ("p1", "p2", "p3", "p4"):
        for i in range(NOWN):
            P.dma("sync", XR[:, i * D:(i + 1) * D], x_own[i], sem="dy")
        b_xr = dy_barrier()
        for i in range(NOWN):
            for s4 in range(4):
                xr_tok[i][s4] = b_xr

        ws_free = None
        wave_free = [None, None]
        qk_guard = [None]
        stage_guard = [None]

        def wave_load(src_dram, w, k, sem):
            return P.dma("sync",
                         AR[:, A_XTW + k * 4096:A_XTW + (k + 1) * 4096]
                         .rearrange("p (dc tc f) -> p dc tc f", tc=2, f=128),
                         src_dram[2 * w:2 * w + 2]
                         .rearrange("tc p (dc f) -> p dc tc f", f=128),
                         waits=[wave_free[k]] + spill_all, sem=sem)

        def proj(kind, g, wsrc):
            nonlocal ws_free
            wtok = P.dma("gpsimd", WS[:].rearrange("p (dc o) -> p dc o", o=512),
                         wsrc[g].rearrange("dc p o -> p dc o"),
                         waits=[ws_free], sem="dwa")
            nw = 4 if kind == "q" else 8
            src = xt_own if kind == "q" else xt_spill
            lastev = None
            wtoks = {}
            for w in range(min(nw, 2)):
                wtoks[w] = wave_load(src, w, w % 2, "dx" if w % 2 == 0 else "dx2")
            emitted = set(range(min(nw, 2)))
            for w in range(nw):
                k = w % 2
                for tt in range(2):
                    tcg = 2 * w + tt
                    bk = (2 * w + tt) % 4
                    mtok = None
                    for dc in range(16):
                        wts = [bank_r[bk], wtoks[w], wtok] if dc == 0 else []
                        mtok = T_(lambda e, bk=bk, k=k, dc=dc, tt=tt:
                                  e.matmul(pb[bk][:],
                                           AR[:, A_XTW + k * 4096 + dc * 256 + tt * 128:
                                              A_XTW + k * 4096 + dc * 256 + tt * 128 + 128],
                                           WS[:, dc * 512:(dc + 1) * 512],
                                           start=(dc == 0), stop=(dc == 15)),
                                  wts, inc="t" if dc == 15 else None)
                    lastev = proj_out(kind, g, tcg, bk, mtok)
                wave_free[k] = lastev
                nxt = w + 2
                if nxt < nw and nxt not in emitted:
                    wtoks[nxt] = wave_load(src, nxt, nxt % 2,
                                           "dx" if nxt % 2 == 0 else "dx2")
                    emitted.add(nxt)
            ws_free = lastev
            return lastev

        def proj_out(kind, g, tcg, bk, mtok):
            if kind == "v":
                ev = V_(lambda e: e.tensor_copy(
                    AR[:, A_V + tcg * 512:A_V + tcg * 512 + 512], pb[bk][:]), [mtok], big=True)
                bank_r[bk] = ev
                return ev
            val = SC[:, S_VAL:S_VAL + 512]
            tmp = SC[:, S_TMP:S_TMP + 512]
            sqv = SC[:, S_SQ:S_SQ + 512]
            ss, ir = st(ST_SS, 4), st(ST_IR, 4)
            t1 = V_(lambda e, bk=bk: e.tensor_copy(val, pb[bk][:]),
                    [mtok, qk_guard[0]], big=True)
            bank_r[bk] = t1
            t2 = S_(lambda e: e.square(sqv, val), [t1], big=True)
            t3 = V_(lambda e: e.tensor_reduce(
                ss, sqv.rearrange("p (a b) -> p a b", b=128),
                axis=AX.X, op=ALU.add), [t2])
            V_(lambda e: e.tensor_scalar(ss, ss, 1.0 / 128.0, EPS, ALU.mult, ALU.add), [t3])
            t4 = V_(lambda e: e.reciprocal(ss, ss))
            t5 = S_(lambda e: e.sqrt(ir, ss), [t4])
            for hh in range(4):
                V_(lambda e, hh=hh: e.tensor_scalar(
                    tmp[:, hh * 128:(hh + 1) * 128], val[:, hh * 128:(hh + 1) * 128],
                    ir[:, hh:hh + 1], None, ALU.mult), [t5], big=(hh > 0))
            gm = G1 if kind == "q" else G2
            stage = AR[:, A_P:A_P + 512]
            t7 = V_(lambda e: e.tensor_tensor(
                stage, tmp, gm[:, g * 512:(g + 1) * 512], ALU.mult),
                [tok_g1, stage_guard[0]], big=True)
            qk_guard[0] = t2
            dst_base = A_QT if kind == "q" else A_KT
            per = 1024 if kind == "q" else 2048
            toks = []
            bank = "t6" if tr_flip[0] % 2 == 0 else "t7"
            tr_flip[0] += 1
            pbank = pbt[bank]
            ttok = None
            for j in range(4):
                w = [t7, bank_r[bank]] if j == 0 else []
                ttok = T_(lambda e, j=j, pbk=pbank: e.transpose(
                    pbk[:, j * 128:(j + 1) * 128],
                    stage[:, j * 128:(j + 1) * 128], C_ID[:]),
                    w, inc="t" if j == 3 else None)
            stage_guard[0] = ttok
            evl = None
            for j in range(4):
                evl = V_(lambda e, j=j, pbk=pbank: e.tensor_copy(
                    AR[:, dst_base + j * per + tcg * 128:
                       dst_base + j * per + tcg * 128 + 128],
                    pbk[:, j * 128:(j + 1) * 128]), [ttok], big=True)
            bank_r[bank] = evl
            return evl

        for g in range(4):
            tq = proj("q", g, wq)
            tk = proj("k", g, wk)
            tv = proj("v", g, wv)
            proj_done = [tq, tk, tv]
            if tap == "ar_qkv" and g == 0:
                P.dma("sync", dbg_ar[:, :28672], AR[:, :28672],
                      waits=proj_done, sem="dout")
                P.op("sync", lambda e: e.nop(),
                     waits=[("dout", P.cnt["dout"], "sync")])
                break

            wo_tok = P.dma("gpsimd",
                           WS[:].rearrange("p (s hc o) -> p s hc o", hc=4, o=512),
                           wo[g].rearrange("s hc p o -> p s hc o"),
                           waits=[ws_free], sem="dwb")

            pv_flip = 0
            aot_tok = []
            p_guard = [None]
            for i in range(NOWN):
                ncol = 256 * (i + 1)
                nch = 2 * i + 2
                nsl = (ncol + 511) // 512
                for h in range(4):
                    qt_ap = AR[:, A_QT + h * 1024 + i * 128:
                               A_QT + h * 1024 + i * 128 + 128]
                    smm = []
                    for sl in range(nsl):
                        cn = min(512, ncol - sl * 512)
                        tkn = T_(lambda e, sl=sl, h=h, cn=cn, q=qt_ap: e.matmul(
                            pb[sl][:, :cn], q,
                            AR[:, A_KT + h * 2048 + sl * 512:
                               A_KT + h * 2048 + sl * 512 + cn],
                            start=True, stop=True),
                            [bank_r[sl]] + (proj_done if (i == 0 and h == 0) else []),
                            inc="t")
                        smm.append(tkn)
                    sl_m, off_m = (256 * i) // 512, (256 * i) % 512
                    tmask = V_(lambda e, sl=sl_m, off=off_m: e.tensor_tensor(
                        pb[sl][:, off:off + 256], pb[sl][:, off:off + 256],
                        C_BM[:], ALU.add), [smm[sl_m]], big=True)
                    rmax = []
                    for sl in range(nsl):
                        cn = min(512, ncol - sl * 512)
                        rmax.append(V_(lambda e, sl=sl, cn=cn: e.reduce_max(
                            st(ST_NM + sl), pb[sl][:, :cn], axis=AX.X, negate=True),
                            [smm[sl], tmask]))
                    tot = rmax[0]
                    for sl in range(1, nsl):
                        tot = V_(lambda e, sl=sl: e.tensor_tensor(
                            st(ST_NM), st(ST_NM), st(ST_NM + sl), ALU.min), [rmax[sl]])
                    tb = V_(lambda e: e.tensor_scalar(
                        st(ST_NT), st(ST_NM), ISQ, None, ALU.mult), [tot])
                    exps = []
                    for sl in range(nsl):
                        cn = min(512, ncol - sl * 512)
                        exps.append(S_(lambda e, sl=sl, cn=cn: e.activation(
                            AR[:, A_P + sl * 512:A_P + sl * 512 + cn],
                            pb[sl][:, :cn], AF.Exp, bias=st(ST_NT), scale=ISQ,
                            accum_out=st(ST_LS + sl)), [tb, tmask, p_guard[0]], big=True))
                        bank_r[sl] = exps[-1]
                    tl = exps[0]
                    V_(lambda e: e.tensor_copy(st(ST_LT), st(ST_LS)), [exps[0]])
                    for sl in range(1, nsl):
                        tl = V_(lambda e, sl=sl: e.tensor_tensor(
                            st(ST_LT), st(ST_LT), st(ST_LS + sl), ALU.add), [exps[sl]])
                    tli = V_(lambda e: e.reciprocal(st(ST_LI), st(ST_LT)), [tl])
                    psc = []
                    for sl in range(nsl):
                        cn = min(512, ncol - sl * 512)
                        psc.append(V_(lambda e, sl=sl, cn=cn: e.tensor_scalar(
                            AR[:, A_P + sl * 512:A_P + sl * 512 + cn],
                            AR[:, A_P + sl * 512:A_P + sl * 512 + cn],
                            st(ST_LI), None, ALU.mult), [tli, exps[sl]], big=(sl > 0)))
                    pt_tok = emit_tr(AR[:, A_P:A_P + ncol], AR[:, A_PT:A_PT + ncol],
                                     nch, psc)
                    p_guard[0] = pt_tok
                    pvb = 4 + (pv_flip % 2)
                    pv_flip += 1
                    mtok = None
                    for kc in range(nch):
                        w = [bank_r[pvb], pt_tok] if kc == 0 else []
                        mtok = T_(lambda e, kc=kc, h=h, pvb=pvb, nch=nch: e.matmul(
                            pb[pvb][:, :128],
                            AR[:, A_V + kc * 512 + h * 128:A_V + kc * 512 + h * 128 + 128],
                            AR[:, A_PT + kc * 128:A_PT + kc * 128 + 128],
                            start=(kc == 0), stop=(kc == nch - 1)),
                            w, inc="t" if kc == nch - 1 else None)
                    ev = V_(lambda e, h=h, i=i, pvb=pvb: e.tensor_copy(
                        AR[:, A_AOT + h * 1024 + i * 128:A_AOT + h * 1024 + i * 128 + 128],
                        pb[pvb][:, :128]), [mtok], big=True)
                    bank_r[pvb] = ev
                    aot_tok.append(ev)

            last_wo = None
            for s4 in range(4):
                for i in range(NOWN):
                    bk = i % 4
                    mtok = None
                    for hc in range(4):
                        w = [bank_r[bk], wo_tok] + (aot_tok if (s4 == 0 and i == 0) else [])
                        w = w if hc == 0 else []
                        mtok = T_(lambda e, bk=bk, hc=hc, i=i, s4=s4: e.matmul(
                            pb[bk][:],
                            AR[:, A_AOT + hc * 1024 + i * 128:
                               A_AOT + hc * 1024 + i * 128 + 128],
                            WS[:, s4 * 2048 + hc * 512:s4 * 2048 + hc * 512 + 512],
                            start=(hc == 0), stop=(hc == 3)),
                            w, inc="t" if hc == 3 else None)
                    ev = V_(lambda e, bk=bk, i=i, s4=s4: e.tensor_tensor(
                        XR[:, i * D + s4 * 512:i * D + s4 * 512 + 512],
                        pb[bk][:],
                        XR[:, i * D + s4 * 512:i * D + s4 * 512 + 512], ALU.add),
                        [mtok, xr_tok[i][s4]], big=True)
                    bank_r[bk] = ev
                    xr_tok[i][s4] = ev
                    last_wo = ev
            ws_free = last_wo

    A1_H = 0
    A1_W1 = 11264
    A1_W3 = 15360
    A1_W2 = 19456
    A1_XN = 30720
    A1_G8 = 32768

    if upto in ("p2", "p3", "p4"):
        g1b = G1[:].bitcast(BF16)
        g2b = G2[:].bitcast(BF16)

        def xt1_strip(dc):
            if dc < 8:
                return WS[:, dc * 1024:(dc + 1) * 1024]
            if dc < 12:
                return g1b[:, (dc - 8) * 1024:(dc - 7) * 1024]
            return g2b[:, (dc - 12) * 1024:(dc - 11) * 1024]

        tgw = P.dma("gpsimd",
                    AR[:, A1_G8:A1_G8 + 128].rearrange("p (dc o) -> p dc o", o=NE),
                    gate_wd.rearrange("dc p o -> p dc o"),
                    waits=[ws_free], sem="dg")

        xt1_tok = []
        for i in range(NOWN):
            xn = AR[:, A1_XN:A1_XN + D]
            tln = emit_ln(XR[:, i * D:(i + 1) * D], xn,
                          xr_tok[i] + [xt1_tok[-1] if xt1_tok else ws_free])
            done = 0
            while done < 16:
                bank = "t6" if tr_flip[0] % 2 == 0 else "t7"
                tr_flip[0] += 1
                pbank = pbt[bank]
                ttok = None
                for j in range(4):
                    w = [tln, bank_r[bank]] if j == 0 else []
                    ttok = T_(lambda e, j=j, dj=done, pbk=pbank, xn=xn: e.transpose(
                        pbk[:, j * 128:(j + 1) * 128],
                        xn[:, (dj + j) * 128:(dj + j + 1) * 128], C_ID[:]),
                        w, inc="t" if j == 3 else None)
                evl = None
                for j in range(4):
                    evl = V_(lambda e, j=j, dj=done, i=i, pbk=pbank: e.tensor_copy(
                        xt1_strip(dj + j)[:, i * 128:(i + 1) * 128],
                        pbk[:, j * 128:(j + 1) * 128]), [ttok, tok_g1, tok_g2], big=True)
                bank_r[bank] = evl
                xt1_tok.append(evl)
                done += 4
        xt1_all = xt1_tok[-1]

        SC_CW = 1536
        cw_tok = [None] * NOWN
        for i in range(NOWN):
            mtok = None
            for dc in range(16):
                w = [xt1_all, tgw, bank_r[5]] if dc == 0 else []
                mtok = T_(lambda e, dc=dc, i=i: e.matmul(
                    pb[5][:, :NE],
                    xt1_strip(dc)[:, i * 128:(i + 1) * 128],
                    AR[:, A1_G8 + dc * NE:A1_G8 + (dc + 1) * NE],
                    start=(dc == 0), stop=(dc == 15)),
                    w, inc="t" if dc == 15 else None)
            pr = SC[:, SC_CW + i * NE:SC_CW + (i + 1) * NE]
            t1 = V_(lambda e: e.reduce_max(st(ST_M1), pb[5][:, :NE], axis=AX.X,
                                           negate=True), [mtok])
            t2 = S_(lambda e, pr=pr: e.activation(pr, pb[5][:, :NE], AF.Exp,
                                                  bias=st(ST_M1),
                                                  accum_out=st(ST_M2)), [t1])
            bank_r[5] = t2
            t3 = V_(lambda e: e.reciprocal(st(ST_M2), st(ST_M2)), [t2])
            t4 = V_(lambda e, pr=pr: e.tensor_scalar(pr, pr, st(ST_M2), None, ALU.mult), [t3])
            tmp8 = SC[:, SC_DUMP:SC_DUMP + NE]
            t5 = V_(lambda e, pr=pr: e.reduce_max(st(ST_M1), pr, axis=AX.X), [t4])
            t6 = V_(lambda e, pr=pr: e.tensor_scalar(tmp8, pr, st(ST_M1), None, ALU.is_ge), [t5])
            t7 = V_(lambda e, pr=pr: e.scalar_tensor_tensor(
                tmp8, tmp8, -1.0e9, pr, ALU.mult, ALU.add), [t6])
            t8 = V_(lambda e: e.reduce_max(st(ST_M2), tmp8, axis=AX.X), [t7])
            t9 = V_(lambda e, pr=pr: e.tensor_scalar(tmp8, pr, st(ST_M2), None, ALU.is_ge), [t8])
            cw_tok[i] = V_(lambda e, pr=pr: e.tensor_tensor(pr, pr, tmp8, ALU.mult), [t9])

        w1_free = [None, None]
        w3_free = [None, None]
        w2_free = [None, None]
        h_free = [None] * 11

        def expert_pass(ew1, ew3, ew2, eidx, cw_col, n_fc, fc0):
            h_toks = []
            for fc in range(n_fc):
                kb = fc % 2
                t1w = P.dma("gpsimd",
                            AR[:, A1_W1 + kb * 2048:A1_W1 + (kb + 1) * 2048]
                            .rearrange("p (dc f) -> p dc f", f=128),
                            ew1[fc].rearrange("dc p f -> p dc f"),
                            waits=[w1_free[kb]], sem="dm1a" if kb == 0 else "dm1b")
                t3w = P.dma("gpsimd",
                            AR[:, A1_W3 + kb * 2048:A1_W3 + (kb + 1) * 2048]
                            .rearrange("p (dc f) -> p dc f", f=128),
                            ew3[fc].rearrange("dc p f -> p dc f"),
                            waits=[w3_free[kb]], sem="dm3a" if kb == 0 else "dm3b")
                ma = mb = None
                for sl in range(2):
                    for dc in range(16):
                        w = [bank_r[sl], t1w, xt1_all] if dc == 0 else []
                        ma = T_(lambda e, sl=sl, dc=dc, kb=kb: e.matmul(
                            pb[sl][:],
                            AR[:, A1_W1 + kb * 2048 + dc * 128:
                               A1_W1 + kb * 2048 + (dc + 1) * 128],
                            xt1_strip(dc)[:, sl * 512:(sl + 1) * 512],
                            start=(dc == 0), stop=(dc == 15)),
                            w, inc="t" if dc == 15 else None)
                    bank_r[sl] = ma
                for sl in range(2):
                    for dc in range(16):
                        w = [bank_r[2 + sl], t3w] if dc == 0 else []
                        mb = T_(lambda e, sl=sl, dc=dc, kb=kb: e.matmul(
                            pb[2 + sl][:],
                            AR[:, A1_W3 + kb * 2048 + dc * 128:
                               A1_W3 + kb * 2048 + (dc + 1) * 128],
                            xt1_strip(dc)[:, sl * 512:(sl + 1) * 512],
                            start=(dc == 0), stop=(dc == 15)),
                            w, inc="t" if dc == 15 else None)
                hts = []
                for sl in range(2):
                    sil = SC[:, sl * 512:(sl + 1) * 512]
                    tsl = S_(lambda e, sl=sl, sil=sil: e.activation(
                        sil, pb[sl][:], AF.Silu), [ma, mb, h_free[fc]], big=True)
                    bank_r[sl] = tsl
                    hv = V_(lambda e, sl=sl, fc=fc, sil=sil: e.tensor_tensor(
                        AR[:, A1_H + fc * 1024 + sl * 512:A1_H + fc * 1024 + (sl + 1) * 512],
                        sil, pb[2 + sl][:], ALU.mult), [tsl], big=True)
                    bank_r[2 + sl] = hv
                    hts.append(hv)
                w1_free[kb] = hts[-1]
                w3_free[kb] = hts[-1]
                h_toks.append(hts[-1])
            for s4 in range(4):
                kb = s4 % 2
                t2w = P.dma("gpsimd",
                            AR[:, A1_W2 + kb * 5632:A1_W2 + kb * 5632 + n_fc * 512]
                            .rearrange("p (fc o) -> p fc o", o=512),
                            ew2[s4].rearrange("fc p o -> p fc o"),
                            waits=[w2_free[kb]], sem="dm2a" if kb == 0 else "dm2b")
                lastmm = None
                for i in range(NOWN):
                    bk = 4 + (i % 2)
                    mtok = None
                    for fc in range(n_fc):
                        w = ([bank_r[bk], t2w, h_toks[-1]] if fc == 0 else [])
                        mtok = T_(lambda e, bk=bk, fc=fc, i=i, kb=kb: e.matmul(
                            pb[bk][:],
                            AR[:, A1_H + fc * 1024 + i * 128:A1_H + fc * 1024 + i * 128 + 128],
                            AR[:, A1_W2 + kb * 5632 + fc * 512:
                               A1_W2 + kb * 5632 + (fc + 1) * 512],
                            start=(fc == 0), stop=(fc == n_fc - 1)),
                            w, inc="t" if fc == n_fc - 1 else None)
                    lastmm = mtok
                    if cw_col is None:
                        ev = V_(lambda e, bk=bk, i=i, s4=s4: e.tensor_tensor(
                            XR[:, i * D + s4 * 512:i * D + s4 * 512 + 512],
                            pb[bk][:],
                            XR[:, i * D + s4 * 512:i * D + s4 * 512 + 512],
                            ALU.add), [mtok, xr_tok[i][s4]], big=True)
                    else:
                        ev = V_(lambda e, bk=bk, i=i, s4=s4: e.scalar_tensor_tensor(
                            XR[:, i * D + s4 * 512:i * D + s4 * 512 + 512],
                            pb[bk][:],
                            SC[:, SC_CW + i * NE + eidx:SC_CW + i * NE + eidx + 1],
                            XR[:, i * D + s4 * 512:i * D + s4 * 512 + 512],
                            ALU.mult, ALU.add), [mtok, xr_tok[i][s4], cw_tok[i]], big=True)
                    bank_r[bk] = ev
                    xr_tok[i][s4] = ev
                w2_free[kb] = ev
            for fc in range(n_fc):
                h_free[fc] = lastmm
            return lastmm

        moe_free = expert_pass([sw1[fc] for fc in range(11)],
                               [sw3[fc] for fc in range(11)],
                               [sw2[s][0:11] for s in range(4)], None, None, 11, 0)
        moe_free = expert_pass([sw1[fc] for fc in range(11, 22)],
                               [sw3[fc] for fc in range(11, 22)],
                               [sw2[s][11:22] for s in range(4)], None, None, 11, 0)
        for e_ in range(NE):
            moe_free = expert_pass([rw1[e_][fc] for fc in range(11)],
                                   [rw3[e_][fc] for fc in range(11)],
                                   [rw2[e_][s] for s in range(4)], e_, True, 11, 0)

    A2_XT = 0
    A2_ZS = 16384
    A2_ZF = 0
    A2_MT = 32768
    SC_AK = 2048
    SC_ARG = 2560
    SC_TK = 2688
    SC_ACC = 2816
    SC_BC = 2944
    SC_H = 0

    if upto in ("p3", "p4"):
        tg1 = P.dma("sync", G1[:], l1gam[:], waits=[xr_tok[0][0], moe_free],
                    sem="dy")
        tg2 = P.dma("sync", G2[:], l1bet[:], sem="dy")
        bg3 = dy_barrier()
        tg1 = tg2 = bg3


        xt2_tok = []
        for i in range(NOWN):
            xn = AR[:, A2_ZS:A2_ZS + D]
            tln = emit_ln(XR[:, i * D:(i + 1) * D], xn,
                          xr_tok[i] + ([xt2_tok[-1]] if xt2_tok else []))
            done = 0
            while done < 16:
                bank = "t6" if tr_flip[0] % 2 == 0 else "t7"
                tr_flip[0] += 1
                pbank = pbt[bank]
                ttok = None
                for j in range(4):
                    w = [tln, bank_r[bank]] if j == 0 else []
                    ttok = T_(lambda e, j=j, dj=done, pbk=pbank, xn=xn: e.transpose(
                        pbk[:, j * 128:(j + 1) * 128],
                        xn[:, (dj + j) * 128:(dj + j + 1) * 128], C_ID[:]),
                        w, inc="t" if j == 3 else None)
                evl = None
                for j in range(4):
                    evl = V_(lambda e, j=j, dj=done, i=i, pbk=pbank: e.tensor_copy(
                        AR[:, A2_XT + (dj + j) * 1024 + i * 128:
                           A2_XT + (dj + j) * 1024 + i * 128 + 128],
                        pbk[:, j * 128:(j + 1) * 128]), [ttok], big=True)
                bank_r[bank] = evl
                xt2_tok.append(evl)
                done += 4
        xt2_all = xt2_tok[-1]

        zdma = []
        zproj_last = None
        tl1w = None
        zst_free = [None, None]
        for s4 in range(4):
            tl1w = P.dma("gpsimd",
                         WS[:].rearrange("p (dc o) -> p dc o", o=512),
                         l1w[s4].rearrange("dc p o -> p dc o"),
                         waits=[moe_free if s4 == 0 else zproj_last], sem="dwa")
            for i in range(NOWN):
                bk = i % 4
                mtok = None
                for dc in range(16):
                    w = [bank_r[bk], tl1w, xt2_all] if dc == 0 else []
                    mtok = T_(lambda e, bk=bk, dc=dc, i=i: e.matmul(
                        pb[bk][:],
                        AR[:, A2_XT + dc * 1024 + i * 128:A2_XT + dc * 1024 + i * 128 + 128],
                        WS[:, dc * 512:(dc + 1) * 512],
                        start=(dc == 0), stop=(dc == 15)),
                        w, inc="t" if dc == 15 else None)
                zproj_last = mtok
                ko = i % 2
                ev = V_(lambda e, bk=bk, ko=ko: e.tensor_copy(
                    AR[:, A2_ZS + ko * 512:A2_ZS + (ko + 1) * 512], pb[bk][:]),
                    [mtok, zst_free[ko]], big=True)
                bank_r[bk] = ev
                zdma.append(P.dma("sync", ag_in[i][:, s4 * 512:(s4 + 1) * 512],
                                  AR[:, A2_ZS + ko * 512:A2_ZS + (ko + 1) * 512],
                                  waits=[ev], sem="dzc"))
                zst_free[ko] = zdma[-1]

        t_ag = P.op("gpsimd", lambda e: e.collective_compute(
            "AllGather", ALU.bypass,
            replica_groups=[[0, 1], [2, 3], [4, 5], [6, 7]],
            ins=[ag_in], outs=[ag_out]),
            waits=[("dzc", P.cnt["dzc"], "sync")], inc="ag", inc_amt=1)

        tcm = V_(lambda e: e.memset(st(ST_CM), 0.0))
        ab_free = None
        for dc in range(NTC):
            tda = P.dma("sync", SC[:, 0:D], ffnw_abs[dc], waits=[ab_free], sem="dab")
            trd = V_(lambda e: e.reduce_max(st(ST_T0), SC[:, 0:D], axis=AX.X,
                                            apply_absolute_value=True), [tda, tcm])
            ab_free = V_(lambda e: e.tensor_tensor(st(ST_CM), st(ST_CM), st(ST_T0),
                                                   ALU.max), [trd])
        tt1 = T_(lambda e: e.matmul(pb[5][0:1, :128], st(ST_CM), C_IDF[:],
                                    start=True, stop=True),
                 [ab_free, bank_r[5]] + tok_const)
        tt2 = V_(lambda e: e.reduce_max(R1[0:1, 128:129], pb[5][0:1, :128], axis=AX.X), [tt1])
        bank_r[5] = tt2
        tt3 = T_(lambda e: e.matmul(pb[5][:, :1], C_ON[:], R1[0:1, 128:129],
                                    start=True, stop=True), [tt2, bank_r[5]])
        tscl = V_(lambda e: e.tensor_copy(st(ST_SCL), pb[5][:, :1]), [tt3])
        bank_r[5] = tscl

        P.dma("sync", R1[0:1, 32:36], lam_logit[:], sem="dy")
        P.dma("sync", R1[0:1, 64:68], eta_row[:], sem="dy")
        tll = tet = dy_barrier()
        ts1 = S_(lambda e: e.activation(R1[0:1, 36:40], R1[0:1, 32:36], AF.Sigmoid), [tll])
        ts2b = V_(lambda e: e.tensor_scalar(R1[0:1, 36:40], R1[0:1, 36:40], 1.0e-30,
                                            None, ALU.max), [ts1])
        ts3 = S_(lambda e: e.activation(R1[0:1, 68:72], R1[0:1, 36:40], AF.Ln), [ts2b])
        rv = []
        for k4 in range(4):
            rv.append(V_(lambda e, k4=k4: e.tensor_scalar(
                R1[0:1, k4 * 16:(k4 + 1) * 16], C_IM[:],
                R1[0:1, 68 + k4:69 + k4], None, ALU.mult), [ts3] + tok_const))
        tbc = T_(lambda e: e.matmul(pb[5][:, :72], C_ON[:], R1[0:1, 0:72],
                                    start=True, stop=True),
                 rv + [tet, ts3, bank_r[5]])
        tbce = V_(lambda e: e.tensor_copy(SC[:, SC_BC:SC_BC + 72], pb[5][:, :72]), [tbc])
        bank_r[5] = tbce

        ak_tok = []
        for k4 in range(4):
            ak_tok.append(V_(lambda e, k4=k4: e.tensor_scalar(
                SC[:, SC_AK + k4 * 128:SC_AK + (k4 + 1) * 128], C_DL[:],
                SC[:, SC_BC + 68 + k4:SC_BC + 69 + k4], None, ALU.mult),
                [tbce] + tok_const))
        mt_tok = []
        for m in range(16):
            tacc = V_(lambda e: e.memset(SC[:, SC_ACC:SC_ACC + 128], 0.0),
                      [mt_tok[-1] if mt_tok else None])
            for k4 in range(4):
                if m < 2:
                    targ = V_(lambda e, k4=k4, m=m: e.tensor_tensor(
                        SC[:, SC_ARG:SC_ARG + 128],
                        SC[:, SC_AK + k4 * 128:SC_AK + (k4 + 1) * 128],
                        C_N2[:, m * 128:(m + 1) * 128], ALU.add), [ak_tok[k4]], big=True)
                    src = SC[:, SC_ARG:SC_ARG + 128]
                else:
                    targ = ak_tok[k4]
                    src = SC[:, SC_AK + k4 * 128:SC_AK + (k4 + 1) * 128]
                tex = S_(lambda e, src=src, k4=k4, m=m: e.activation(
                    SC[:, SC_TK:SC_TK + 128], src, AF.Exp,
                    bias=SC[:, SC_BC + 16 * k4 + m:SC_BC + 16 * k4 + m + 1]),
                    [targ, tacc], big=True)
                tacc = V_(lambda e, k4=k4: e.scalar_tensor_tensor(
                    SC[:, SC_ACC:SC_ACC + 128], SC[:, SC_TK:SC_TK + 128],
                    SC[:, SC_BC + 64 + k4:SC_BC + 64 + k4 + 1],
                    SC[:, SC_ACC:SC_ACC + 128], ALU.mult, ALU.add), [tex], big=True)
            if m < 2:
                tacc = V_(lambda e, m=m: e.tensor_tensor(
                    SC[:, SC_ACC:SC_ACC + 128], SC[:, SC_ACC:SC_ACC + 128],
                    C_E2[:, m * 128:(m + 1) * 128], ALU.add), [tacc], big=True)
            mt_tok.append(V_(lambda e, m=m: e.tensor_copy(
                AR[:, A2_MT + m * 128:A2_MT + (m + 1) * 128],
                SC[:, SC_ACC:SC_ACC + 128]), [tacc], big=True))
        mt_all = mt_tok[-1]

        zrb = []
        agw = [t_ag, zproj_last]
        for g2 in range(NTC):
            p2, idx = g2 % 2, g2 // 2
            zrb.append(P.dma("sync", AR[:, A2_ZF + g2 * 2048:A2_ZF + (g2 + 1) * 2048],
                             ag_out[p2, idx], waits=agw, sem="dzc"))
        zrb_all = ("dzc", P.cnt["dzc"], "sync")

        if tap == "ar_l1":
            P.dma("sync", dbg_ar[:, :34816], AR[:, :34816],
                  waits=[mt_all, ("dzc", P.cnt["dzc"], "sync")], sem="dout")
            P.op("sync", lambda e: e.nop(), waits=[("dout", P.cnt["dout"], "sync")])

        for i in range(NOWN):
            htoks = []
            for s4 in range(4):
                bk = s4 % 4
                mtok = None
                nj = 2 * i + 2
                for j in range(nj):
                    midx = (2 * i + 1) - j
                    w = [bank_r[bk], mt_all, zrb_all] if j == 0 else []
                    mtok = T_(lambda e, bk=bk, j=j, s4=s4, midx=midx, nj=nj: e.matmul(
                        pb[bk][:],
                        AR[:, A2_MT + midx * 128:A2_MT + (midx + 1) * 128],
                        AR[:, A2_ZF + j * 2048 + s4 * 512:A2_ZF + j * 2048 + (s4 + 1) * 512],
                        start=(j == 0), stop=(j == nj - 1)),
                        w, inc="t" if j == nj - 1 else None)
                ev = V_(lambda e, bk=bk, s4=s4: e.tensor_copy(
                    SC[:, SC_H + s4 * 512:SC_H + (s4 + 1) * 512], pb[bk][:]),
                    [mtok, ab_free], big=True)
                bank_r[bk] = ev
                htoks.append(ev)
            hn = SC[:, SC_DUMP:SC_DUMP + D]
            h_ap = SC[:, SC_H:SC_H + D]
            mu, rs, sq, t0 = st(ST_MU), st(ST_RS), st(ST_SQ), st(ST_T0)
            V_(lambda e: e.reduce_sum(mu, h_ap, axis=AX.X), htoks)
            tsq = S_(lambda e: e.activation(SC[:, SC_DUMP:SC_DUMP + D], h_ap,
                                            AF.Square, accum_out=sq), htoks, big=True)
            V_(lambda e: e.tensor_scalar(mu, mu, 1.0 / D, None, ALU.mult))
            tv = V_(lambda e: e.tensor_scalar(sq, sq, 1.0 / D, None, ALU.mult), [tsq])
            V_(lambda e: e.tensor_tensor(t0, mu, mu, ALU.mult))
            V_(lambda e: e.tensor_tensor(t0, sq, t0, ALU.subtract), [tv])
            tve2 = V_(lambda e: e.tensor_scalar(t0, t0, 1.0, EPS, ALU.mult, ALU.add))
            tsr = S_(lambda e: e.sqrt(t0, t0), [tve2])
            trc = V_(lambda e: e.reciprocal(rs, t0), [tsr])
            tnm = V_(lambda e: e.tensor_scalar(h_ap, h_ap, mu, rs,
                                               ALU.subtract, ALU.mult), [trc], big=True)
            tg = V_(lambda e: e.tensor_tensor(h_ap, h_ap, G1[:], ALU.mult),
                    [tnm, tg1], big=True)
            tb2 = V_(lambda e: e.tensor_tensor(h_ap, h_ap, G2[:], ALU.add), [tg, tg2], big=True)
            for s4 in range(4):
                tfin = V_(lambda e, i=i, s4=s4: e.tensor_tensor(
                    XR[:, i * D + s4 * 512:i * D + s4 * 512 + 512],
                    XR[:, i * D + s4 * 512:i * D + s4 * 512 + 512],
                    SC[:, SC_H + s4 * 512:SC_H + (s4 + 1) * 512], ALU.add),
                    [tb2, xr_tok[i][s4]], big=True)
                xr_tok[i][s4] = tfin

    A3_XNT = 0
    A3_SW = 16384

    if upto == "p4":
        tg1b = P.dma("sync", G1[:], lnw_t[:],
                     waits=[xr_tok[i][s] for i in range(NOWN) for s in range(4)],
                     sem="dy")
        tg2b = P.dma("sync", G2[:], lnb_t[:], sem="dy")

        xnt_tok = []
        for i in range(NOWN):
            xn = SC[:, 0:D]
            tln = emit_ln(XR[:, i * D:(i + 1) * D], xn, xr_tok[i] + [tg1b, tg2b])
            tm1 = V_(lambda e: e.tensor_tensor(xn, xn, G1[:], ALU.mult), [tln, tg1b], big=True)
            xnb = AR[:, A2_ZS:A2_ZS + D]
            tm2 = V_(lambda e: e.tensor_tensor(xnb, xn, G2[:], ALU.add),
                     [tm1, tg2b] + ([xnt_tok[-1]] if xnt_tok else []), big=True)
            done = 0
            while done < 16:
                bank = "t6" if tr_flip[0] % 2 == 0 else "t7"
                tr_flip[0] += 1
                pbank = pbt[bank]
                ttok = None
                for j in range(4):
                    w = [tm2, bank_r[bank]] if j == 0 else []
                    ttok = T_(lambda e, j=j, dj=done, pbk=pbank, xnb=xnb: e.transpose(
                        pbk[:, j * 128:(j + 1) * 128],
                        xnb[:, (dj + j) * 128:(dj + j + 1) * 128], C_ID[:]),
                        w, inc="t" if j == 3 else None)
                evl = None
                for j in range(4):
                    evl = V_(lambda e, j=j, dj=done, i=i, pbk=pbank: e.tensor_copy(
                        AR[:, A3_XNT + (dj + j) * 1024 + i * 128:
                           A3_XNT + (dj + j) * 1024 + i * 128 + 128],
                        pbk[:, j * 128:(j + 1) * 128]), [ttok], big=True)
                bank_r[bank] = evl
                xnt_tok.append(evl)
                done += 4
        xnt_all = xnt_tok[-1]

        tg1c = P.dma("sync", G1[:], fgam[:], waits=[xnt_all], sem="dy")
        tg2c = P.dma("sync", G2[:], fbet[:], sem="dy")

        sw_free = [None, None]
        out_cnt = 0
        for s4 in range(4):
            kb = s4 % 2
            swb = AR[:, A3_SW + kb * 8192:A3_SW + (kb + 1) * 8192]
            tdw = P.dma("gpsimd",
                        swb.rearrange("p (dc o) -> p dc o", o=512),
                        ffnw[s4].rearrange("dc p o -> p dc o"),
                        waits=[sw_free[kb], xnt_all],
                        sem="dm1a" if kb == 0 else "dm1b")
            tsg = S_(lambda e, swb=swb: e.sign(swb, swb), [tdw], big=True)
            lastmm = None
            for i in range(NOWN):
                bk = i % 4
                mtok = None
                for dc in range(16):
                    w = [bank_r[bk], tsg, xnt_all] if dc == 0 else []
                    mtok = T_(lambda e, bk=bk, dc=dc, i=i, swb=swb: e.matmul(
                        pb[bk][:],
                        AR[:, A3_XNT + dc * 1024 + i * 128:A3_XNT + dc * 1024 + i * 128 + 128],
                        swb[:, dc * 512:(dc + 1) * 512],
                        start=(dc == 0), stop=(dc == 15)),
                        w, inc="t" if dc == 15 else None)
                lastmm = mtok
                ko = out_cnt % 2
                out_cnt += 1
                t1 = V_(lambda e, bk=bk, ko=ko, s4=s4: e.scalar_tensor_tensor(
                    SC[:, ko * 512:ko * 512 + 512], pb[bk][:], st(ST_SCL),
                    G1[:, s4 * 512:s4 * 512 + 512], ALU.mult, ALU.mult),
                    [mtok, tscl, tg1c], big=True)
                bank_r[bk] = t1
                t2 = V_(lambda e, ko=ko, s4=s4: e.tensor_tensor(
                    SC[:, ko * 512:ko * 512 + 512], SC[:, ko * 512:ko * 512 + 512],
                    G2[:, s4 * 512:s4 * 512 + 512], ALU.add), [t1, tg2c], big=True)
                t3 = V_(lambda e, ko=ko, i=i, s4=s4: e.tensor_tensor(
                    SC[:, ko * 512:ko * 512 + 512], SC[:, ko * 512:ko * 512 + 512],
                    XR[:, i * D + s4 * 512:i * D + s4 * 512 + 512], ALU.add),
                    [t2, xr_tok[i][s4], ("dout", P.cnt.get("dout", 0), "sync")], big=True)
                P.dma("sync", y_out[i][:, s4 * 512:s4 * 512 + 512],
                      SC[:, ko * 512:ko * 512 + 512], waits=[t3], sem="dout")
            sw_free[kb] = lastmm
        P.op("sync", lambda e: e.nop(), waits=[("dout", P.cnt["dout"], "sync")])

    if tap is True:
        tw = []
        for i in range(NOWN):
            tw.append(P.dma("sync", dbg[i], XR[:, i * D:(i + 1) * D],
                            waits=xr_tok[i] if xr_tok[i][0] else spill_all,
                            sem="dout"))
        P.op("sync", lambda e: e.nop(), waits=[("dout", P.cnt["dout"], "sync")])
    if upto != "p4" and not tap:
        P.op("sync", lambda e: e.nop(), waits=spill_all)

    semh = {s: nc.alloc_semaphore(s) for s in ALL_SEMS}
    blk_cm = nc.Block()
    blk = blk_cm.__enter__()
    P.emit(blk, semh)
    blk_cm.__exit__(None, None, None)
    return nc


def prep_inputs(L0_attn_x, L0_attn_Wq, L0_attn_Wk, L0_attn_Wv, L0_attn_Wo,
                L0_attn_gq, L0_attn_gk, L0_ffn_gate_w, L0_ffn_sw1, L0_ffn_sw2,
                L0_ffn_sw3, L0_ffn_rw1, L0_ffn_rw2, L0_ffn_rw3, L1_attn_W,
                L1_attn_lam_logit, L1_attn_eta, L1_attn_gamma, L1_attn_beta,
                L1_ffn_weight, L1_ffn_gamma, L1_ffn_beta, L1_ffn_ln_weight,
                L1_ffn_ln_bias):
    f = np.float32
    asn = np.ascontiguousarray

    def t2(w):
        return asn(np.asarray(w, f).T)

    wqT = t2(L0_attn_Wq)
    wkT = t2(L0_attn_Wk)
    wvT = t2(L0_attn_Wv)
    woT = t2(L0_attn_Wo)
    def gsplit(w):
        return asn(w.reshape(16, 128, 4, 512).transpose(2, 0, 1, 3))
    wq_h = gsplit(wqT)
    wk_h = gsplit(wkT)
    wv_h = gsplit(wvT)
    wo_h = asn(woT.reshape(4, 4, 128, 4, 512).transpose(0, 3, 1, 2, 4))
    gq = np.asarray(L0_attn_gq, f)
    gk = np.asarray(L0_attn_gk, f)
    gqt = asn(np.broadcast_to(np.tile(gq, 16)[None, :], (128, D)))
    gkt = asn(np.broadcast_to(np.tile(gk, 16)[None, :], (128, D)))
    gate_h = asn(np.asarray(L0_ffn_gate_w, f).reshape(16, 128, NE))
    sw1_h = asn(np.asarray(L0_ffn_sw1, f).reshape(16, 128, 22, 128)
                .transpose(2, 0, 1, 3))
    sw3_h = asn(np.asarray(L0_ffn_sw3, f).reshape(16, 128, 22, 128)
                .transpose(2, 0, 1, 3))
    sw2_h = asn(np.asarray(L0_ffn_sw2, f).reshape(22, 128, 4, 512)
                .transpose(2, 0, 1, 3))
    rw1_h = asn(np.asarray(L0_ffn_rw1, f).reshape(NE, 16, 128, 11, 128)
                .transpose(0, 3, 1, 2, 4))
    rw3_h = asn(np.asarray(L0_ffn_rw3, f).reshape(NE, 16, 128, 11, 128)
                .transpose(0, 3, 1, 2, 4))
    rw2_h = asn(np.asarray(L0_ffn_rw2, f).reshape(NE, 11, 128, 4, 512)
                .transpose(0, 3, 1, 2, 4))
    l1wT = t2(L1_attn_W)
    l1w_h = asn(l1wT.reshape(16, 128, 4, 512).transpose(2, 0, 1, 3))
    ffnT = t2(L1_ffn_weight)
    ffn_h = asn(ffnT.reshape(16, 128, 4, 512).transpose(2, 0, 1, 3))
    ffn_abs = asn(ffnT.reshape(16, 128, D))
    lam_h = np.asarray(L1_attn_lam_logit, f).reshape(1, 4)
    eta_h = np.asarray(L1_attn_eta, f).reshape(1, 4)

    def tile_row(v):
        return asn(np.broadcast_to(np.asarray(v, f)[None, :], (128, D)))

    l1gam = tile_row(L1_attn_gamma)
    l1bet = tile_row(L1_attn_beta)
    lnw_t = tile_row(L1_ffn_ln_weight)
    lnb_t = tile_row(L1_ffn_ln_bias)
    fgam = tile_row(L1_ffn_gamma)
    fbet = tile_row(L1_ffn_beta)

    ident = np.eye(128, dtype=f)
    onesr = np.ones((1, 128), f)
    r = np.arange(128)
    deltam = asn((r[None, :] - r[:, None]).astype(f))
    iotam_base = 128.0 * (np.arange(16, dtype=f) - 1.0)

    x = np.asarray(L0_attn_x, f)
    in_maps = []
    for c in range(8):
        b, p = c // 2, c % 2
        own = [2 * i + p for i in range(NOWN)]
        x_own = asn(x[b].reshape(NTC, 128, D)[own])
        x_full = asn(x[b].reshape(NTC, 128, D))
        cidx = np.arange(256)[None, :]
        ridx = r[:, None]
        bmask = np.where(cidx <= 128 * p + ridx, 0.0, NEGV).astype(f)
        n2 = np.zeros((128, 256), f)
        e2 = np.zeros((128, 256), f)
        tri_bad = (deltam < 0)
        for m in range(2):
            mm = m * 128
            if m == 1 - p:
                n2[:, mm:mm + 128] = np.where(tri_bad, NEGV, 0.0)
                e2[:, mm:mm + 128] = ident
            elif m < 1 - p:
                n2[:, mm:mm + 128] = NEGV
        iotam = (iotam_base + 128.0 * p).reshape(1, 16).astype(f)
        in_maps.append({
            "x_own": x_own, "x_full": x_full,
            "wq": wq_h, "wk": wk_h, "wv": wv_h, "wo": wo_h,
            "gqt": gqt, "gkt": gkt, "gate_w": gate_h,
            "sw1": sw1_h, "sw3": sw3_h, "sw2": sw2_h,
            "rw1": rw1_h, "rw3": rw3_h, "rw2": rw2_h,
            "l1w": l1w_h, "lam_logit": lam_h, "eta": eta_h,
            "l1gam": l1gam, "l1bet": l1bet,
            "ffnw": ffn_h, "ffnw_abs": ffn_abs,
            "lnw_t": lnw_t, "lnb_t": lnb_t, "fgam": fgam, "fbet": fbet,
            "ident": ident, "onesr": onesr, "bmask": bmask,
            "delta": deltam, "neg2": n2, "eye2": e2, "iotam": iotam,
        })
    return in_maps


def assemble(results, key="y"):
    x_out = np.empty((B_GLOBAL, T_GLOBAL, D), np.float32)
    for c in range(8):
        b, p = c // 2, c % 2
        yc = results[c][key]
        for i in range(NOWN):
            g = 2 * i + p
            x_out[b, g * 128:(g + 1) * 128] = yc[i]
    return x_out


B_GLOBAL = 4
T_GLOBAL = 2048


def kernel(**inputs):
    in_maps = prep_inputs(**inputs)
    nc = build_nc()
    res = run_bass_kernel_spmd(nc, in_maps, core_ids=list(range(8)))
    return assemble(res.results)


if __name__ == "__main__":
    import sys, time
    t0 = time.time()
    nc = build_nc(upto=sys.argv[1] if len(sys.argv) > 1 else "p4")
    print("built ok in %.1fs" % (time.time() - t0))


# version 06
# speedup vs baseline: 1.1418x; 1.0698x over previous
import numpy as np
import concourse.bass as bass
import concourse.mybir as mybir
from concourse.bass_utils import run_bass_kernel_spmd

F32 = mybir.dt.float32
BF16 = mybir.dt.bfloat16
AF = mybir.ActivationFunctionType
ALU = mybir.AluOpType
AX = mybir.AxisListType

D = 2048
H = 16
NTC = 16
NOWN = 8
NE = 8
EPS = 1e-5
NEGV = -1.0e30
ISQ = 0.08838834764831845

DMA_SEMS = ("dy", "dg", "dx", "dx2", "dsp", "dsp2", "dab", "dwa", "dwb",
            "dm1a", "dm1b", "dm3a", "dm3b", "dm2a", "dm2b", "dzc", "dout")
ALL_SEMS = DMA_SEMS + ("t", "v", "s", "b", "ag")


class Prog:
    ENGS = ("tensor", "vector", "scalar", "sync", "gpsimd")

    def __init__(self):
        self.ops = {e: [] for e in self.ENGS}
        self.cnt = {}
        self.ecnt = {}
        self.gcnt = {}

    def op(self, eng, fn, waits=(), inc=None, inc_amt=1, big=False):
        w = [t for t in waits if t is not None]
        self.ops[eng].append((fn, w, inc, inc_amt, big))
        if inc:
            self.cnt[inc] = self.cnt.get(inc, 0) + inc_amt
            return (inc, self.cnt[inc], eng)
        return None

    def dma(self, eng, out, in_, waits=(), sem="dy"):
        return self.op(eng, lambda e, o=out, i=in_: e.dma_start(out=o, in_=i),
                       waits, sem, 16, big=True)

    def emit(self, blk, semh):
        for eng in self.ENGS:
            oplist = self.ops[eng]
            if not oplist:
                continue

            hop = eng in ("vector", "scalar")

            def body(e, oplist=oplist, eng=eng, hop=hop):
                seen = {}
                prev = None
                for fn, waits, inc, amt, big in oplist:
                    if hop and prev is not None and not (prev[2] and big):
                        s, n = prev[0], prev[1]
                        if seen.get(s, 0) < n:
                            e.wait_ge(semh[s], n)
                            seen[s] = n
                    for s, n, peng in waits:
                        if n <= 0 or seen.get(s, 0) >= n:
                            continue
                        if peng == eng and s not in DMA_SEMS:
                            continue
                        e.wait_ge(semh[s], n)
                        seen[s] = n
                    inst = fn(e)
                    if inc is not None:
                        inst.then_inc(semh[inc], amt)
                        prev = (inc, self.cnt_emit[eng].setdefault(inc, 0) + 1, big) if False else None
                    prev = None
                    if inc is not None:
                        c = self.ecnt.setdefault((eng, inc), 0) + 1
                        self.ecnt[(eng, inc)] = c
                        g = self.gcnt.setdefault(inc, 0) + 1
                        self.gcnt[inc] = g
                        prev = (inc, g, big)
            getattr(blk, eng)(body)


def build_nc(upto="p4", tap=False):
    nc = bass.Bass("TRN2", target_bir_lowering=False, detect_race_conditions=False)
    P = Prog()

    def din(name, shape, dt=F32):
        return nc.dram_tensor(name, list(shape), dt, kind="ExternalInput").ap()

    x_own = din("x_own", [NOWN, 128, D])
    x_full = din("x_full", [NTC, 128, D])
    wq = din("wq", [4, 16, 128, 512])
    wk = din("wk", [4, 16, 128, 512])
    wv = din("wv", [4, 16, 128, 512])
    wo = din("wo", [4, 4, 4, 128, 512])
    gqt = din("gqt", [128, D])
    gkt = din("gkt", [128, D])
    gate_wd = din("gate_w", [16, 128, NE])
    sw1 = din("sw1", [22, 16, 128, 128])
    sw3 = din("sw3", [22, 16, 128, 128])
    sw2 = din("sw2", [4, 22, 128, 512])
    rw1 = din("rw1", [NE, 11, 16, 128, 128])
    rw3 = din("rw3", [NE, 11, 16, 128, 128])
    rw2 = din("rw2", [NE, 4, 11, 128, 512])
    l1w = din("l1w", [4, 16, 128, 512])
    lam_logit = din("lam_logit", [1, 4])
    eta_row = din("eta", [1, 4])
    l1gam = din("l1gam", [128, D])
    l1bet = din("l1bet", [128, D])
    ffnw = din("ffnw", [4, 16, 128, 512])
    ffnw_abs = din("ffnw_abs", [16, 128, D])
    lnw_t = din("lnw_t", [128, D])
    lnb_t = din("lnb_t", [128, D])
    fgam = din("fgam", [128, D])
    fbet = din("fbet", [128, D])
    ident = din("ident", [128, 128])
    onesr = din("onesr", [1, 128])
    bmask = din("bmask", [128, 256])
    delta = din("delta", [128, 128])
    neg2 = din("neg2", [128, 256])
    eye2 = din("eye2", [128, 256])
    iotam = din("iotam", [1, 16])

    y_out = nc.dram_tensor("y", [NOWN, 128, D], F32, kind="ExternalOutput").ap()
    dbg = None
    if tap is True:
        dbg = nc.dram_tensor("dbg", [NOWN, 128, D], F32, kind="ExternalOutput").ap()
    dbg_ar = None
    if isinstance(tap, str) and (tap.startswith("ar") or tap.startswith("ln")):
        dbg_ar = nc.dram_tensor("dbgar", [128, 36864], BF16, kind="ExternalOutput").ap()

    xt_spill = nc.dram_tensor("xt_spill", [NTC, 128, D], BF16, kind="Internal").ap()
    xt_own = nc.dram_tensor("xt_own", [NOWN, 128, D], BF16, kind="Internal").ap()
    ag_in = nc.dram_tensor("ag_in", [NOWN, 128, D], BF16, kind="Internal").ap()
    ag_out = nc.dram_tensor("ag_out", [2, NOWN, 128, D], BF16, kind="Internal").ap()

    XR = nc.alloc_sbuf_tensor("XR", [128, NOWN * D], F32)
    AR = nc.alloc_sbuf_tensor("AR", [128, 36864], BF16)
    SC = nc.alloc_sbuf_tensor("SC", [128, 4096], F32)
    G1 = nc.alloc_sbuf_tensor("G1", [128, D], F32)
    G2 = nc.alloc_sbuf_tensor("G2", [128, D], F32)
    WS = nc.alloc_sbuf_tensor("WS", [128, 8192], BF16)
    C_ID = nc.alloc_sbuf_tensor("C_ID", [128, 128], BF16)
    C_IDF = nc.alloc_sbuf_tensor("C_IDF", [128, 128], F32)
    C_ON = nc.alloc_sbuf_tensor("C_ON", [1, 128], F32)
    C_BM = nc.alloc_sbuf_tensor("C_BM", [128, 256], F32)
    C_DL = nc.alloc_sbuf_tensor("C_DL", [128, 128], F32)
    C_N2 = nc.alloc_sbuf_tensor("C_N2", [128, 256], F32)
    C_E2 = nc.alloc_sbuf_tensor("C_E2", [128, 256], F32)
    C_IM = nc.alloc_sbuf_tensor("C_IM", [1, 16], F32)
    ST = nc.alloc_sbuf_tensor("ST", [128, 64], F32)
    R1 = nc.alloc_sbuf_tensor("R1", [1, 256], F32)

    pb = [nc.alloc_psum_tensor(f"pb{i}", [128, 512], F32)[:] for i in range(6)]
    pt6 = nc.alloc_psum_tensor("pt6", [128, 1024], BF16)
    pt7 = nc.alloc_psum_tensor("pt7", [128, 1024], BF16)
    pbt = {"t6": pt6, "t7": pt7}

    bank_r = {i: None for i in range(6)}
    bank_r["t6"] = None
    bank_r["t7"] = None

    def T_(fn, waits=(), inc="t", big=True):
        return P.op("tensor", fn, waits, inc, big=big)

    def V_(fn, waits=(), inc="v", big=False):
        return P.op("vector", fn, waits, inc, big=big)

    def S_(fn, waits=(), inc="s", big=False):
        return P.op("scalar", fn, waits, inc, big=big)

    ST_MU, ST_RS, ST_SQ, ST_T0 = 0, 1, 2, 3
    ST_SS, ST_IR = 8, 12
    ST_NM, ST_NT = 16, 20
    ST_LS, ST_LT, ST_LI = 24, 28, 29
    ST_CM, ST_SCL = 32, 33
    ST_M1, ST_M2 = 34, 35
    ST_LAMC = 40
    ST_ETAC = 44
    SC_DUMP = 2048

    def st(c, n=1):
        return ST[:, c:c + n]
    DBG_LN = [bool(tap == "ln1t")]

    def emit_ln(x_ap, out_ap, ready):
        mu, rs, sq, t0 = st(ST_MU), st(ST_RS), st(ST_SQ), st(ST_T0)
        dump = SC[:, SC_DUMP:SC_DUMP + D]
        tmu = S_(lambda e: e.activation(dump, x_ap, AF.Copy, accum_out=mu), ready, big=True)
        tsq = S_(lambda e: e.activation(dump, x_ap, AF.Square, accum_out=sq), ready, big=True)
        V_(lambda e: e.tensor_scalar(mu, mu, 1.0 / D, None, ALU.mult), [tmu])
        tv = V_(lambda e: e.tensor_scalar(sq, sq, 1.0 / D, None, ALU.mult), [tsq])
        if DBG_LN[0]:
            V_(lambda e: e.tensor_copy(st(16), mu))
            V_(lambda e: e.tensor_copy(st(17), sq), [tv])
        V_(lambda e: e.tensor_tensor(t0, mu, mu, ALU.mult))
        if DBG_LN[0]:
            V_(lambda e: e.tensor_copy(st(18), t0))
        V_(lambda e: e.tensor_tensor(t0, sq, t0, ALU.subtract), [tv])
        if DBG_LN[0]:
            V_(lambda e: e.tensor_copy(st(19), t0))
        tve = V_(lambda e: e.tensor_scalar(t0, t0, 1.0, EPS, ALU.mult, ALU.add))
        if DBG_LN[0]:
            V_(lambda e: e.tensor_copy(st(20), t0))
        ts2 = S_(lambda e: e.sqrt(t0, t0), [tve])
        if DBG_LN[0]:
            V_(lambda e: e.tensor_copy(st(21), t0), [ts2])
        V_(lambda e: e.reciprocal(rs, t0), [ts2])
        return V_(lambda e: e.tensor_scalar(out_ap, x_ap, mu, rs, ALU.subtract, ALU.mult),
                  list(ready), big=True)

    tr_flip = [0]

    def emit_tr(src_base_ap, dst_ap, n, ready):
        toks = []
        done = 0
        while done < n:
            take = min(4, n - done)
            bank = "t6" if tr_flip[0] % 2 == 0 else "t7"
            tr_flip[0] += 1
            pbank = pbt[bank]
            ttok = None
            for j in range(take):
                w = list(ready) + [bank_r[bank]] if j == 0 else []
                ttok = T_(lambda e, j=j, dj=done, pbk=pbank: e.transpose(
                    pbk[:, j * 128:(j + 1) * 128],
                    src_base_ap[:, (dj + j) * 128:(dj + j + 1) * 128], C_ID[:]),
                    w, inc="t" if j == take - 1 else None)
            ev = V_(lambda e, dj=done, tk=take, pbk=pbank: e.tensor_copy(
                dst_ap[:, dj * 128:(dj + tk) * 128], pbk[:, :tk * 128]), [ttok], big=True)
            bank_r[bank] = ev
            toks.append(ev)
            done += take
        return toks[-1]

    tok_const = [
        P.dma("gpsimd", C_ID[:], ident[:], sem="dg"),
        P.dma("sync", C_IDF[:], ident[:], sem="dy"),
        P.dma("sync", C_ON[:], onesr[:], sem="dy"),
        P.dma("sync", C_BM[:], bmask[:], sem="dy"),
        P.dma("sync", C_DL[:], delta[:], sem="dy"),
        P.dma("sync", C_N2[:], neg2[:], sem="dy"),
        P.dma("sync", C_E2[:], eye2[:], sem="dy"),
        P.dma("sync", C_IM[:], iotam[:], sem="dy"),
    ]
    tok_g1 = P.dma("sync", G1[:], gqt[:], sem="dy")
    tok_g2 = P.dma("sync", G2[:], gkt[:], sem="dy")

    def dy_barrier():
        return P.op("sync", lambda e: e.nop(),
                    waits=[("dy", P.cnt["dy"], "sync")], inc="b")

    b_const = dy_barrier()
    tok_const = [b_const]
    tok_g1 = b_const
    tok_g2 = b_const

    A_LN = 0
    A_ROW = 2048
    A_XF = 6144

    def xf32(k):
        return AR[:, A_XF + k * 4096: A_XF + (k + 1) * 4096].bitcast(F32)

    spill_last = None
    xtile_free = [None, None]
    row_free = [None, None]
    ln_free = [None]

    def p0_tile(src_dram, dst_dram, idx, k):
        nonlocal spill_last
        tdma = P.dma("sync", xf32(k), src_dram[idx], waits=[xtile_free[k]],
                     sem="dx" if k == 0 else "dx2")
        ln_ap = AR[:, A_LN:A_LN + D]
        t_ln = emit_ln(xf32(k), ln_ap, [tdma, ln_free[0]] + tok_const)
        xtile_free[k] = t_ln
        rb = AR[:, A_ROW + k * 2048:A_ROW + k * 2048 + D]
        ev = emit_tr(ln_ap, rb, 16, [t_ln, row_free[k]])
        ln_free[0] = ev
        spill_last = P.dma("sync", dst_dram[idx], rb, waits=[ev],
                           sem="dsp" if k == 0 else "dsp2")
        row_free[k] = spill_last

    n_p0 = NTC
    if isinstance(tap, str) and tap.startswith("ln1t"):
        n_p0 = 1
    for tc in range(n_p0):
        p0_tile(x_full, xt_spill, tc, tc % 2)
    if n_p0 == NTC:
        for i in range(NOWN):
            p0_tile(x_own, xt_own, i, i % 2)
    if tap == "ln1t":
        P.dma("sync", dbg_ar[:, :2048], AR[:, :2048],
              waits=[("dsp", P.cnt["dsp"], "sync")], sem="dout")
        P.dma("gpsimd", dbg_ar[:, 2048:2080], ST[:, 0:32],
              waits=[("dsp", P.cnt["dsp"], "sync")], sem="dout")
        P.op("sync", lambda e: e.nop(), waits=[("dout", P.cnt["dout"], "sync")])
    spill_all = [("dsp", P.cnt.get("dsp", 0), "sync"), ("dsp2", P.cnt.get("dsp2", 0), "sync")]

    if tap == "ar_raw":
        P.dma("sync", dbg_ar[:, :6144], AR[:, :6144], waits=spill_all, sem="dout")
        P.op("sync", lambda e: e.nop(), waits=[("dout", P.cnt["dout"], "sync")])
    if tap == "ar_xt":
        rb1 = []
        for tc in range(8):
            rb1.append(P.dma("sync", AR[:, tc * 2048:(tc + 1) * 2048],
                             xt_spill[tc], waits=spill_all, sem="dy"))
        for i in range(8):
            rb1.append(P.dma("sync", AR[:, 16384 + i * 2048:16384 + (i + 1) * 2048],
                             xt_own[i], waits=spill_all, sem="dy"))
        P.op("sync", lambda e: e.nop(), waits=[("dy", P.cnt["dy"], "sync")])
        P.dma("sync", dbg_ar, AR[:], waits=rb1, sem="dout")
        P.op("sync", lambda e: e.nop(), waits=[("dout", P.cnt["dout"], "sync")])

    A_XTW = 0
    A_KT = 8192
    A_V = 16384
    A_QT = 24576
    A_P = 28672
    A_PT = 30720
    A_AOT = 32768

    S_VAL, S_TMP, S_SQ = 0, 512, 1024

    xr_tok = [[None] * 4 for _ in range(NOWN)]
    if upto in ("p1", "p2", "p3", "p4"):
        for i in range(NOWN):
            P.dma("sync", XR[:, i * D:(i + 1) * D], x_own[i], sem="dy")
        b_xr = dy_barrier()
        for i in range(NOWN):
            for s4 in range(4):
                xr_tok[i][s4] = b_xr

        ws_free = None
        wave_free = [None, None]
        qk_guard = [None]
        stage_guard = [None, None]

        def wave_load(src_dram, w, k, sem):
            return P.dma("sync",
                         AR[:, A_XTW + k * 4096:A_XTW + (k + 1) * 4096]
                         .rearrange("p (dc tc f) -> p dc tc f", tc=2, f=128),
                         src_dram[2 * w:2 * w + 2]
                         .rearrange("tc p (dc f) -> p dc tc f", f=128),
                         waits=[wave_free[k]] + spill_all, sem=sem)

        def proj(kind, g, wsrc):
            nonlocal ws_free
            wtok = P.dma("gpsimd", WS[:].rearrange("p (dc o) -> p dc o", o=512),
                         wsrc[g].rearrange("dc p o -> p dc o"),
                         waits=[ws_free], sem="dwa")
            nw = 4 if kind == "q" else 8
            src = xt_own if kind == "q" else xt_spill
            lastev = None
            wtoks = {}
            for w in range(min(nw, 2)):
                wtoks[w] = wave_load(src, w, w % 2, "dx" if w % 2 == 0 else "dx2")
            emitted = set(range(min(nw, 2)))
            pend_tr = [None]
            for w in range(nw):
                k = w % 2
                for tt in range(2):
                    tcg = 2 * w + tt
                    bk = (2 * w + tt) % 4
                    mtok = None
                    for dc in range(16):
                        wts = [bank_r[bk], wtoks[w], wtok] if dc == 0 else []
                        mtok = T_(lambda e, bk=bk, k=k, dc=dc, tt=tt:
                                  e.matmul(pb[bk],
                                           AR[:, A_XTW + k * 4096 + dc * 256 + tt * 128:
                                              A_XTW + k * 4096 + dc * 256 + tt * 128 + 128],
                                           WS[:, dc * 512:(dc + 1) * 512],
                                           start=(dc == 0), stop=(dc == 15)),
                                  wts, inc="t" if dc == 15 else None)
                    lastev = proj_out(kind, g, tcg, bk, mtok, pend_tr)
                    last_mm = mtok
                wave_free[k] = last_mm
                nxt = w + 2
                if nxt < nw and nxt not in emitted:
                    wtoks[nxt] = wave_load(src, nxt, nxt % 2,
                                           "dx" if nxt % 2 == 0 else "dx2")
                    emitted.add(nxt)
            if pend_tr[0] is not None:
                lastev = pend_tr[0]()
                pend_tr[0] = None
            ws_free = lastev
            return lastev

        def proj_out(kind, g, tcg, bk, mtok, pend_tr):
            if kind == "v":
                if pend_tr[0] is not None:
                    pend_tr[0]()
                    pend_tr[0] = None
                ev = V_(lambda e: e.tensor_copy(
                    AR[:, A_V + tcg * 512:A_V + tcg * 512 + 512], pb[bk]), [mtok], big=True)
                bank_r[bk] = ev
                return ev
            val = SC[:, S_VAL:S_VAL + 512]
            tmp = SC[:, S_TMP:S_TMP + 512]
            sqv = SC[:, S_SQ:S_SQ + 512]
            ss, ir = st(ST_SS, 4), st(ST_IR, 4)
            t1 = V_(lambda e, bk=bk: e.tensor_copy(val, pb[bk]),
                    [mtok, qk_guard[0]], big=True)
            bank_r[bk] = t1
            t2 = S_(lambda e: e.square(sqv, val), [t1], big=True)
            t3 = V_(lambda e: e.tensor_reduce(
                ss, sqv.rearrange("p (a b) -> p a b", b=128),
                axis=AX.X, op=ALU.add), [t2])
            V_(lambda e: e.tensor_scalar(ss, ss, 1.0 / 128.0, EPS, ALU.mult, ALU.add), [t3])
            t4 = V_(lambda e: e.reciprocal(ss, ss))
            t5 = S_(lambda e: e.sqrt(ir, ss), [t4])
            for hh in range(4):
                V_(lambda e, hh=hh: e.tensor_scalar(
                    tmp[:, hh * 128:(hh + 1) * 128], val[:, hh * 128:(hh + 1) * 128],
                    ir[:, hh:hh + 1], None, ALU.mult), [t5], big=(hh > 0))
            gm = G1 if kind == "q" else G2
            stage = AR[:, A_P + (tcg % 2) * 512:A_P + (tcg % 2) * 512 + 512]
            t7 = V_(lambda e, stage=stage: e.tensor_tensor(
                stage, tmp, gm[:, g * 512:(g + 1) * 512], ALU.mult),
                [tok_g1, stage_guard[tcg % 2]], big=True)
            qk_guard[0] = t2
            dst_base = A_QT if kind == "q" else A_KT
            per = 1024 if kind == "q" else 2048

            def do_tr(stage=stage, tcg=tcg, t7=t7, dst_base=dst_base, per=per):
                bank = "t6" if tr_flip[0] % 2 == 0 else "t7"
                tr_flip[0] += 1
                pbank = pbt[bank]
                ttok = None
                for j in range(4):
                    w = [t7, bank_r[bank]] if j == 0 else []
                    ttok = T_(lambda e, j=j, pbk=pbank, stage=stage: e.transpose(
                        pbk[:, j * 128:(j + 1) * 128],
                        stage[:, j * 128:(j + 1) * 128], C_ID[:]),
                        w, inc="t" if j == 3 else None)
                stage_guard[tcg % 2] = ttok
                evl = None
                for j in range(4):
                    evl = V_(lambda e, j=j, pbk=pbank: e.tensor_copy(
                        AR[:, dst_base + j * per + tcg * 128:
                           dst_base + j * per + tcg * 128 + 128],
                        pbk[:, j * 128:(j + 1) * 128]), [ttok], big=True)
                bank_r[bank] = evl
                return evl

            prev = pend_tr[0]
            pend_tr[0] = do_tr
            if prev is not None:
                return prev()
            return t7

        for g in range(4):
            tq = proj("q", g, wq)
            tk = proj("k", g, wk)
            tv = proj("v", g, wv)
            proj_done = [tq, tk, tv]
            if tap == "ar_qkv" and g == 0:
                P.dma("sync", dbg_ar[:, :28672], AR[:, :28672],
                      waits=proj_done, sem="dout")
                P.op("sync", lambda e: e.nop(),
                     waits=[("dout", P.cnt["dout"], "sync")])
                break

            wo_tok = P.dma("gpsimd",
                           AR[:, :8192].rearrange("p (s hc o) -> p s hc o",
                                                  hc=4, o=512),
                           wo[g].rearrange("s hc p o -> p s hc o"),
                           waits=proj_done, sem="dwb")

            pv_flip = 0
            aot_tok = []
            p_guard = [None]
            pend = [None]

            def emit_ptpv(i, h, nch, psc_tok):
                pt_tok = emit_tr(AR[:, A_P:A_P + nch * 128],
                                 AR[:, A_PT:A_PT + nch * 128], nch, [psc_tok])
                p_guard[0] = pt_tok
                pvb = 4 + (pv_flip_l[0] % 2)
                pv_flip_l[0] += 1
                mtok = None
                for kc in range(nch):
                    w = [bank_r[pvb], pt_tok] if kc == 0 else []
                    mtok = T_(lambda e, kc=kc, h=h, pvb=pvb, nch=nch: e.matmul(
                        pb[pvb][:, :128],
                        AR[:, A_V + kc * 512 + h * 128:A_V + kc * 512 + h * 128 + 128],
                        AR[:, A_PT + kc * 128:A_PT + kc * 128 + 128],
                        start=(kc == 0), stop=(kc == nch - 1)),
                        w, inc="t" if kc == nch - 1 else None)
                ev = V_(lambda e, h=h, i=i, pvb=pvb: e.tensor_copy(
                    AR[:, A_AOT + h * 1024 + i * 128:A_AOT + h * 1024 + i * 128 + 128],
                    pb[pvb][:, :128]), [mtok], big=True)
                bank_r[pvb] = ev
                aot_tok.append(ev)
            pv_flip_l = [0]

            for i in range(NOWN):
                ncol = 256 * (i + 1)
                nch = 2 * i + 2
                nsl = (ncol + 511) // 512
                for h in range(4):
                    qt_ap = AR[:, A_QT + h * 1024 + i * 128:
                               A_QT + h * 1024 + i * 128 + 128]
                    smm = []
                    for sl in range(nsl):
                        cn = min(512, ncol - sl * 512)
                        tkn = T_(lambda e, sl=sl, h=h, cn=cn, q=qt_ap: e.matmul(
                            pb[sl][:, :cn], q,
                            AR[:, A_KT + h * 2048 + sl * 512:
                               A_KT + h * 2048 + sl * 512 + cn],
                            start=True, stop=True),
                            [bank_r[sl]] + (proj_done if (i == 0 and h == 0) else []),
                            inc="t")
                        smm.append(tkn)
                    if pend[0] is not None:
                        emit_ptpv(*pend[0])
                    sl_m, off_m = (256 * i) // 512, (256 * i) % 512
                    tmask = V_(lambda e, sl=sl_m, off=off_m: e.tensor_tensor(
                        pb[sl][:, off:off + 256], pb[sl][:, off:off + 256],
                        C_BM[:], ALU.add), smm, big=True)
                    rmax = []
                    for sl in range(nsl):
                        cn = min(512, ncol - sl * 512)
                        rmax.append(V_(lambda e, sl=sl, cn=cn: e.reduce_max(
                            st(ST_NM + sl), pb[sl][:, :cn], axis=AX.X, negate=True),
                            [smm[sl], tmask]))
                    tot = rmax[0]
                    for sl in range(1, nsl):
                        tot = V_(lambda e, sl=sl: e.tensor_tensor(
                            st(ST_NM), st(ST_NM), st(ST_NM + sl), ALU.min), [rmax[sl]])
                    tb = V_(lambda e: e.tensor_scalar(
                        st(ST_NT), st(ST_NM), ISQ, None, ALU.mult), [tot])
                    exps = []
                    for sl in range(nsl):
                        cn = min(512, ncol - sl * 512)
                        exps.append(S_(lambda e, sl=sl, cn=cn: e.activation(
                            AR[:, A_P + sl * 512:A_P + sl * 512 + cn],
                            pb[sl][:, :cn], AF.Exp, bias=st(ST_NT), scale=ISQ,
                            accum_out=st(ST_LS + sl)), [tb, p_guard[0]], big=True))
                        bank_r[sl] = exps[-1]
                    V_(lambda e: e.tensor_copy(st(ST_LT), st(ST_LS)), [exps[0]])
                    tl = None
                    for sl in range(1, nsl):
                        tl = V_(lambda e, sl=sl: e.tensor_tensor(
                            st(ST_LT), st(ST_LT), st(ST_LS + sl), ALU.add), [exps[sl]])
                    tli = V_(lambda e: e.reciprocal(st(ST_LI), st(ST_LT)), [tl])
                    psc = V_(lambda e, ncol=ncol: e.tensor_scalar(
                        AR[:, A_P:A_P + ncol], AR[:, A_P:A_P + ncol],
                        st(ST_LI), None, ALU.mult), [tli], big=True)
                    pend[0] = (i, h, nch, psc)
            emit_ptpv(*pend[0])

            last_wo = None
            for s4 in range(4):
                for i in range(NOWN):
                    bk = i % 4
                    mtok = None
                    for hc in range(4):
                        w = [bank_r[bk], wo_tok] + (aot_tok if (s4 == 0 and i == 0) else [])
                        w = w if hc == 0 else []
                        mtok = T_(lambda e, bk=bk, hc=hc, i=i, s4=s4: e.matmul(
                            pb[bk],
                            AR[:, A_AOT + hc * 1024 + i * 128:
                               A_AOT + hc * 1024 + i * 128 + 128],
                            AR[:, s4 * 2048 + hc * 512:s4 * 2048 + hc * 512 + 512],
                            start=(hc == 0), stop=(hc == 3)),
                            w, inc="t" if hc == 3 else None)
                    ev = V_(lambda e, bk=bk, i=i, s4=s4: e.tensor_tensor(
                        XR[:, i * D + s4 * 512:i * D + s4 * 512 + 512],
                        pb[bk],
                        XR[:, i * D + s4 * 512:i * D + s4 * 512 + 512], ALU.add),
                        [mtok, xr_tok[i][s4]], big=True)
                    bank_r[bk] = ev
                    xr_tok[i][s4] = ev
                    last_wo = ev
            wave_free[0] = last_wo
            wave_free[1] = last_wo

    A1_H = 0
    A1_W1 = 11264
    A1_W3 = 15360
    A1_W2 = 19456
    A1_XN = 30720
    A1_G8 = 32768

    if upto in ("p2", "p3", "p4"):
        g1b = G1[:].bitcast(BF16)
        g2b = G2[:].bitcast(BF16)

        def xt1_strip(dc):
            if dc < 8:
                return WS[:, dc * 1024:(dc + 1) * 1024]
            if dc < 12:
                return g1b[:, (dc - 8) * 1024:(dc - 7) * 1024]
            return g2b[:, (dc - 12) * 1024:(dc - 11) * 1024]

        tgw = P.dma("gpsimd",
                    AR[:, A1_G8:A1_G8 + 128].rearrange("p (dc o) -> p dc o", o=NE),
                    gate_wd.rearrange("dc p o -> p dc o"),
                    waits=[xr_tok[i][s] for i in range(NOWN) for s in range(4)],
                    sem="dg")

        xt1_tok = []
        for i in range(NOWN):
            xn = AR[:, A1_XN:A1_XN + D]
            tln = emit_ln(XR[:, i * D:(i + 1) * D], xn,
                          xr_tok[i] + [xt1_tok[-1] if xt1_tok else ws_free])
            done = 0
            while done < 16:
                bank = "t6" if tr_flip[0] % 2 == 0 else "t7"
                tr_flip[0] += 1
                pbank = pbt[bank]
                ttok = None
                for j in range(4):
                    w = [tln, bank_r[bank]] if j == 0 else []
                    ttok = T_(lambda e, j=j, dj=done, pbk=pbank, xn=xn: e.transpose(
                        pbk[:, j * 128:(j + 1) * 128],
                        xn[:, (dj + j) * 128:(dj + j + 1) * 128], C_ID[:]),
                        w, inc="t" if j == 3 else None)
                evl = None
                for j in range(4):
                    evl = V_(lambda e, j=j, dj=done, i=i, pbk=pbank: e.tensor_copy(
                        xt1_strip(dj + j)[:, i * 128:(i + 1) * 128],
                        pbk[:, j * 128:(j + 1) * 128]), [ttok, tok_g1, tok_g2], big=True)
                bank_r[bank] = evl
                xt1_tok.append(evl)
                done += 4
        xt1_all = xt1_tok[-1]

        SC_CW = 1536
        cw_tok = [None] * NOWN
        for i in range(NOWN):
            mtok = None
            for dc in range(16):
                w = [xt1_all, tgw, bank_r[5]] if dc == 0 else []
                mtok = T_(lambda e, dc=dc, i=i: e.matmul(
                    pb[5][:, :NE],
                    xt1_strip(dc)[:, i * 128:(i + 1) * 128],
                    AR[:, A1_G8 + dc * NE:A1_G8 + (dc + 1) * NE],
                    start=(dc == 0), stop=(dc == 15)),
                    w, inc="t" if dc == 15 else None)
            pr = SC[:, SC_CW + i * NE:SC_CW + (i + 1) * NE]
            t1 = V_(lambda e: e.reduce_max(st(ST_M1), pb[5][:, :NE], axis=AX.X,
                                           negate=True), [mtok])
            t2 = S_(lambda e, pr=pr: e.activation(pr, pb[5][:, :NE], AF.Exp,
                                                  bias=st(ST_M1),
                                                  accum_out=st(ST_M2)), [t1])
            bank_r[5] = t2
            t3 = V_(lambda e: e.reciprocal(st(ST_M2), st(ST_M2)), [t2])
            t4 = V_(lambda e, pr=pr: e.tensor_scalar(pr, pr, st(ST_M2), None, ALU.mult), [t3])
            tmp8 = SC[:, SC_DUMP:SC_DUMP + NE]
            t5 = V_(lambda e, pr=pr: e.reduce_max(st(ST_M1), pr, axis=AX.X), [t4])
            t6 = V_(lambda e, pr=pr: e.tensor_scalar(tmp8, pr, st(ST_M1), None, ALU.is_ge), [t5])
            t7 = V_(lambda e, pr=pr: e.scalar_tensor_tensor(
                tmp8, tmp8, -1.0e9, pr, ALU.mult, ALU.add), [t6])
            t8 = V_(lambda e: e.reduce_max(st(ST_M2), tmp8, axis=AX.X), [t7])
            t9 = V_(lambda e, pr=pr: e.tensor_scalar(tmp8, pr, st(ST_M2), None, ALU.is_ge), [t8])
            cw_tok[i] = V_(lambda e, pr=pr: e.tensor_tensor(pr, pr, tmp8, ALU.mult), [t9])

        if tap == "ar_moe":
            P.dma("gpsimd", dbg_ar[:, :64], SC[:, SC_CW:SC_CW + 64],
                  waits=cw_tok, sem="dout")

        w1_free = [None, None]
        w3_free = [None, None]
        w2_free = [None, None]
        h_free = [None] * 11

        def expert_pass(ew1, ew3, ew2, eidx, cw_col, n_fc, fc0):
            h_toks = []
            for fc in range(n_fc):
                kb = fc % 2
                t1w = P.dma("gpsimd",
                            AR[:, A1_W1 + kb * 2048:A1_W1 + (kb + 1) * 2048]
                            .rearrange("p (dc f) -> p dc f", f=128),
                            ew1[fc].rearrange("dc p f -> p dc f"),
                            waits=[w1_free[kb]], sem="dm1a" if kb == 0 else "dm1b")
                t3w = P.dma("gpsimd",
                            AR[:, A1_W3 + kb * 2048:A1_W3 + (kb + 1) * 2048]
                            .rearrange("p (dc f) -> p dc f", f=128),
                            ew3[fc].rearrange("dc p f -> p dc f"),
                            waits=[w3_free[kb]], sem="dm3a" if kb == 0 else "dm3b")
                ma = mb = None
                for sl in range(2):
                    for dc in range(16):
                        w = [bank_r[sl], t1w, xt1_all] if dc == 0 else []
                        ma = T_(lambda e, sl=sl, dc=dc, kb=kb: e.matmul(
                            pb[sl],
                            AR[:, A1_W1 + kb * 2048 + dc * 128:
                               A1_W1 + kb * 2048 + (dc + 1) * 128],
                            xt1_strip(dc)[:, sl * 512:(sl + 1) * 512],
                            start=(dc == 0), stop=(dc == 15)),
                            w, inc="t" if dc == 15 else None)
                    bank_r[sl] = ma
                for sl in range(2):
                    for dc in range(16):
                        w = [bank_r[2 + sl], t3w] if dc == 0 else []
                        mb = T_(lambda e, sl=sl, dc=dc, kb=kb: e.matmul(
                            pb[2 + sl][:],
                            AR[:, A1_W3 + kb * 2048 + dc * 128:
                               A1_W3 + kb * 2048 + (dc + 1) * 128],
                            xt1_strip(dc)[:, sl * 512:(sl + 1) * 512],
                            start=(dc == 0), stop=(dc == 15)),
                            w, inc="t" if dc == 15 else None)
                hts = []
                for sl in range(2):
                    sil = SC[:, sl * 512:(sl + 1) * 512]
                    tsl = S_(lambda e, sl=sl, sil=sil: e.activation(
                        sil, pb[sl], AF.Silu), [ma, mb, h_free[fc]], big=True)
                    bank_r[sl] = tsl
                    hv = V_(lambda e, sl=sl, fc=fc, sil=sil: e.tensor_tensor(
                        AR[:, A1_H + fc * 1024 + sl * 512:A1_H + fc * 1024 + (sl + 1) * 512],
                        sil, pb[2 + sl][:], ALU.mult), [tsl], big=True)
                    bank_r[2 + sl] = hv
                    hts.append(hv)
                w1_free[kb] = hts[-1]
                w3_free[kb] = hts[-1]
                h_toks.append(hts[-1])
            for s4 in range(4):
                kb = s4 % 2
                t2w = P.dma("gpsimd",
                            AR[:, A1_W2 + kb * 5632:A1_W2 + kb * 5632 + n_fc * 512]
                            .rearrange("p (fc o) -> p fc o", o=512),
                            ew2[s4].rearrange("fc p o -> p fc o"),
                            waits=[w2_free[kb]], sem="dm2a" if kb == 0 else "dm2b")
                lastmm = None
                for i in range(NOWN):
                    bk = 4 + (i % 2)
                    mtok = None
                    for fc in range(n_fc):
                        w = ([bank_r[bk], t2w, h_toks[-1]] if fc == 0 else [])
                        mtok = T_(lambda e, bk=bk, fc=fc, i=i, kb=kb: e.matmul(
                            pb[bk],
                            AR[:, A1_H + fc * 1024 + i * 128:A1_H + fc * 1024 + i * 128 + 128],
                            AR[:, A1_W2 + kb * 5632 + fc * 512:
                               A1_W2 + kb * 5632 + (fc + 1) * 512],
                            start=(fc == 0), stop=(fc == n_fc - 1)),
                            w, inc="t" if fc == n_fc - 1 else None)
                    lastmm = mtok
                    if cw_col is None:
                        ev = V_(lambda e, bk=bk, i=i, s4=s4: e.tensor_tensor(
                            XR[:, i * D + s4 * 512:i * D + s4 * 512 + 512],
                            pb[bk],
                            XR[:, i * D + s4 * 512:i * D + s4 * 512 + 512],
                            ALU.add), [mtok, xr_tok[i][s4]], big=True)
                    else:
                        ev = V_(lambda e, bk=bk, i=i, s4=s4: e.scalar_tensor_tensor(
                            XR[:, i * D + s4 * 512:i * D + s4 * 512 + 512],
                            pb[bk],
                            SC[:, SC_CW + i * NE + eidx:SC_CW + i * NE + eidx + 1],
                            XR[:, i * D + s4 * 512:i * D + s4 * 512 + 512],
                            ALU.mult, ALU.add), [mtok, xr_tok[i][s4], cw_tok[i]], big=True)
                    bank_r[bk] = ev
                    xr_tok[i][s4] = ev
                w2_free[kb] = ev
            for fc in range(n_fc):
                h_free[fc] = lastmm
            return lastmm

        moe_free = expert_pass([sw1[fc] for fc in range(11)],
                               [sw3[fc] for fc in range(11)],
                               [sw2[s][0:11] for s in range(4)], None, None, 11, 0)
        if tap == "ar_moe":
            P.dma("sync", dbg_ar[:, 1024:1024 + 11264], AR[:, A1_H:A1_H + 11264],
                  waits=[moe_free], sem="dout")
            P.op("sync", lambda e: e.nop(), waits=[("dout", P.cnt["dout"], "sync")])
        moe_free = expert_pass([sw1[fc] for fc in range(11, 22)],
                               [sw3[fc] for fc in range(11, 22)],
                               [sw2[s][11:22] for s in range(4)], None, None, 11, 0)
        for e_ in range(NE):
            moe_free = expert_pass([rw1[e_][fc] for fc in range(11)],
                                   [rw3[e_][fc] for fc in range(11)],
                                   [rw2[e_][s] for s in range(4)], e_, True, 11, 0)

    A2_XT = 0
    A2_ZS = 16384
    A2_ZF = 0
    A2_MT = 32768
    SC_AK = 2048
    SC_ARG = 2560
    SC_TK = 2688
    SC_ACC = 2816
    SC_BC = 2944
    SC_H = 0

    if upto in ("p3", "p4"):
        tg1 = P.dma("sync", G1[:], l1gam[:], waits=[xr_tok[0][0], moe_free],
                    sem="dy")
        tg2 = P.dma("sync", G2[:], l1bet[:], sem="dy")
        bg3 = dy_barrier()
        tg1 = tg2 = bg3


        xt2_tok = []
        for i in range(NOWN):
            xn = AR[:, A2_ZS:A2_ZS + D]
            tln = emit_ln(XR[:, i * D:(i + 1) * D], xn,
                          xr_tok[i] + ([xt2_tok[-1]] if xt2_tok else []))
            done = 0
            while done < 16:
                bank = "t6" if tr_flip[0] % 2 == 0 else "t7"
                tr_flip[0] += 1
                pbank = pbt[bank]
                ttok = None
                for j in range(4):
                    w = [tln, bank_r[bank]] if j == 0 else []
                    ttok = T_(lambda e, j=j, dj=done, pbk=pbank, xn=xn: e.transpose(
                        pbk[:, j * 128:(j + 1) * 128],
                        xn[:, (dj + j) * 128:(dj + j + 1) * 128], C_ID[:]),
                        w, inc="t" if j == 3 else None)
                evl = None
                for j in range(4):
                    evl = V_(lambda e, j=j, dj=done, i=i, pbk=pbank: e.tensor_copy(
                        AR[:, A2_XT + (dj + j) * 1024 + i * 128:
                           A2_XT + (dj + j) * 1024 + i * 128 + 128],
                        pbk[:, j * 128:(j + 1) * 128]), [ttok], big=True)
                bank_r[bank] = evl
                xt2_tok.append(evl)
                done += 4
        xt2_all = xt2_tok[-1]

        zdma = []
        zproj_last = None
        tl1w = None
        zst_free = [None, None]
        for s4 in range(4):
            tl1w = P.dma("gpsimd",
                         WS[:].rearrange("p (dc o) -> p dc o", o=512),
                         l1w[s4].rearrange("dc p o -> p dc o"),
                         waits=[moe_free if s4 == 0 else zproj_last], sem="dwa")
            for i in range(NOWN):
                bk = i % 4
                mtok = None
                for dc in range(16):
                    w = [bank_r[bk], tl1w, xt2_all] if dc == 0 else []
                    mtok = T_(lambda e, bk=bk, dc=dc, i=i: e.matmul(
                        pb[bk],
                        AR[:, A2_XT + dc * 1024 + i * 128:A2_XT + dc * 1024 + i * 128 + 128],
                        WS[:, dc * 512:(dc + 1) * 512],
                        start=(dc == 0), stop=(dc == 15)),
                        w, inc="t" if dc == 15 else None)
                zproj_last = mtok
                ko = i % 2
                ev = V_(lambda e, bk=bk, ko=ko: e.tensor_copy(
                    AR[:, A2_ZS + ko * 512:A2_ZS + (ko + 1) * 512], pb[bk]),
                    [mtok, zst_free[ko]], big=True)
                bank_r[bk] = ev
                zdma.append(P.dma("sync", ag_in[i][:, s4 * 512:(s4 + 1) * 512],
                                  AR[:, A2_ZS + ko * 512:A2_ZS + (ko + 1) * 512],
                                  waits=[ev], sem="dzc"))
                zst_free[ko] = zdma[-1]

        t_ag = P.op("gpsimd", lambda e: e.collective_compute(
            "AllGather", ALU.bypass,
            replica_groups=[[0, 1], [2, 3], [4, 5], [6, 7]],
            ins=[ag_in], outs=[ag_out]),
            waits=[("dzc", P.cnt["dzc"], "sync")], inc="ag", inc_amt=1)

        tcm = V_(lambda e: e.memset(st(ST_CM), 0.0))
        ab_free = None
        for dc in range(NTC):
            tda = P.dma("sync", SC[:, 0:D], ffnw_abs[dc], waits=[ab_free], sem="dab")
            trd = V_(lambda e: e.reduce_max(st(ST_T0), SC[:, 0:D], axis=AX.X,
                                            apply_absolute_value=True), [tda, tcm])
            ab_free = V_(lambda e: e.tensor_tensor(st(ST_CM), st(ST_CM), st(ST_T0),
                                                   ALU.max), [trd])
        tt1 = T_(lambda e: e.matmul(pb[5][0:1, :128], st(ST_CM), C_IDF[:],
                                    start=True, stop=True),
                 [ab_free, bank_r[5]] + tok_const)
        tt2 = V_(lambda e: e.reduce_max(R1[0:1, 128:129], pb[5][0:1, :128], axis=AX.X), [tt1])
        bank_r[5] = tt2
        tt3 = T_(lambda e: e.matmul(pb[5][:, :1], C_ON[:], R1[0:1, 128:129],
                                    start=True, stop=True), [tt2, bank_r[5]])
        tscl = V_(lambda e: e.tensor_copy(st(ST_SCL), pb[5][:, :1]), [tt3])
        bank_r[5] = tscl

        P.dma("sync", R1[0:1, 32:36], lam_logit[:], sem="dy")
        P.dma("sync", R1[0:1, 64:68], eta_row[:], sem="dy")
        tll = tet = dy_barrier()
        ts1 = S_(lambda e: e.activation(R1[0:1, 36:40], R1[0:1, 32:36], AF.Sigmoid), [tll])
        ts2b = V_(lambda e: e.tensor_scalar(R1[0:1, 36:40], R1[0:1, 36:40], 1.0e-30,
                                            None, ALU.max), [ts1])
        ts3 = S_(lambda e: e.activation(R1[0:1, 68:72], R1[0:1, 36:40], AF.Ln), [ts2b])
        rv = []
        for k4 in range(4):
            rv.append(V_(lambda e, k4=k4: e.tensor_scalar(
                R1[0:1, k4 * 16:(k4 + 1) * 16], C_IM[:],
                R1[0:1, 68 + k4:69 + k4], None, ALU.mult), [ts3] + tok_const))
        tbc = T_(lambda e: e.matmul(pb[5][:, :72], C_ON[:], R1[0:1, 0:72],
                                    start=True, stop=True),
                 rv + [tet, ts3, bank_r[5]])
        tbce = V_(lambda e: e.tensor_copy(SC[:, SC_BC:SC_BC + 72], pb[5][:, :72]), [tbc])
        bank_r[5] = tbce

        ak_tok = []
        for k4 in range(4):
            ak_tok.append(V_(lambda e, k4=k4: e.tensor_scalar(
                SC[:, SC_AK + k4 * 128:SC_AK + (k4 + 1) * 128], C_DL[:],
                SC[:, SC_BC + 68 + k4:SC_BC + 69 + k4], None, ALU.mult),
                [tbce] + tok_const))
        mt_tok = []
        for m in range(16):
            tacc = V_(lambda e: e.memset(SC[:, SC_ACC:SC_ACC + 128], 0.0),
                      [mt_tok[-1] if mt_tok else None])
            for k4 in range(4):
                if m < 2:
                    targ = V_(lambda e, k4=k4, m=m: e.tensor_tensor(
                        SC[:, SC_ARG:SC_ARG + 128],
                        SC[:, SC_AK + k4 * 128:SC_AK + (k4 + 1) * 128],
                        C_N2[:, m * 128:(m + 1) * 128], ALU.add), [ak_tok[k4]], big=True)
                    src = SC[:, SC_ARG:SC_ARG + 128]
                else:
                    targ = ak_tok[k4]
                    src = SC[:, SC_AK + k4 * 128:SC_AK + (k4 + 1) * 128]
                tex = S_(lambda e, src=src, k4=k4, m=m: e.activation(
                    SC[:, SC_TK:SC_TK + 128], src, AF.Exp,
                    bias=SC[:, SC_BC + 16 * k4 + m:SC_BC + 16 * k4 + m + 1]),
                    [targ, tacc], big=True)
                tacc = V_(lambda e, k4=k4: e.scalar_tensor_tensor(
                    SC[:, SC_ACC:SC_ACC + 128], SC[:, SC_TK:SC_TK + 128],
                    SC[:, SC_BC + 64 + k4:SC_BC + 64 + k4 + 1],
                    SC[:, SC_ACC:SC_ACC + 128], ALU.mult, ALU.add), [tex], big=True)
            if m < 2:
                tacc = V_(lambda e, m=m: e.tensor_tensor(
                    SC[:, SC_ACC:SC_ACC + 128], SC[:, SC_ACC:SC_ACC + 128],
                    C_E2[:, m * 128:(m + 1) * 128], ALU.add), [tacc], big=True)
            mt_tok.append(V_(lambda e, m=m: e.tensor_copy(
                AR[:, A2_MT + m * 128:A2_MT + (m + 1) * 128],
                SC[:, SC_ACC:SC_ACC + 128]), [tacc], big=True))
        mt_all = mt_tok[-1]

        zrb = []
        agw = [t_ag, zproj_last]
        for g2 in range(NTC):
            p2, idx = g2 % 2, g2 // 2
            zrb.append(P.dma("sync", AR[:, A2_ZF + g2 * 2048:A2_ZF + (g2 + 1) * 2048],
                             ag_out[p2, idx], waits=agw, sem="dzc"))
        zrb_all = ("dzc", P.cnt["dzc"], "sync")

        if tap == "ar_l1":
            P.dma("sync", dbg_ar[:, :34816], AR[:, :34816],
                  waits=[mt_all, ("dzc", P.cnt["dzc"], "sync")], sem="dout")
            P.op("sync", lambda e: e.nop(), waits=[("dout", P.cnt["dout"], "sync")])

        for i in range(NOWN):
            htoks = []
            for s4 in range(4):
                bk = s4 % 4
                mtok = None
                nj = 2 * i + 2
                for j in range(nj):
                    midx = (2 * i + 1) - j
                    w = [bank_r[bk], mt_all, zrb_all] if j == 0 else []
                    mtok = T_(lambda e, bk=bk, j=j, s4=s4, midx=midx, nj=nj: e.matmul(
                        pb[bk],
                        AR[:, A2_MT + midx * 128:A2_MT + (midx + 1) * 128],
                        AR[:, A2_ZF + j * 2048 + s4 * 512:A2_ZF + j * 2048 + (s4 + 1) * 512],
                        start=(j == 0), stop=(j == nj - 1)),
                        w, inc="t" if j == nj - 1 else None)
                ev = V_(lambda e, bk=bk, s4=s4: e.tensor_copy(
                    SC[:, SC_H + s4 * 512:SC_H + (s4 + 1) * 512], pb[bk]),
                    [mtok, ab_free], big=True)
                bank_r[bk] = ev
                htoks.append(ev)
            hn = SC[:, SC_DUMP:SC_DUMP + D]
            h_ap = SC[:, SC_H:SC_H + D]
            mu, rs, sq, t0 = st(ST_MU), st(ST_RS), st(ST_SQ), st(ST_T0)
            V_(lambda e: e.reduce_sum(mu, h_ap, axis=AX.X), htoks)
            tsq = S_(lambda e: e.activation(SC[:, SC_DUMP:SC_DUMP + D], h_ap,
                                            AF.Square, accum_out=sq), htoks, big=True)
            V_(lambda e: e.tensor_scalar(mu, mu, 1.0 / D, None, ALU.mult))
            tv = V_(lambda e: e.tensor_scalar(sq, sq, 1.0 / D, None, ALU.mult), [tsq])
            V_(lambda e: e.tensor_tensor(t0, mu, mu, ALU.mult))
            V_(lambda e: e.tensor_tensor(t0, sq, t0, ALU.subtract), [tv])
            tve2 = V_(lambda e: e.tensor_scalar(t0, t0, 1.0, EPS, ALU.mult, ALU.add))
            tsr = S_(lambda e: e.sqrt(t0, t0), [tve2])
            trc = V_(lambda e: e.reciprocal(rs, t0), [tsr])
            tnm = V_(lambda e: e.tensor_scalar(h_ap, h_ap, mu, rs,
                                               ALU.subtract, ALU.mult), [trc], big=True)
            tg = V_(lambda e: e.tensor_tensor(h_ap, h_ap, G1[:], ALU.mult),
                    [tnm, tg1], big=True)
            tb2 = V_(lambda e: e.tensor_tensor(h_ap, h_ap, G2[:], ALU.add), [tg, tg2], big=True)
            for s4 in range(4):
                tfin = V_(lambda e, i=i, s4=s4: e.tensor_tensor(
                    XR[:, i * D + s4 * 512:i * D + s4 * 512 + 512],
                    XR[:, i * D + s4 * 512:i * D + s4 * 512 + 512],
                    SC[:, SC_H + s4 * 512:SC_H + (s4 + 1) * 512], ALU.add),
                    [tb2, xr_tok[i][s4]], big=True)
                xr_tok[i][s4] = tfin

    A3_XNT = 0
    A3_SW = 16384

    if upto == "p4":
        tg1b = P.dma("sync", G1[:], lnw_t[:],
                     waits=[xr_tok[i][s] for i in range(NOWN) for s in range(4)],
                     sem="dy")
        tg2b = P.dma("sync", G2[:], lnb_t[:], sem="dy")

        xnt_tok = []
        for i in range(NOWN):
            xn = SC[:, 0:D]
            tln = emit_ln(XR[:, i * D:(i + 1) * D], xn, xr_tok[i] + [tg1b, tg2b])
            tm1 = V_(lambda e: e.tensor_tensor(xn, xn, G1[:], ALU.mult), [tln, tg1b], big=True)
            xnb = AR[:, A2_ZS:A2_ZS + D]
            tm2 = V_(lambda e: e.tensor_tensor(xnb, xn, G2[:], ALU.add),
                     [tm1, tg2b] + ([xnt_tok[-1]] if xnt_tok else []), big=True)
            done = 0
            while done < 16:
                bank = "t6" if tr_flip[0] % 2 == 0 else "t7"
                tr_flip[0] += 1
                pbank = pbt[bank]
                ttok = None
                for j in range(4):
                    w = [tm2, bank_r[bank]] if j == 0 else []
                    ttok = T_(lambda e, j=j, dj=done, pbk=pbank, xnb=xnb: e.transpose(
                        pbk[:, j * 128:(j + 1) * 128],
                        xnb[:, (dj + j) * 128:(dj + j + 1) * 128], C_ID[:]),
                        w, inc="t" if j == 3 else None)
                evl = None
                for j in range(4):
                    evl = V_(lambda e, j=j, dj=done, i=i, pbk=pbank: e.tensor_copy(
                        AR[:, A3_XNT + (dj + j) * 1024 + i * 128:
                           A3_XNT + (dj + j) * 1024 + i * 128 + 128],
                        pbk[:, j * 128:(j + 1) * 128]), [ttok], big=True)
                bank_r[bank] = evl
                xnt_tok.append(evl)
                done += 4
        xnt_all = xnt_tok[-1]

        tg1c = P.dma("sync", G1[:], fgam[:], waits=[xnt_all], sem="dy")
        tg2c = P.dma("sync", G2[:], fbet[:], sem="dy")

        sw_free = [None, None]
        out_cnt = 0
        for s4 in range(4):
            kb = s4 % 2
            swb = AR[:, A3_SW + kb * 8192:A3_SW + (kb + 1) * 8192]
            tdw = P.dma("gpsimd",
                        swb.rearrange("p (dc o) -> p dc o", o=512),
                        ffnw[s4].rearrange("dc p o -> p dc o"),
                        waits=[sw_free[kb], xnt_all],
                        sem="dm1a" if kb == 0 else "dm1b")
            tsg = S_(lambda e, swb=swb: e.sign(swb, swb), [tdw], big=True)
            lastmm = None
            for i in range(NOWN):
                bk = i % 4
                mtok = None
                for dc in range(16):
                    w = [bank_r[bk], tsg, xnt_all] if dc == 0 else []
                    mtok = T_(lambda e, bk=bk, dc=dc, i=i, swb=swb: e.matmul(
                        pb[bk],
                        AR[:, A3_XNT + dc * 1024 + i * 128:A3_XNT + dc * 1024 + i * 128 + 128],
                        swb[:, dc * 512:(dc + 1) * 512],
                        start=(dc == 0), stop=(dc == 15)),
                        w, inc="t" if dc == 15 else None)
                lastmm = mtok
                ko = out_cnt % 2
                out_cnt += 1
                t1 = V_(lambda e, bk=bk, ko=ko, s4=s4: e.scalar_tensor_tensor(
                    SC[:, ko * 512:ko * 512 + 512], pb[bk], st(ST_SCL),
                    G1[:, s4 * 512:s4 * 512 + 512], ALU.mult, ALU.mult),
                    [mtok, tscl, tg1c], big=True)
                bank_r[bk] = t1
                t2 = V_(lambda e, ko=ko, s4=s4: e.tensor_tensor(
                    SC[:, ko * 512:ko * 512 + 512], SC[:, ko * 512:ko * 512 + 512],
                    G2[:, s4 * 512:s4 * 512 + 512], ALU.add), [t1, tg2c], big=True)
                t3 = V_(lambda e, ko=ko, i=i, s4=s4: e.tensor_tensor(
                    SC[:, ko * 512:ko * 512 + 512], SC[:, ko * 512:ko * 512 + 512],
                    XR[:, i * D + s4 * 512:i * D + s4 * 512 + 512], ALU.add),
                    [t2, xr_tok[i][s4], ("dout", P.cnt.get("dout", 0), "sync")], big=True)
                P.dma("sync", y_out[i][:, s4 * 512:s4 * 512 + 512],
                      SC[:, ko * 512:ko * 512 + 512], waits=[t3], sem="dout")
            sw_free[kb] = lastmm
        P.op("sync", lambda e: e.nop(), waits=[("dout", P.cnt["dout"], "sync")])

    if tap is True:
        tw = []
        for i in range(NOWN):
            tw.append(P.dma("sync", dbg[i], XR[:, i * D:(i + 1) * D],
                            waits=xr_tok[i] if xr_tok[i][0] else spill_all,
                            sem="dout"))
        P.op("sync", lambda e: e.nop(), waits=[("dout", P.cnt["dout"], "sync")])
    if upto != "p4" and not tap:
        P.op("sync", lambda e: e.nop(), waits=spill_all)

    semh = {s: nc.alloc_semaphore(s) for s in ALL_SEMS}
    blk_cm = nc.Block()
    blk = blk_cm.__enter__()
    P.emit(blk, semh)
    blk_cm.__exit__(None, None, None)
    return nc


def prep_inputs(L0_attn_x, L0_attn_Wq, L0_attn_Wk, L0_attn_Wv, L0_attn_Wo,
                L0_attn_gq, L0_attn_gk, L0_ffn_gate_w, L0_ffn_sw1, L0_ffn_sw2,
                L0_ffn_sw3, L0_ffn_rw1, L0_ffn_rw2, L0_ffn_rw3, L1_attn_W,
                L1_attn_lam_logit, L1_attn_eta, L1_attn_gamma, L1_attn_beta,
                L1_ffn_weight, L1_ffn_gamma, L1_ffn_beta, L1_ffn_ln_weight,
                L1_ffn_ln_bias):
    f = np.float32
    asn = np.ascontiguousarray

    def t2(w):
        return asn(np.asarray(w, f).T)

    wqT = t2(L0_attn_Wq)
    wkT = t2(L0_attn_Wk)
    wvT = t2(L0_attn_Wv)
    woT = t2(L0_attn_Wo)
    def gsplit(w):
        return asn(w.reshape(16, 128, 4, 512).transpose(2, 0, 1, 3))
    wq_h = gsplit(wqT)
    wk_h = gsplit(wkT)
    wv_h = gsplit(wvT)
    wo_h = asn(woT.reshape(4, 4, 128, 4, 512).transpose(0, 3, 1, 2, 4))
    gq = np.asarray(L0_attn_gq, f)
    gk = np.asarray(L0_attn_gk, f)
    gqt = asn(np.broadcast_to(np.tile(gq, 16)[None, :], (128, D)))
    gkt = asn(np.broadcast_to(np.tile(gk, 16)[None, :], (128, D)))
    gate_h = asn(np.asarray(L0_ffn_gate_w, f).reshape(16, 128, NE))
    sw1_h = asn(np.asarray(L0_ffn_sw1, f).reshape(16, 128, 22, 128)
                .transpose(2, 0, 1, 3))
    sw3_h = asn(np.asarray(L0_ffn_sw3, f).reshape(16, 128, 22, 128)
                .transpose(2, 0, 1, 3))
    sw2_h = asn(np.asarray(L0_ffn_sw2, f).reshape(22, 128, 4, 512)
                .transpose(2, 0, 1, 3))
    rw1_h = asn(np.asarray(L0_ffn_rw1, f).reshape(NE, 16, 128, 11, 128)
                .transpose(0, 3, 1, 2, 4))
    rw3_h = asn(np.asarray(L0_ffn_rw3, f).reshape(NE, 16, 128, 11, 128)
                .transpose(0, 3, 1, 2, 4))
    rw2_h = asn(np.asarray(L0_ffn_rw2, f).reshape(NE, 11, 128, 4, 512)
                .transpose(0, 3, 1, 2, 4))
    l1wT = t2(L1_attn_W)
    l1w_h = asn(l1wT.reshape(16, 128, 4, 512).transpose(2, 0, 1, 3))
    ffnT = t2(L1_ffn_weight)
    ffn_h = asn(ffnT.reshape(16, 128, 4, 512).transpose(2, 0, 1, 3))
    ffn_abs = asn(ffnT.reshape(16, 128, D))
    lam_h = np.asarray(L1_attn_lam_logit, f).reshape(1, 4)
    eta_h = np.asarray(L1_attn_eta, f).reshape(1, 4)

    def tile_row(v):
        return asn(np.broadcast_to(np.asarray(v, f)[None, :], (128, D)))

    l1gam = tile_row(L1_attn_gamma)
    l1bet = tile_row(L1_attn_beta)
    lnw_t = tile_row(L1_ffn_ln_weight)
    lnb_t = tile_row(L1_ffn_ln_bias)
    fgam = tile_row(L1_ffn_gamma)
    fbet = tile_row(L1_ffn_beta)

    ident = np.eye(128, dtype=f)
    onesr = np.ones((1, 128), f)
    r = np.arange(128)
    deltam = asn((r[None, :] - r[:, None]).astype(f))
    iotam_base = 128.0 * (np.arange(16, dtype=f) - 1.0)

    x = np.asarray(L0_attn_x, f)
    in_maps = []
    for c in range(8):
        b, p = c // 2, c % 2
        own = [2 * i + p for i in range(NOWN)]
        x_own = asn(x[b].reshape(NTC, 128, D)[own])
        x_full = asn(x[b].reshape(NTC, 128, D))
        cidx = np.arange(256)[None, :]
        ridx = r[:, None]
        bmask = np.where(cidx <= 128 * p + ridx, 0.0, NEGV).astype(f)
        n2 = np.zeros((128, 256), f)
        e2 = np.zeros((128, 256), f)
        tri_bad = (deltam < 0)
        for m in range(2):
            mm = m * 128
            if m == 1 - p:
                n2[:, mm:mm + 128] = np.where(tri_bad, NEGV, 0.0)
                e2[:, mm:mm + 128] = ident
            elif m < 1 - p:
                n2[:, mm:mm + 128] = NEGV
        iotam = (iotam_base + 128.0 * p).reshape(1, 16).astype(f)
        in_maps.append({
            "x_own": x_own, "x_full": x_full,
            "wq": wq_h, "wk": wk_h, "wv": wv_h, "wo": wo_h,
            "gqt": gqt, "gkt": gkt, "gate_w": gate_h,
            "sw1": sw1_h, "sw3": sw3_h, "sw2": sw2_h,
            "rw1": rw1_h, "rw3": rw3_h, "rw2": rw2_h,
            "l1w": l1w_h, "lam_logit": lam_h, "eta": eta_h,
            "l1gam": l1gam, "l1bet": l1bet,
            "ffnw": ffn_h, "ffnw_abs": ffn_abs,
            "lnw_t": lnw_t, "lnb_t": lnb_t, "fgam": fgam, "fbet": fbet,
            "ident": ident, "onesr": onesr, "bmask": bmask,
            "delta": deltam, "neg2": n2, "eye2": e2, "iotam": iotam,
        })
    return in_maps


def assemble(results, key="y"):
    x_out = np.empty((B_GLOBAL, T_GLOBAL, D), np.float32)
    for c in range(8):
        b, p = c // 2, c % 2
        yc = results[c][key]
        for i in range(NOWN):
            g = 2 * i + p
            x_out[b, g * 128:(g + 1) * 128] = yc[i]
    return x_out


B_GLOBAL = 4
T_GLOBAL = 2048


def kernel(**inputs):
    in_maps = prep_inputs(**inputs)
    nc = build_nc()
    res = run_bass_kernel_spmd(nc, in_maps, core_ids=list(range(8)))
    return assemble(res.results)


if __name__ == "__main__":
    import sys, time
    t0 = time.time()
    nc = build_nc(upto=sys.argv[1] if len(sys.argv) > 1 else "p4")
    print("built ok in %.1fs" % (time.time() - t0))


# version 07 (final)
# speedup vs baseline: 1.1842x; 1.0371x over previous
import numpy as np
import concourse.bass as bass
import concourse.mybir as mybir
from concourse.bass_utils import run_bass_kernel_spmd

F32 = mybir.dt.float32
BF16 = mybir.dt.bfloat16
AF = mybir.ActivationFunctionType
ALU = mybir.AluOpType
AX = mybir.AxisListType

D = 2048
H = 16
NTC = 16
NOWN = 8
NE = 8
EPS = 1e-5
NEGV = -1.0e30
ISQ = 0.08838834764831845

DMA_SEMS = ("dy", "dg", "dx", "dx2", "dsp", "dsp2", "dab", "dwa", "dwb",
            "dm1a", "dm1b", "dm3a", "dm3b", "dm2a", "dm2b", "dzc", "dout")
ALL_SEMS = DMA_SEMS + ("t", "v", "s", "b", "ag")


class Prog:
    ENGS = ("tensor", "vector", "scalar", "sync", "gpsimd")

    def __init__(self):
        self.ops = {e: [] for e in self.ENGS}
        self.cnt = {}
        self.ecnt = {}
        self.gcnt = {}

    def op(self, eng, fn, waits=(), inc=None, inc_amt=1, big=False):
        w = [t for t in waits if t is not None]
        self.ops[eng].append((fn, w, inc, inc_amt, big))
        if inc:
            self.cnt[inc] = self.cnt.get(inc, 0) + inc_amt
            return (inc, self.cnt[inc], eng)
        return None

    def dma(self, eng, out, in_, waits=(), sem="dy"):
        return self.op(eng, lambda e, o=out, i=in_: e.dma_start(out=o, in_=i),
                       waits, sem, 16, big=True)

    def emit(self, blk, semh):
        for eng in self.ENGS:
            oplist = self.ops[eng]
            if not oplist:
                continue

            hop = eng in ("vector", "scalar")

            def body(e, oplist=oplist, eng=eng, hop=hop):
                seen = {}
                prev = None
                for fn, waits, inc, amt, big in oplist:
                    if hop and prev is not None and not (prev[2] and big):
                        s, n = prev[0], prev[1]
                        if seen.get(s, 0) < n:
                            e.wait_ge(semh[s], n)
                            seen[s] = n
                    for s, n, peng in waits:
                        if n <= 0 or seen.get(s, 0) >= n:
                            continue
                        if peng == eng and s not in DMA_SEMS:
                            continue
                        e.wait_ge(semh[s], n)
                        seen[s] = n
                    inst = fn(e)
                    if inc is not None:
                        inst.then_inc(semh[inc], amt)
                        prev = (inc, self.cnt_emit[eng].setdefault(inc, 0) + 1, big) if False else None
                    prev = None
                    if inc is not None:
                        c = self.ecnt.setdefault((eng, inc), 0) + 1
                        self.ecnt[(eng, inc)] = c
                        g = self.gcnt.setdefault(inc, 0) + 1
                        self.gcnt[inc] = g
                        prev = (inc, g, big)
            getattr(blk, eng)(body)


def build_nc(upto="p4", tap=False):
    nc = bass.Bass("TRN2", target_bir_lowering=False, detect_race_conditions=False)
    P = Prog()

    def din(name, shape, dt=F32):
        return nc.dram_tensor(name, list(shape), dt, kind="ExternalInput").ap()

    x_own = din("x_own", [NOWN, 128, D])
    x_full = din("x_full", [NTC, 128, D])
    wq = din("wq", [4, 16, 128, 512])
    wk = din("wk", [4, 16, 128, 512])
    wv = din("wv", [4, 16, 128, 512])
    wo = din("wo", [4, 4, 4, 128, 512])
    gqt = din("gqt", [128, D])
    gkt = din("gkt", [128, D])
    gate_wd = din("gate_w", [16, 128, NE])
    sw1 = din("sw1", [22, 16, 128, 128])
    sw3 = din("sw3", [22, 16, 128, 128])
    sw2 = din("sw2", [4, 22, 128, 512])
    rw1 = din("rw1", [NE, 11, 16, 128, 128])
    rw3 = din("rw3", [NE, 11, 16, 128, 128])
    rw2 = din("rw2", [NE, 4, 11, 128, 512])
    l1w = din("l1w", [4, 16, 128, 512])
    lam_logit = din("lam_logit", [1, 4])
    eta_row = din("eta", [1, 4])
    l1gam = din("l1gam", [128, D])
    l1bet = din("l1bet", [128, D])
    ffnw = din("ffnw", [4, 16, 128, 512])
    ffnw_abs = din("ffnw_abs", [16, 128, D])
    lnw_t = din("lnw_t", [128, D])
    lnb_t = din("lnb_t", [128, D])
    fgam = din("fgam", [128, D])
    fbet = din("fbet", [128, D])
    ident = din("ident", [128, 128])
    onesr = din("onesr", [1, 128])
    bmask = din("bmask", [128, 256])
    delta = din("delta", [128, 128])
    neg2 = din("neg2", [128, 256])
    eye2 = din("eye2", [128, 256])
    iotam = din("iotam", [1, 16])

    y_out = nc.dram_tensor("y", [NOWN, 128, D], F32, kind="ExternalOutput").ap()
    dbg = None
    if tap is True:
        dbg = nc.dram_tensor("dbg", [NOWN, 128, D], F32, kind="ExternalOutput").ap()
    dbg_ar = None
    if isinstance(tap, str) and (tap.startswith("ar") or tap.startswith("ln")):
        dbg_ar = nc.dram_tensor("dbgar", [128, 36864], BF16, kind="ExternalOutput").ap()

    xt_spill = nc.dram_tensor("xt_spill", [NTC, 128, D], BF16, kind="Internal").ap()
    xt_own = nc.dram_tensor("xt_own", [NOWN, 128, D], BF16, kind="Internal").ap()
    ag_in = nc.dram_tensor("ag_in", [NOWN, 128, D], BF16, kind="Internal").ap()
    ag_out = nc.dram_tensor("ag_out", [2, NOWN, 128, D], BF16, kind="Internal").ap()

    XR = nc.alloc_sbuf_tensor("XR", [128, NOWN * D], F32)
    AR = nc.alloc_sbuf_tensor("AR", [128, 36864], BF16)
    SC = nc.alloc_sbuf_tensor("SC", [128, 4096], F32)
    G1 = nc.alloc_sbuf_tensor("G1", [128, D], F32)
    G2 = nc.alloc_sbuf_tensor("G2", [128, D], F32)
    WS = nc.alloc_sbuf_tensor("WS", [128, 8192], BF16)
    C_ID = nc.alloc_sbuf_tensor("C_ID", [128, 128], BF16)
    C_IDF = nc.alloc_sbuf_tensor("C_IDF", [128, 128], F32)
    C_ON = nc.alloc_sbuf_tensor("C_ON", [1, 128], F32)
    C_BM = nc.alloc_sbuf_tensor("C_BM", [128, 256], F32)
    C_DL = nc.alloc_sbuf_tensor("C_DL", [128, 128], F32)
    C_N2 = nc.alloc_sbuf_tensor("C_N2", [128, 256], F32)
    C_E2 = nc.alloc_sbuf_tensor("C_E2", [128, 256], F32)
    C_IM = nc.alloc_sbuf_tensor("C_IM", [1, 16], F32)
    ST = nc.alloc_sbuf_tensor("ST", [128, 64], F32)
    R1 = nc.alloc_sbuf_tensor("R1", [1, 256], F32)

    pb = [nc.alloc_psum_tensor(f"pb{i}", [128, 512], F32)[:] for i in range(6)]
    pt6 = nc.alloc_psum_tensor("pt6", [128, 1024], BF16)
    pt7 = nc.alloc_psum_tensor("pt7", [128, 1024], BF16)
    pbt = {"t6": pt6, "t7": pt7}

    bank_r = {i: None for i in range(6)}
    bank_r["t6"] = None
    bank_r["t7"] = None

    def T_(fn, waits=(), inc="t", big=True):
        return P.op("tensor", fn, waits, inc, big=big)

    def V_(fn, waits=(), inc="v", big=False):
        return P.op("vector", fn, waits, inc, big=big)

    def S_(fn, waits=(), inc="s", big=False):
        return P.op("scalar", fn, waits, inc, big=big)

    ST_MU, ST_RS, ST_SQ, ST_T0 = 0, 1, 2, 3
    ST_SS, ST_IR = 8, 12
    ST_NM, ST_NT = 16, 20
    ST_LS, ST_LT, ST_LI = 24, 28, 29
    ST_CM, ST_SCL = 32, 33
    ST_M1, ST_M2 = 34, 35
    ST_LAMC = 40
    ST_ETAC = 44
    SC_DUMP = 2048

    def st(c, n=1):
        return ST[:, c:c + n]
    DBG_LN = [bool(tap == "ln1t")]

    def emit_ln(x_ap, out_ap, ready):
        mu, rs, sq, t0 = st(ST_MU), st(ST_RS), st(ST_SQ), st(ST_T0)
        dump = SC[:, SC_DUMP:SC_DUMP + D]
        tmu = S_(lambda e: e.activation(dump, x_ap, AF.Copy, accum_out=mu), ready, big=True)
        tsq = S_(lambda e: e.activation(dump, x_ap, AF.Square, accum_out=sq), ready, big=True)
        V_(lambda e: e.tensor_scalar(mu, mu, 1.0 / D, None, ALU.mult), [tmu])
        tv = V_(lambda e: e.tensor_scalar(sq, sq, 1.0 / D, None, ALU.mult), [tsq])
        if DBG_LN[0]:
            V_(lambda e: e.tensor_copy(st(16), mu))
            V_(lambda e: e.tensor_copy(st(17), sq), [tv])
        V_(lambda e: e.tensor_tensor(t0, mu, mu, ALU.mult))
        if DBG_LN[0]:
            V_(lambda e: e.tensor_copy(st(18), t0))
        V_(lambda e: e.tensor_tensor(t0, sq, t0, ALU.subtract), [tv])
        if DBG_LN[0]:
            V_(lambda e: e.tensor_copy(st(19), t0))
        tve = V_(lambda e: e.tensor_scalar(t0, t0, 1.0, EPS, ALU.mult, ALU.add))
        if DBG_LN[0]:
            V_(lambda e: e.tensor_copy(st(20), t0))
        ts2 = S_(lambda e: e.sqrt(t0, t0), [tve])
        if DBG_LN[0]:
            V_(lambda e: e.tensor_copy(st(21), t0), [ts2])
        V_(lambda e: e.reciprocal(rs, t0), [ts2])
        return V_(lambda e: e.tensor_scalar(out_ap, x_ap, mu, rs, ALU.subtract, ALU.mult),
                  list(ready), big=True)

    tr_flip = [0]

    def emit_tr(src_base_ap, dst_ap, n, ready):
        toks = []
        done = 0
        while done < n:
            take = min(4, n - done)
            bank = "t6" if tr_flip[0] % 2 == 0 else "t7"
            tr_flip[0] += 1
            pbank = pbt[bank]
            ttok = None
            for j in range(take):
                w = list(ready) + [bank_r[bank]] if j == 0 else []
                ttok = T_(lambda e, j=j, dj=done, pbk=pbank: e.transpose(
                    pbk[:, j * 128:(j + 1) * 128],
                    src_base_ap[:, (dj + j) * 128:(dj + j + 1) * 128], C_ID[:]),
                    w, inc="t" if j == take - 1 else None)
            ev = V_(lambda e, dj=done, tk=take, pbk=pbank: e.tensor_copy(
                dst_ap[:, dj * 128:(dj + tk) * 128], pbk[:, :tk * 128]), [ttok], big=True)
            bank_r[bank] = ev
            toks.append(ev)
            done += take
        return toks[-1]

    tok_const = [
        P.dma("gpsimd", C_ID[:], ident[:], sem="dg"),
        P.dma("sync", C_IDF[:], ident[:], sem="dy"),
        P.dma("sync", C_ON[:], onesr[:], sem="dy"),
        P.dma("sync", C_BM[:], bmask[:], sem="dy"),
        P.dma("sync", C_DL[:], delta[:], sem="dy"),
        P.dma("sync", C_N2[:], neg2[:], sem="dy"),
        P.dma("sync", C_E2[:], eye2[:], sem="dy"),
        P.dma("sync", C_IM[:], iotam[:], sem="dy"),
    ]
    tok_g1 = P.dma("sync", G1[:], gqt[:], sem="dy")
    tok_g2 = P.dma("sync", G2[:], gkt[:], sem="dy")

    def dy_barrier():
        return P.op("sync", lambda e: e.nop(),
                    waits=[("dy", P.cnt["dy"], "sync")], inc="b")

    b_const = dy_barrier()
    tok_const = [b_const]
    tok_g1 = b_const
    tok_g2 = b_const

    A_LN = 0
    A_ROW = 2048
    A_XF = 6144

    def xf32(k):
        return AR[:, A_XF + k * 4096: A_XF + (k + 1) * 4096].bitcast(F32)

    spill_last = None
    xtile_free = [None, None]
    row_free = [None, None]
    ln_free = [None]

    def p0_tile(src_dram, dst_dram, idx, k):
        nonlocal spill_last
        tdma = P.dma("sync", xf32(k), src_dram[idx], waits=[xtile_free[k]],
                     sem="dx" if k == 0 else "dx2")
        ln_ap = AR[:, A_LN:A_LN + D]
        t_ln = emit_ln(xf32(k), ln_ap, [tdma, ln_free[0]] + tok_const)
        xtile_free[k] = t_ln
        rb = AR[:, A_ROW + k * 2048:A_ROW + k * 2048 + D]
        ev = emit_tr(ln_ap, rb, 16, [t_ln, row_free[k]])
        ln_free[0] = ev
        spill_last = P.dma("sync", dst_dram[idx], rb, waits=[ev],
                           sem="dsp" if k == 0 else "dsp2")
        row_free[k] = spill_last

    n_p0 = NTC
    if isinstance(tap, str) and tap.startswith("ln1t"):
        n_p0 = 1
    for tc in range(n_p0):
        p0_tile(x_full, xt_spill, tc, tc % 2)
    if n_p0 == NTC:
        for i in range(NOWN):
            p0_tile(x_own, xt_own, i, i % 2)
    if tap == "ln1t":
        P.dma("sync", dbg_ar[:, :2048], AR[:, :2048],
              waits=[("dsp", P.cnt["dsp"], "sync")], sem="dout")
        P.dma("gpsimd", dbg_ar[:, 2048:2080], ST[:, 0:32],
              waits=[("dsp", P.cnt["dsp"], "sync")], sem="dout")
        P.op("sync", lambda e: e.nop(), waits=[("dout", P.cnt["dout"], "sync")])
    spill_all = [("dsp", P.cnt.get("dsp", 0), "sync"), ("dsp2", P.cnt.get("dsp2", 0), "sync")]

    if tap == "ar_raw":
        P.dma("sync", dbg_ar[:, :6144], AR[:, :6144], waits=spill_all, sem="dout")
        P.op("sync", lambda e: e.nop(), waits=[("dout", P.cnt["dout"], "sync")])
    if tap == "ar_xt":
        rb1 = []
        for tc in range(8):
            rb1.append(P.dma("sync", AR[:, tc * 2048:(tc + 1) * 2048],
                             xt_spill[tc], waits=spill_all, sem="dy"))
        for i in range(8):
            rb1.append(P.dma("sync", AR[:, 16384 + i * 2048:16384 + (i + 1) * 2048],
                             xt_own[i], waits=spill_all, sem="dy"))
        P.op("sync", lambda e: e.nop(), waits=[("dy", P.cnt["dy"], "sync")])
        P.dma("sync", dbg_ar, AR[:], waits=rb1, sem="dout")
        P.op("sync", lambda e: e.nop(), waits=[("dout", P.cnt["dout"], "sync")])

    A_XTW = 0
    A_KT = 8192
    A_V = 16384
    A_QT = 24576
    A_P = 28672
    A_PT = 30720
    A_AOT = 32768

    S_VAL, S_TMP, S_SQ = 0, 512, 1024

    xr_tok = [[None] * 4 for _ in range(NOWN)]
    if upto in ("p1", "p2", "p3", "p4"):
        for i in range(NOWN):
            P.dma("sync", XR[:, i * D:(i + 1) * D], x_own[i], sem="dy")
        b_xr = dy_barrier()
        for i in range(NOWN):
            for s4 in range(4):
                xr_tok[i][s4] = b_xr

        ws_free = None
        wave_free = [None, None]
        qk_guard = [None]
        stage_guard = [None, None]

        def wave_load(src_dram, w, k, sem):
            return P.dma("sync",
                         AR[:, A_XTW + k * 4096:A_XTW + (k + 1) * 4096]
                         .rearrange("p (dc tc f) -> p dc tc f", tc=2, f=128),
                         src_dram[2 * w:2 * w + 2]
                         .rearrange("tc p (dc f) -> p dc tc f", f=128),
                         waits=[wave_free[k]] + spill_all, sem=sem)

        def proj(kind, g, wsrc):
            nonlocal ws_free
            wtok = P.dma("gpsimd", WS[:].rearrange("p (dc o) -> p dc o", o=512),
                         wsrc[g].rearrange("dc p o -> p dc o"),
                         waits=[ws_free], sem="dwa")
            nw = 4 if kind == "q" else 8
            src = xt_own if kind == "q" else xt_spill
            lastev = None
            wtoks = {}
            for w in range(min(nw, 2)):
                wtoks[w] = wave_load(src, w, w % 2, "dx" if w % 2 == 0 else "dx2")
            emitted = set(range(min(nw, 2)))
            pend_tr = [None]
            for w in range(nw):
                k = w % 2
                for tt in range(2):
                    tcg = 2 * w + tt
                    bk = (2 * w + tt) % 4
                    mtok = None
                    for dc in range(16):
                        wts = [bank_r[bk], wtoks[w], wtok] if dc == 0 else []
                        mtok = T_(lambda e, bk=bk, k=k, dc=dc, tt=tt:
                                  e.matmul(pb[bk],
                                           AR[:, A_XTW + k * 4096 + dc * 256 + tt * 128:
                                              A_XTW + k * 4096 + dc * 256 + tt * 128 + 128],
                                           WS[:, dc * 512:(dc + 1) * 512],
                                           start=(dc == 0), stop=(dc == 15)),
                                  wts, inc="t" if dc == 15 else None)
                    lastev = proj_out(kind, g, tcg, bk, mtok, pend_tr)
                    last_mm = mtok
                wave_free[k] = last_mm
                nxt = w + 2
                if nxt < nw and nxt not in emitted:
                    wtoks[nxt] = wave_load(src, nxt, nxt % 2,
                                           "dx" if nxt % 2 == 0 else "dx2")
                    emitted.add(nxt)
            if pend_tr[0] is not None:
                lastev = pend_tr[0]()
                pend_tr[0] = None
            ws_free = lastev
            return lastev

        def proj_out(kind, g, tcg, bk, mtok, pend_tr):
            if kind == "v":
                if pend_tr[0] is not None:
                    pend_tr[0]()
                    pend_tr[0] = None
                ev = V_(lambda e: e.tensor_copy(
                    AR[:, A_V + tcg * 512:A_V + tcg * 512 + 512], pb[bk]), [mtok], big=True)
                bank_r[bk] = ev
                return ev
            val = SC[:, S_VAL:S_VAL + 512]
            tmp = SC[:, S_TMP:S_TMP + 512]
            sqv = SC[:, S_SQ:S_SQ + 512]
            ss, ir = st(ST_SS, 4), st(ST_IR, 4)
            t1 = V_(lambda e, bk=bk: e.tensor_copy(val, pb[bk]),
                    [mtok, qk_guard[0]], big=True)
            bank_r[bk] = t1
            t2 = S_(lambda e: e.square(sqv, val), [t1], big=True)
            t3 = V_(lambda e: e.tensor_reduce(
                ss, sqv.rearrange("p (a b) -> p a b", b=128),
                axis=AX.X, op=ALU.add), [t2])
            V_(lambda e: e.tensor_scalar(ss, ss, 1.0 / 128.0, EPS, ALU.mult, ALU.add), [t3])
            t4 = V_(lambda e: e.reciprocal(ss, ss))
            t5 = S_(lambda e: e.sqrt(ir, ss), [t4])
            for hh in range(4):
                V_(lambda e, hh=hh: e.tensor_scalar(
                    tmp[:, hh * 128:(hh + 1) * 128], val[:, hh * 128:(hh + 1) * 128],
                    ir[:, hh:hh + 1], None, ALU.mult), [t5], big=(hh > 0))
            gm = G1 if kind == "q" else G2
            stage = AR[:, A_P + (tcg % 2) * 512:A_P + (tcg % 2) * 512 + 512]
            t7 = V_(lambda e, stage=stage: e.tensor_tensor(
                stage, tmp, gm[:, g * 512:(g + 1) * 512], ALU.mult),
                [tok_g1, stage_guard[tcg % 2]], big=True)
            qk_guard[0] = t2
            dst_base = A_QT if kind == "q" else A_KT
            per = 1024 if kind == "q" else 2048

            def do_tr(stage=stage, tcg=tcg, t7=t7, dst_base=dst_base, per=per):
                bank = "t6" if tr_flip[0] % 2 == 0 else "t7"
                tr_flip[0] += 1
                pbank = pbt[bank]
                ttok = None
                for j in range(4):
                    w = [t7, bank_r[bank]] if j == 0 else []
                    ttok = T_(lambda e, j=j, pbk=pbank, stage=stage: e.transpose(
                        pbk[:, j * 128:(j + 1) * 128],
                        stage[:, j * 128:(j + 1) * 128], C_ID[:]),
                        w, inc="t" if j == 3 else None)
                stage_guard[tcg % 2] = ttok
                evl = None
                for j in range(4):
                    evl = V_(lambda e, j=j, pbk=pbank: e.tensor_copy(
                        AR[:, dst_base + j * per + tcg * 128:
                           dst_base + j * per + tcg * 128 + 128],
                        pbk[:, j * 128:(j + 1) * 128]), [ttok], big=True)
                bank_r[bank] = evl
                return evl

            prev = pend_tr[0]
            pend_tr[0] = do_tr
            if prev is not None:
                return prev()
            return t7

        for g in range(4):
            tq = proj("q", g, wq)
            tk = proj("k", g, wk)
            tv = proj("v", g, wv)
            proj_done = [tq, tk, tv]
            if tap == "ar_qkv" and g == 0:
                P.dma("sync", dbg_ar[:, :28672], AR[:, :28672],
                      waits=proj_done, sem="dout")
                P.op("sync", lambda e: e.nop(),
                     waits=[("dout", P.cnt["dout"], "sync")])
                break

            wo_tok = P.dma("gpsimd",
                           AR[:, :8192].rearrange("p (s hc o) -> p s hc o",
                                                  hc=4, o=512),
                           wo[g].rearrange("s hc p o -> p s hc o"),
                           waits=proj_done, sem="dwb")

            pv_flip = 0
            aot_tok = []
            p_guard = [None]
            pend = [None]

            def emit_ptpv(i, h, nch, psc_tok):
                pt_tok = emit_tr(AR[:, A_P:A_P + nch * 128],
                                 AR[:, A_PT:A_PT + nch * 128], nch, [psc_tok])
                p_guard[0] = pt_tok
                pvb = 4 + (pv_flip_l[0] % 2)
                pv_flip_l[0] += 1
                mtok = None
                for kc in range(nch):
                    w = [bank_r[pvb], pt_tok] if kc == 0 else []
                    mtok = T_(lambda e, kc=kc, h=h, pvb=pvb, nch=nch: e.matmul(
                        pb[pvb][:, :128],
                        AR[:, A_V + kc * 512 + h * 128:A_V + kc * 512 + h * 128 + 128],
                        AR[:, A_PT + kc * 128:A_PT + kc * 128 + 128],
                        start=(kc == 0), stop=(kc == nch - 1)),
                        w, inc="t" if kc == nch - 1 else None)
                ev = V_(lambda e, h=h, i=i, pvb=pvb: e.tensor_copy(
                    AR[:, A_AOT + h * 1024 + i * 128:A_AOT + h * 1024 + i * 128 + 128],
                    pb[pvb][:, :128]), [mtok], big=True)
                bank_r[pvb] = ev
                aot_tok.append(ev)
            pv_flip_l = [0]

            for i in range(NOWN):
                ncol = 256 * (i + 1)
                nch = 2 * i + 2
                nsl = (ncol + 511) // 512
                for h in range(4):
                    qt_ap = AR[:, A_QT + h * 1024 + i * 128:
                               A_QT + h * 1024 + i * 128 + 128]
                    smm = []
                    for sl in range(nsl):
                        cn = min(512, ncol - sl * 512)
                        tkn = T_(lambda e, sl=sl, h=h, cn=cn, q=qt_ap: e.matmul(
                            pb[sl][:, :cn], q,
                            AR[:, A_KT + h * 2048 + sl * 512:
                               A_KT + h * 2048 + sl * 512 + cn],
                            start=True, stop=True),
                            [bank_r[sl]] + (proj_done if (i == 0 and h == 0) else []),
                            inc="t")
                        smm.append(tkn)
                    if pend[0] is not None:
                        emit_ptpv(*pend[0])
                    sl_m, off_m = (256 * i) // 512, (256 * i) % 512
                    tmask = V_(lambda e, sl=sl_m, off=off_m: e.tensor_tensor(
                        pb[sl][:, off:off + 256], pb[sl][:, off:off + 256],
                        C_BM[:], ALU.add), smm, big=True)
                    rmax = []
                    for sl in range(nsl):
                        cn = min(512, ncol - sl * 512)
                        rmax.append(V_(lambda e, sl=sl, cn=cn: e.reduce_max(
                            st(ST_NM + sl), pb[sl][:, :cn], axis=AX.X, negate=True),
                            [smm[sl], tmask]))
                    tot = rmax[0]
                    for sl in range(1, nsl):
                        tot = V_(lambda e, sl=sl: e.tensor_tensor(
                            st(ST_NM), st(ST_NM), st(ST_NM + sl), ALU.min), [rmax[sl]])
                    tb = V_(lambda e: e.tensor_scalar(
                        st(ST_NT), st(ST_NM), ISQ, None, ALU.mult), [tot])
                    exps = []
                    for sl in range(nsl):
                        cn = min(512, ncol - sl * 512)
                        exps.append(S_(lambda e, sl=sl, cn=cn: e.activation(
                            AR[:, A_P + sl * 512:A_P + sl * 512 + cn],
                            pb[sl][:, :cn], AF.Exp, bias=st(ST_NT), scale=ISQ,
                            accum_out=st(ST_LS + sl)), [tb, p_guard[0]], big=True))
                        bank_r[sl] = exps[-1]
                    V_(lambda e: e.tensor_copy(st(ST_LT), st(ST_LS)), [exps[0]])
                    tl = None
                    for sl in range(1, nsl):
                        tl = V_(lambda e, sl=sl: e.tensor_tensor(
                            st(ST_LT), st(ST_LT), st(ST_LS + sl), ALU.add), [exps[sl]])
                    tli = V_(lambda e: e.reciprocal(st(ST_LI), st(ST_LT)), [tl])
                    psc = V_(lambda e, ncol=ncol: e.tensor_scalar(
                        AR[:, A_P:A_P + ncol], AR[:, A_P:A_P + ncol],
                        st(ST_LI), None, ALU.mult), [tli], big=True)
                    pend[0] = (i, h, nch, psc)
            emit_ptpv(*pend[0])

            last_wo = None
            for s4 in range(4):
                for i in range(NOWN):
                    bk = i % 4
                    mtok = None
                    for hc in range(4):
                        w = [bank_r[bk], wo_tok] + (aot_tok if (s4 == 0 and i == 0) else [])
                        w = w if hc == 0 else []
                        mtok = T_(lambda e, bk=bk, hc=hc, i=i, s4=s4: e.matmul(
                            pb[bk],
                            AR[:, A_AOT + hc * 1024 + i * 128:
                               A_AOT + hc * 1024 + i * 128 + 128],
                            AR[:, s4 * 2048 + hc * 512:s4 * 2048 + hc * 512 + 512],
                            start=(hc == 0), stop=(hc == 3)),
                            w, inc="t" if hc == 3 else None)
                    ev = V_(lambda e, bk=bk, i=i, s4=s4: e.tensor_tensor(
                        XR[:, i * D + s4 * 512:i * D + s4 * 512 + 512],
                        pb[bk],
                        XR[:, i * D + s4 * 512:i * D + s4 * 512 + 512], ALU.add),
                        [mtok, xr_tok[i][s4]], big=True)
                    bank_r[bk] = ev
                    xr_tok[i][s4] = ev
                    last_wo = ev
            wave_free[0] = last_wo
            wave_free[1] = last_wo

    A1_H = 0
    A1_W1 = 11264
    A1_W3 = 15360
    A1_W2 = 19456
    A1_XN = 30720
    A1_G8 = 32768

    if upto in ("p2", "p3", "p4"):
        g1b = G1[:].bitcast(BF16)
        g2b = G2[:].bitcast(BF16)

        def xt1_strip(dc):
            if dc < 8:
                return WS[:, dc * 1024:(dc + 1) * 1024]
            if dc < 12:
                return g1b[:, (dc - 8) * 1024:(dc - 7) * 1024]
            return g2b[:, (dc - 12) * 1024:(dc - 11) * 1024]

        tgw = P.dma("gpsimd",
                    AR[:, A1_G8:A1_G8 + 128].rearrange("p (dc o) -> p dc o", o=NE),
                    gate_wd.rearrange("dc p o -> p dc o"),
                    waits=[xr_tok[i][s] for i in range(NOWN) for s in range(4)],
                    sem="dg")

        xt1_tok = []
        for i in range(NOWN):
            xn = AR[:, A1_XN:A1_XN + D]
            tln = emit_ln(XR[:, i * D:(i + 1) * D], xn,
                          xr_tok[i] + [xt1_tok[-1] if xt1_tok else ws_free])
            done = 0
            while done < 16:
                bank = "t6" if tr_flip[0] % 2 == 0 else "t7"
                tr_flip[0] += 1
                pbank = pbt[bank]
                ttok = None
                for j in range(4):
                    w = [tln, bank_r[bank]] if j == 0 else []
                    ttok = T_(lambda e, j=j, dj=done, pbk=pbank, xn=xn: e.transpose(
                        pbk[:, j * 128:(j + 1) * 128],
                        xn[:, (dj + j) * 128:(dj + j + 1) * 128], C_ID[:]),
                        w, inc="t" if j == 3 else None)
                evl = None
                for j in range(4):
                    evl = V_(lambda e, j=j, dj=done, i=i, pbk=pbank: e.tensor_copy(
                        xt1_strip(dj + j)[:, i * 128:(i + 1) * 128],
                        pbk[:, j * 128:(j + 1) * 128]), [ttok, tok_g1, tok_g2], big=True)
                bank_r[bank] = evl
                xt1_tok.append(evl)
                done += 4
        xt1_all = xt1_tok[-1]

        SC_CW = 1536
        cw_tok = [None] * NOWN
        for i in range(NOWN):
            mtok = None
            for dc in range(16):
                w = [xt1_all, tgw, bank_r[5]] if dc == 0 else []
                mtok = T_(lambda e, dc=dc, i=i: e.matmul(
                    pb[5][:, :NE],
                    xt1_strip(dc)[:, i * 128:(i + 1) * 128],
                    AR[:, A1_G8 + dc * NE:A1_G8 + (dc + 1) * NE],
                    start=(dc == 0), stop=(dc == 15)),
                    w, inc="t" if dc == 15 else None)
            pr = SC[:, SC_CW + i * NE:SC_CW + (i + 1) * NE]
            t1 = V_(lambda e: e.reduce_max(st(ST_M1), pb[5][:, :NE], axis=AX.X,
                                           negate=True), [mtok])
            t2 = S_(lambda e, pr=pr: e.activation(pr, pb[5][:, :NE], AF.Exp,
                                                  bias=st(ST_M1),
                                                  accum_out=st(ST_M2)), [t1])
            bank_r[5] = t2
            t3 = V_(lambda e: e.reciprocal(st(ST_M2), st(ST_M2)), [t2])
            t4 = V_(lambda e, pr=pr: e.tensor_scalar(pr, pr, st(ST_M2), None, ALU.mult), [t3])
            tmp8 = SC[:, SC_DUMP:SC_DUMP + NE]
            t5 = V_(lambda e, pr=pr: e.reduce_max(st(ST_M1), pr, axis=AX.X), [t4])
            t6 = V_(lambda e, pr=pr: e.tensor_scalar(tmp8, pr, st(ST_M1), None, ALU.is_ge), [t5])
            t7 = V_(lambda e, pr=pr: e.scalar_tensor_tensor(
                tmp8, tmp8, -1.0e9, pr, ALU.mult, ALU.add), [t6])
            t8 = V_(lambda e: e.reduce_max(st(ST_M2), tmp8, axis=AX.X), [t7])
            t9 = V_(lambda e, pr=pr: e.tensor_scalar(tmp8, pr, st(ST_M2), None, ALU.is_ge), [t8])
            cw_tok[i] = V_(lambda e, pr=pr: e.tensor_tensor(pr, pr, tmp8, ALU.mult), [t9])

        if tap == "ar_moe":
            P.dma("gpsimd", dbg_ar[:, :64], SC[:, SC_CW:SC_CW + 64],
                  waits=cw_tok, sem="dout")

        w1_free = [None, None]
        w3_free = [None, None]
        w2_free = [None, None]
        h_free = [None] * 11

        def expert_pass(ew1, ew3, ew2, eidx, cw_col, n_fc, fc0):
            h_toks = []
            for fc in range(n_fc):
                kb = fc % 2
                t1w = P.dma("gpsimd",
                            AR[:, A1_W1 + kb * 2048:A1_W1 + (kb + 1) * 2048]
                            .rearrange("p (dc f) -> p dc f", f=128),
                            ew1[fc].rearrange("dc p f -> p dc f"),
                            waits=[w1_free[kb]], sem="dm1a" if kb == 0 else "dm1b")
                t3w = P.dma("gpsimd",
                            AR[:, A1_W3 + kb * 2048:A1_W3 + (kb + 1) * 2048]
                            .rearrange("p (dc f) -> p dc f", f=128),
                            ew3[fc].rearrange("dc p f -> p dc f"),
                            waits=[w3_free[kb]], sem="dm3a" if kb == 0 else "dm3b")
                ma = mb = None
                for sl in range(2):
                    for dc in range(16):
                        w = [bank_r[sl], t1w, xt1_all] if dc == 0 else []
                        ma = T_(lambda e, sl=sl, dc=dc, kb=kb: e.matmul(
                            pb[sl],
                            AR[:, A1_W1 + kb * 2048 + dc * 128:
                               A1_W1 + kb * 2048 + (dc + 1) * 128],
                            xt1_strip(dc)[:, sl * 512:(sl + 1) * 512],
                            start=(dc == 0), stop=(dc == 15)),
                            w, inc="t" if dc == 15 else None)
                    bank_r[sl] = ma
                for sl in range(2):
                    for dc in range(16):
                        w = [bank_r[2 + sl], t3w] if dc == 0 else []
                        mb = T_(lambda e, sl=sl, dc=dc, kb=kb: e.matmul(
                            pb[2 + sl][:],
                            AR[:, A1_W3 + kb * 2048 + dc * 128:
                               A1_W3 + kb * 2048 + (dc + 1) * 128],
                            xt1_strip(dc)[:, sl * 512:(sl + 1) * 512],
                            start=(dc == 0), stop=(dc == 15)),
                            w, inc="t" if dc == 15 else None)
                hts = []
                for sl in range(2):
                    sil = SC[:, sl * 512:(sl + 1) * 512]
                    tsl = S_(lambda e, sl=sl, sil=sil: e.activation(
                        sil, pb[sl], AF.Silu), [ma, mb, h_free[fc]], big=True)
                    bank_r[sl] = tsl
                    hv = V_(lambda e, sl=sl, fc=fc, sil=sil: e.tensor_tensor(
                        AR[:, A1_H + fc * 1024 + sl * 512:A1_H + fc * 1024 + (sl + 1) * 512],
                        sil, pb[2 + sl][:], ALU.mult), [tsl], big=True)
                    bank_r[2 + sl] = hv
                    hts.append(hv)
                w1_free[kb] = hts[-1]
                w3_free[kb] = hts[-1]
                h_toks.append(hts[-1])
            for s4 in range(4):
                kb = s4 % 2
                t2w = P.dma("gpsimd",
                            AR[:, A1_W2 + kb * 5632:A1_W2 + kb * 5632 + n_fc * 512]
                            .rearrange("p (fc o) -> p fc o", o=512),
                            ew2[s4].rearrange("fc p o -> p fc o"),
                            waits=[w2_free[kb]], sem="dm2a" if kb == 0 else "dm2b")
                lastmm = None
                for i in range(NOWN):
                    bk = i % 4
                    mtok = None
                    for fc in range(n_fc):
                        w = ([bank_r[bk], t2w, h_toks[-1]] if fc == 0 else [])
                        mtok = T_(lambda e, bk=bk, fc=fc, i=i, kb=kb: e.matmul(
                            pb[bk],
                            AR[:, A1_H + fc * 1024 + i * 128:A1_H + fc * 1024 + i * 128 + 128],
                            AR[:, A1_W2 + kb * 5632 + fc * 512:
                               A1_W2 + kb * 5632 + (fc + 1) * 512],
                            start=(fc == 0), stop=(fc == n_fc - 1)),
                            w, inc="t" if fc == n_fc - 1 else None)
                    lastmm = mtok
                    if cw_col is None:
                        ev = V_(lambda e, bk=bk, i=i, s4=s4: e.tensor_tensor(
                            XR[:, i * D + s4 * 512:i * D + s4 * 512 + 512],
                            pb[bk],
                            XR[:, i * D + s4 * 512:i * D + s4 * 512 + 512],
                            ALU.add), [mtok, xr_tok[i][s4]], big=True)
                    else:
                        ev = V_(lambda e, bk=bk, i=i, s4=s4: e.scalar_tensor_tensor(
                            XR[:, i * D + s4 * 512:i * D + s4 * 512 + 512],
                            pb[bk],
                            SC[:, SC_CW + i * NE + eidx:SC_CW + i * NE + eidx + 1],
                            XR[:, i * D + s4 * 512:i * D + s4 * 512 + 512],
                            ALU.mult, ALU.add), [mtok, xr_tok[i][s4], cw_tok[i]], big=True)
                    bank_r[bk] = ev
                    xr_tok[i][s4] = ev
                w2_free[kb] = ev
            for fc in range(n_fc):
                h_free[fc] = lastmm
            return lastmm

        moe_free = expert_pass([sw1[fc] for fc in range(11)],
                               [sw3[fc] for fc in range(11)],
                               [sw2[s][0:11] for s in range(4)], None, None, 11, 0)
        if tap == "ar_moe":
            P.dma("sync", dbg_ar[:, 1024:1024 + 11264], AR[:, A1_H:A1_H + 11264],
                  waits=[moe_free], sem="dout")
            P.op("sync", lambda e: e.nop(), waits=[("dout", P.cnt["dout"], "sync")])
        moe_free = expert_pass([sw1[fc] for fc in range(11, 22)],
                               [sw3[fc] for fc in range(11, 22)],
                               [sw2[s][11:22] for s in range(4)], None, None, 11, 0)
        for e_ in range(NE):
            moe_free = expert_pass([rw1[e_][fc] for fc in range(11)],
                                   [rw3[e_][fc] for fc in range(11)],
                                   [rw2[e_][s] for s in range(4)], e_, True, 11, 0)

    A2_XT = 0
    A2_ZS = 16384
    A2_ZF = 0
    A2_MT = 32768
    SC_AK = 2048
    SC_ARG = 2560
    SC_TK = 2688
    SC_ACC = 2816
    SC_BC = 2944
    SC_H = 0

    if upto in ("p3", "p4"):
        tg1 = P.dma("sync", G1[:], l1gam[:], waits=[xr_tok[0][0], moe_free],
                    sem="dy")
        tg2 = P.dma("sync", G2[:], l1bet[:], sem="dy")
        bg3 = dy_barrier()
        tg1 = tg2 = bg3


        xt2_tok = []
        for i in range(NOWN):
            xn = AR[:, A2_ZS:A2_ZS + D]
            tln = emit_ln(XR[:, i * D:(i + 1) * D], xn,
                          xr_tok[i] + ([xt2_tok[-1]] if xt2_tok else []))
            done = 0
            while done < 16:
                bank = "t6" if tr_flip[0] % 2 == 0 else "t7"
                tr_flip[0] += 1
                pbank = pbt[bank]
                ttok = None
                for j in range(4):
                    w = [tln, bank_r[bank]] if j == 0 else []
                    ttok = T_(lambda e, j=j, dj=done, pbk=pbank, xn=xn: e.transpose(
                        pbk[:, j * 128:(j + 1) * 128],
                        xn[:, (dj + j) * 128:(dj + j + 1) * 128], C_ID[:]),
                        w, inc="t" if j == 3 else None)
                evl = None
                for j in range(4):
                    evl = V_(lambda e, j=j, dj=done, i=i, pbk=pbank: e.tensor_copy(
                        AR[:, A2_XT + (dj + j) * 1024 + i * 128:
                           A2_XT + (dj + j) * 1024 + i * 128 + 128],
                        pbk[:, j * 128:(j + 1) * 128]), [ttok], big=True)
                bank_r[bank] = evl
                xt2_tok.append(evl)
                done += 4
        xt2_all = xt2_tok[-1]

        zdma = []
        zproj_last = None
        tl1w = None
        zst_free = [None, None]
        for s4 in range(4):
            tl1w = P.dma("gpsimd",
                         WS[:].rearrange("p (dc o) -> p dc o", o=512),
                         l1w[s4].rearrange("dc p o -> p dc o"),
                         waits=[moe_free if s4 == 0 else zproj_last], sem="dwa")
            for i in range(NOWN):
                bk = i % 4
                mtok = None
                for dc in range(16):
                    w = [bank_r[bk], tl1w, xt2_all] if dc == 0 else []
                    mtok = T_(lambda e, bk=bk, dc=dc, i=i: e.matmul(
                        pb[bk],
                        AR[:, A2_XT + dc * 1024 + i * 128:A2_XT + dc * 1024 + i * 128 + 128],
                        WS[:, dc * 512:(dc + 1) * 512],
                        start=(dc == 0), stop=(dc == 15)),
                        w, inc="t" if dc == 15 else None)
                zproj_last = mtok
                ko = i % 2
                ev = V_(lambda e, bk=bk, ko=ko: e.tensor_copy(
                    AR[:, A2_ZS + ko * 512:A2_ZS + (ko + 1) * 512], pb[bk]),
                    [mtok, zst_free[ko]], big=True)
                bank_r[bk] = ev
                zdma.append(P.dma("sync", ag_in[i][:, s4 * 512:(s4 + 1) * 512],
                                  AR[:, A2_ZS + ko * 512:A2_ZS + (ko + 1) * 512],
                                  waits=[ev], sem="dzc"))
                zst_free[ko] = zdma[-1]

        t_ag = P.op("gpsimd", lambda e: e.collective_compute(
            "AllGather", ALU.bypass,
            replica_groups=[[0, 1], [2, 3], [4, 5], [6, 7]],
            ins=[ag_in], outs=[ag_out]),
            waits=[("dzc", P.cnt["dzc"], "sync")], inc="ag", inc_amt=1)

        tcm = V_(lambda e: e.memset(st(ST_CM), 0.0))
        ab_free = None
        for dc in range(NTC):
            tda = P.dma("sync", SC[:, 0:D], ffnw_abs[dc], waits=[ab_free], sem="dab")
            trd = V_(lambda e: e.reduce_max(st(ST_T0), SC[:, 0:D], axis=AX.X,
                                            apply_absolute_value=True), [tda, tcm])
            ab_free = V_(lambda e: e.tensor_tensor(st(ST_CM), st(ST_CM), st(ST_T0),
                                                   ALU.max), [trd])
        tt1 = T_(lambda e: e.matmul(pb[5][0:1, :128], st(ST_CM), C_IDF[:],
                                    start=True, stop=True),
                 [ab_free, bank_r[5]] + tok_const)
        tt2 = V_(lambda e: e.reduce_max(R1[0:1, 128:129], pb[5][0:1, :128], axis=AX.X), [tt1])
        bank_r[5] = tt2
        tt3 = T_(lambda e: e.matmul(pb[5][:, :1], C_ON[:], R1[0:1, 128:129],
                                    start=True, stop=True), [tt2, bank_r[5]])
        tscl = V_(lambda e: e.tensor_copy(st(ST_SCL), pb[5][:, :1]), [tt3])
        bank_r[5] = tscl

        P.dma("sync", R1[0:1, 32:36], lam_logit[:], sem="dy")
        P.dma("sync", R1[0:1, 64:68], eta_row[:], sem="dy")
        tll = tet = dy_barrier()
        ts1 = S_(lambda e: e.activation(R1[0:1, 36:40], R1[0:1, 32:36], AF.Sigmoid), [tll])
        ts2b = V_(lambda e: e.tensor_scalar(R1[0:1, 36:40], R1[0:1, 36:40], 1.0e-30,
                                            None, ALU.max), [ts1])
        ts3 = S_(lambda e: e.activation(R1[0:1, 68:72], R1[0:1, 36:40], AF.Ln), [ts2b])
        rv = []
        for k4 in range(4):
            rv.append(V_(lambda e, k4=k4: e.tensor_scalar(
                R1[0:1, k4 * 16:(k4 + 1) * 16], C_IM[:],
                R1[0:1, 68 + k4:69 + k4], None, ALU.mult), [ts3] + tok_const))
        tbc = T_(lambda e: e.matmul(pb[5][:, :72], C_ON[:], R1[0:1, 0:72],
                                    start=True, stop=True),
                 rv + [tet, ts3, bank_r[5]])
        tbce = V_(lambda e: e.tensor_copy(SC[:, SC_BC:SC_BC + 72], pb[5][:, :72]), [tbc])
        bank_r[5] = tbce

        ak_tok = []
        for k4 in range(4):
            ak_tok.append(V_(lambda e, k4=k4: e.tensor_scalar(
                SC[:, SC_AK + k4 * 128:SC_AK + (k4 + 1) * 128], C_DL[:],
                SC[:, SC_BC + 68 + k4:SC_BC + 69 + k4], None, ALU.mult),
                [tbce] + tok_const))
        mt_tok = []
        for m in range(16):
            tacc = V_(lambda e: e.memset(SC[:, SC_ACC:SC_ACC + 128], 0.0),
                      [mt_tok[-1] if mt_tok else None])
            for k4 in range(4):
                if m < 2:
                    targ = V_(lambda e, k4=k4, m=m: e.tensor_tensor(
                        SC[:, SC_ARG:SC_ARG + 128],
                        SC[:, SC_AK + k4 * 128:SC_AK + (k4 + 1) * 128],
                        C_N2[:, m * 128:(m + 1) * 128], ALU.add), [ak_tok[k4]], big=True)
                    src = SC[:, SC_ARG:SC_ARG + 128]
                else:
                    targ = ak_tok[k4]
                    src = SC[:, SC_AK + k4 * 128:SC_AK + (k4 + 1) * 128]
                tex = S_(lambda e, src=src, k4=k4, m=m: e.activation(
                    SC[:, SC_TK:SC_TK + 128], src, AF.Exp,
                    bias=SC[:, SC_BC + 16 * k4 + m:SC_BC + 16 * k4 + m + 1]),
                    [targ, tacc], big=True)
                tacc = V_(lambda e, k4=k4: e.scalar_tensor_tensor(
                    SC[:, SC_ACC:SC_ACC + 128], SC[:, SC_TK:SC_TK + 128],
                    SC[:, SC_BC + 64 + k4:SC_BC + 64 + k4 + 1],
                    SC[:, SC_ACC:SC_ACC + 128], ALU.mult, ALU.add), [tex], big=True)
            if m < 2:
                tacc = V_(lambda e, m=m: e.tensor_tensor(
                    SC[:, SC_ACC:SC_ACC + 128], SC[:, SC_ACC:SC_ACC + 128],
                    C_E2[:, m * 128:(m + 1) * 128], ALU.add), [tacc], big=True)
            mt_tok.append(V_(lambda e, m=m: e.tensor_copy(
                AR[:, A2_MT + m * 128:A2_MT + (m + 1) * 128],
                SC[:, SC_ACC:SC_ACC + 128]), [tacc], big=True))
        mt_all = mt_tok[-1]

        zrb = []
        agw = [t_ag, zproj_last]
        for g2 in range(NTC):
            p2, idx = g2 % 2, g2 // 2
            zrb.append(P.dma("sync", AR[:, A2_ZF + g2 * 2048:A2_ZF + (g2 + 1) * 2048],
                             ag_out[p2, idx], waits=agw, sem="dzc"))
        zrb_all = ("dzc", P.cnt["dzc"], "sync")

        if tap == "ar_l1":
            P.dma("sync", dbg_ar[:, :34816], AR[:, :34816],
                  waits=[mt_all, ("dzc", P.cnt["dzc"], "sync")], sem="dout")
            P.op("sync", lambda e: e.nop(), waits=[("dout", P.cnt["dout"], "sync")])

        for i in range(NOWN):
            htoks = []
            for s4 in range(4):
                bk = s4 % 4
                mtok = None
                nj = 2 * i + 2
                for j in range(nj):
                    midx = (2 * i + 1) - j
                    w = [bank_r[bk], mt_all, zrb_all] if j == 0 else []
                    mtok = T_(lambda e, bk=bk, j=j, s4=s4, midx=midx, nj=nj: e.matmul(
                        pb[bk],
                        AR[:, A2_MT + midx * 128:A2_MT + (midx + 1) * 128],
                        AR[:, A2_ZF + j * 2048 + s4 * 512:A2_ZF + j * 2048 + (s4 + 1) * 512],
                        start=(j == 0), stop=(j == nj - 1)),
                        w, inc="t" if j == nj - 1 else None)
                ev = V_(lambda e, bk=bk, s4=s4: e.tensor_copy(
                    SC[:, SC_H + s4 * 512:SC_H + (s4 + 1) * 512], pb[bk]),
                    [mtok, ab_free], big=True)
                bank_r[bk] = ev
                htoks.append(ev)
            hn = SC[:, SC_DUMP:SC_DUMP + D]
            h_ap = SC[:, SC_H:SC_H + D]
            mu, rs, sq, t0 = st(ST_MU), st(ST_RS), st(ST_SQ), st(ST_T0)
            V_(lambda e: e.reduce_sum(mu, h_ap, axis=AX.X), htoks)
            tsq = S_(lambda e: e.activation(SC[:, SC_DUMP:SC_DUMP + D], h_ap,
                                            AF.Square, accum_out=sq), htoks, big=True)
            V_(lambda e: e.tensor_scalar(mu, mu, 1.0 / D, None, ALU.mult))
            tv = V_(lambda e: e.tensor_scalar(sq, sq, 1.0 / D, None, ALU.mult), [tsq])
            V_(lambda e: e.tensor_tensor(t0, mu, mu, ALU.mult))
            V_(lambda e: e.tensor_tensor(t0, sq, t0, ALU.subtract), [tv])
            tve2 = V_(lambda e: e.tensor_scalar(t0, t0, 1.0, EPS, ALU.mult, ALU.add))
            tsr = S_(lambda e: e.sqrt(t0, t0), [tve2])
            trc = V_(lambda e: e.reciprocal(rs, t0), [tsr])
            tnm = V_(lambda e: e.tensor_scalar(h_ap, h_ap, mu, rs,
                                               ALU.subtract, ALU.mult), [trc], big=True)
            tg = V_(lambda e: e.tensor_tensor(h_ap, h_ap, G1[:], ALU.mult),
                    [tnm, tg1], big=True)
            tb2 = V_(lambda e: e.tensor_tensor(h_ap, h_ap, G2[:], ALU.add), [tg, tg2], big=True)
            for s4 in range(4):
                tfin = V_(lambda e, i=i, s4=s4: e.tensor_tensor(
                    XR[:, i * D + s4 * 512:i * D + s4 * 512 + 512],
                    XR[:, i * D + s4 * 512:i * D + s4 * 512 + 512],
                    SC[:, SC_H + s4 * 512:SC_H + (s4 + 1) * 512], ALU.add),
                    [tb2, xr_tok[i][s4]], big=True)
                xr_tok[i][s4] = tfin

    A3_XNT = 0
    A3_SW = 16384

    if upto == "p4":
        tg1b = P.dma("sync", G1[:], lnw_t[:],
                     waits=[xr_tok[i][s] for i in range(NOWN) for s in range(4)],
                     sem="dy")
        tg2b = P.dma("sync", G2[:], lnb_t[:], sem="dy")

        xnt_tok = []
        for i in range(NOWN):
            xn = SC[:, 0:D]
            tln = emit_ln(XR[:, i * D:(i + 1) * D], xn, xr_tok[i] + [tg1b, tg2b])
            tm1 = V_(lambda e: e.tensor_tensor(xn, xn, G1[:], ALU.mult), [tln, tg1b], big=True)
            xnb = AR[:, A2_ZS:A2_ZS + D]
            tm2 = V_(lambda e: e.tensor_tensor(xnb, xn, G2[:], ALU.add),
                     [tm1, tg2b] + ([xnt_tok[-1]] if xnt_tok else []), big=True)
            done = 0
            while done < 16:
                bank = "t6" if tr_flip[0] % 2 == 0 else "t7"
                tr_flip[0] += 1
                pbank = pbt[bank]
                ttok = None
                for j in range(4):
                    w = [tm2, bank_r[bank]] if j == 0 else []
                    ttok = T_(lambda e, j=j, dj=done, pbk=pbank, xnb=xnb: e.transpose(
                        pbk[:, j * 128:(j + 1) * 128],
                        xnb[:, (dj + j) * 128:(dj + j + 1) * 128], C_ID[:]),
                        w, inc="t" if j == 3 else None)
                evl = None
                for j in range(4):
                    evl = V_(lambda e, j=j, dj=done, i=i, pbk=pbank: e.tensor_copy(
                        AR[:, A3_XNT + (dj + j) * 1024 + i * 128:
                           A3_XNT + (dj + j) * 1024 + i * 128 + 128],
                        pbk[:, j * 128:(j + 1) * 128]), [ttok], big=True)
                bank_r[bank] = evl
                xnt_tok.append(evl)
                done += 4
        xnt_all = xnt_tok[-1]

        tg1c = P.dma("sync", G1[:], fgam[:], waits=[xnt_all], sem="dy")
        tg2c = P.dma("sync", G2[:], fbet[:], sem="dy")

        sw_free = [None, None]
        out_cnt = 0
        for s4 in range(4):
            kb = s4 % 2
            swb = AR[:, A3_SW + kb * 8192:A3_SW + (kb + 1) * 8192]
            tdw = P.dma("gpsimd",
                        swb.rearrange("p (dc o) -> p dc o", o=512),
                        ffnw[s4].rearrange("dc p o -> p dc o"),
                        waits=[sw_free[kb], xnt_all],
                        sem="dm1a" if kb == 0 else "dm1b")
            tsg = S_(lambda e, swb=swb: e.sign(swb, swb), [tdw], big=True)
            lastmm = None
            for i in range(NOWN):
                bk = i % 4
                mtok = None
                for dc in range(16):
                    w = [bank_r[bk], tsg, xnt_all] if dc == 0 else []
                    mtok = T_(lambda e, bk=bk, dc=dc, i=i, swb=swb: e.matmul(
                        pb[bk],
                        AR[:, A3_XNT + dc * 1024 + i * 128:A3_XNT + dc * 1024 + i * 128 + 128],
                        swb[:, dc * 512:(dc + 1) * 512],
                        start=(dc == 0), stop=(dc == 15)),
                        w, inc="t" if dc == 15 else None)
                lastmm = mtok
                ko = out_cnt % 2
                out_cnt += 1
                t1 = V_(lambda e, bk=bk, ko=ko, s4=s4: e.scalar_tensor_tensor(
                    SC[:, ko * 512:ko * 512 + 512], pb[bk], st(ST_SCL),
                    G1[:, s4 * 512:s4 * 512 + 512], ALU.mult, ALU.mult),
                    [mtok, tscl, tg1c], big=True)
                bank_r[bk] = t1
                t2 = V_(lambda e, ko=ko, s4=s4: e.tensor_tensor(
                    SC[:, ko * 512:ko * 512 + 512], SC[:, ko * 512:ko * 512 + 512],
                    G2[:, s4 * 512:s4 * 512 + 512], ALU.add), [t1, tg2c], big=True)
                t3 = V_(lambda e, ko=ko, i=i, s4=s4: e.tensor_tensor(
                    SC[:, ko * 512:ko * 512 + 512], SC[:, ko * 512:ko * 512 + 512],
                    XR[:, i * D + s4 * 512:i * D + s4 * 512 + 512], ALU.add),
                    [t2, xr_tok[i][s4], ("dout", P.cnt.get("dout", 0), "sync")], big=True)
                P.dma("sync", y_out[i][:, s4 * 512:s4 * 512 + 512],
                      SC[:, ko * 512:ko * 512 + 512], waits=[t3], sem="dout")
            sw_free[kb] = lastmm
        P.op("sync", lambda e: e.nop(), waits=[("dout", P.cnt["dout"], "sync")])

    if tap is True:
        tw = []
        for i in range(NOWN):
            tw.append(P.dma("sync", dbg[i], XR[:, i * D:(i + 1) * D],
                            waits=xr_tok[i] if xr_tok[i][0] else spill_all,
                            sem="dout"))
        P.op("sync", lambda e: e.nop(), waits=[("dout", P.cnt["dout"], "sync")])
    if upto != "p4" and not tap:
        P.op("sync", lambda e: e.nop(), waits=spill_all)

    semh = {s: nc.alloc_semaphore(s) for s in ALL_SEMS}
    blk_cm = nc.Block()
    blk = blk_cm.__enter__()
    P.emit(blk, semh)
    blk_cm.__exit__(None, None, None)
    return nc


def prep_inputs(L0_attn_x, L0_attn_Wq, L0_attn_Wk, L0_attn_Wv, L0_attn_Wo,
                L0_attn_gq, L0_attn_gk, L0_ffn_gate_w, L0_ffn_sw1, L0_ffn_sw2,
                L0_ffn_sw3, L0_ffn_rw1, L0_ffn_rw2, L0_ffn_rw3, L1_attn_W,
                L1_attn_lam_logit, L1_attn_eta, L1_attn_gamma, L1_attn_beta,
                L1_ffn_weight, L1_ffn_gamma, L1_ffn_beta, L1_ffn_ln_weight,
                L1_ffn_ln_bias):
    f = np.float32
    asn = np.ascontiguousarray

    def t2(w):
        return asn(np.asarray(w, f).T)

    wqT = t2(L0_attn_Wq)
    wkT = t2(L0_attn_Wk)
    wvT = t2(L0_attn_Wv)
    woT = t2(L0_attn_Wo)
    def gsplit(w):
        return asn(w.reshape(16, 128, 4, 512).transpose(2, 0, 1, 3))
    wq_h = gsplit(wqT)
    wk_h = gsplit(wkT)
    wv_h = gsplit(wvT)
    wo_h = asn(woT.reshape(4, 4, 128, 4, 512).transpose(0, 3, 1, 2, 4))
    gq = np.asarray(L0_attn_gq, f)
    gk = np.asarray(L0_attn_gk, f)
    gqt = asn(np.broadcast_to(np.tile(gq, 16)[None, :], (128, D)))
    gkt = asn(np.broadcast_to(np.tile(gk, 16)[None, :], (128, D)))
    gate_h = asn(np.asarray(L0_ffn_gate_w, f).reshape(16, 128, NE))
    sw1_h = asn(np.asarray(L0_ffn_sw1, f).reshape(16, 128, 22, 128)
                .transpose(2, 0, 1, 3))
    sw3_h = asn(np.asarray(L0_ffn_sw3, f).reshape(16, 128, 22, 128)
                .transpose(2, 0, 1, 3))
    sw2_h = asn(np.asarray(L0_ffn_sw2, f).reshape(22, 128, 4, 512)
                .transpose(2, 0, 1, 3))
    rw1_h = asn(np.asarray(L0_ffn_rw1, f).reshape(NE, 16, 128, 11, 128)
                .transpose(0, 3, 1, 2, 4))
    rw3_h = asn(np.asarray(L0_ffn_rw3, f).reshape(NE, 16, 128, 11, 128)
                .transpose(0, 3, 1, 2, 4))
    rw2_h = asn(np.asarray(L0_ffn_rw2, f).reshape(NE, 11, 128, 4, 512)
                .transpose(0, 3, 1, 2, 4))
    l1wT = t2(L1_attn_W)
    l1w_h = asn(l1wT.reshape(16, 128, 4, 512).transpose(2, 0, 1, 3))
    ffnT = t2(L1_ffn_weight)
    ffn_h = asn(ffnT.reshape(16, 128, 4, 512).transpose(2, 0, 1, 3))
    ffn_abs = asn(ffnT.reshape(16, 128, D))
    lam_h = np.asarray(L1_attn_lam_logit, f).reshape(1, 4)
    eta_h = np.asarray(L1_attn_eta, f).reshape(1, 4)

    def tile_row(v):
        return asn(np.broadcast_to(np.asarray(v, f)[None, :], (128, D)))

    l1gam = tile_row(L1_attn_gamma)
    l1bet = tile_row(L1_attn_beta)
    lnw_t = tile_row(L1_ffn_ln_weight)
    lnb_t = tile_row(L1_ffn_ln_bias)
    fgam = tile_row(L1_ffn_gamma)
    fbet = tile_row(L1_ffn_beta)

    ident = np.eye(128, dtype=f)
    onesr = np.ones((1, 128), f)
    r = np.arange(128)
    deltam = asn((r[None, :] - r[:, None]).astype(f))
    iotam_base = 128.0 * (np.arange(16, dtype=f) - 1.0)

    x = np.asarray(L0_attn_x, f)
    in_maps = []
    for c in range(8):
        b, p = c // 2, c % 2
        own = [2 * i + p for i in range(NOWN)]
        x_own = asn(x[b].reshape(NTC, 128, D)[own])
        x_full = asn(x[b].reshape(NTC, 128, D))
        cidx = np.arange(256)[None, :]
        ridx = r[:, None]
        bmask = np.where(cidx <= 128 * p + ridx, 0.0, NEGV).astype(f)
        n2 = np.zeros((128, 256), f)
        e2 = np.zeros((128, 256), f)
        tri_bad = (deltam < 0)
        for m in range(2):
            mm = m * 128
            if m == 1 - p:
                n2[:, mm:mm + 128] = np.where(tri_bad, NEGV, 0.0)
                e2[:, mm:mm + 128] = ident
            elif m < 1 - p:
                n2[:, mm:mm + 128] = NEGV
        iotam = (iotam_base + 128.0 * p).reshape(1, 16).astype(f)
        in_maps.append({
            "x_own": x_own, "x_full": x_full,
            "wq": wq_h, "wk": wk_h, "wv": wv_h, "wo": wo_h,
            "gqt": gqt, "gkt": gkt, "gate_w": gate_h,
            "sw1": sw1_h, "sw3": sw3_h, "sw2": sw2_h,
            "rw1": rw1_h, "rw3": rw3_h, "rw2": rw2_h,
            "l1w": l1w_h, "lam_logit": lam_h, "eta": eta_h,
            "l1gam": l1gam, "l1bet": l1bet,
            "ffnw": ffn_h, "ffnw_abs": ffn_abs,
            "lnw_t": lnw_t, "lnb_t": lnb_t, "fgam": fgam, "fbet": fbet,
            "ident": ident, "onesr": onesr, "bmask": bmask,
            "delta": deltam, "neg2": n2, "eye2": e2, "iotam": iotam,
        })
    return in_maps


def assemble(results, key="y"):
    x_out = np.empty((B_GLOBAL, T_GLOBAL, D), np.float32)
    for c in range(8):
        b, p = c // 2, c % 2
        yc = results[c][key]
        for i in range(NOWN):
            g = 2 * i + p
            x_out[b, g * 128:(g + 1) * 128] = yc[i]
    return x_out


B_GLOBAL = 4
T_GLOBAL = 2048


def kernel(**inputs):
    in_maps = prep_inputs(**inputs)
    nc = build_nc()
    res = run_bass_kernel_spmd(nc, in_maps, core_ids=list(range(8)))
    return assemble(res.results)


if __name__ == "__main__":
    import sys, time
    t0 = time.time()
    nc = build_nc(upto=sys.argv[1] if len(sys.argv) > 1 else "p4")
    print("built ok in %.1fs" % (time.time() - t0))
